# version 02
# baseline (speedup 1.0000x reference)
import collections
import contextlib

import numpy as np
import ml_dtypes

import concourse.bass as bass
import concourse.mybir as mybir
from concourse.bass_utils import run_bass_kernel_spmd

F32 = mybir.dt.float32
BF16 = mybir.dt.bfloat16
AX = mybir.AxisListType.X
AF = mybir.ActivationFunctionType
ALU = mybir.AluOpType
BF16NP = ml_dtypes.bfloat16

B_SZ = 4
S_LEN = 4096
D = 1024
CHUNK = 256
TOK = 2048
CH_A = [0, 3, 4, 7, 8, 11, 12, 15]
CH_B = [1, 2, 5, 6, 9, 10, 13, 14]
NEG_BIG = -1.0e30


def _gq_list(is_a):
    ch = CH_A if is_a else CH_B
    return [2 * ch[l // 2] + (l % 2) for l in range(16)]


L1_KMAX = [max(a, b) + 1 for a, b in zip(_gq_list(True), _gq_list(False))]


def shard_tokens(x):
    shards = []
    for b in range(B_SZ):
        for ch in (CH_A, CH_B):
            rows = np.concatenate([x[b, 256 * c:256 * c + 256] for c in ch], axis=0)
            shards.append(np.ascontiguousarray(rows.T))
    return shards


def unshard_tokens(shards, dtype=np.float32):
    out = np.empty((B_SZ, S_LEN, D), dtype)
    for b in range(B_SZ):
        for k, ch in enumerate((CH_A, CH_B)):
            rows = np.asarray(shards[2 * b + k]).T
            for li, c in enumerate(ch):
                out[b, 256 * c:256 * c + 256] = rows[256 * li:256 * li + 256]
    return out


def host_l1_masks(is_a, transposed):
    gq = _gq_list(is_a)
    m = np.zeros((16, 3, 128, 128), np.float32)
    q = np.arange(128)
    for lq in range(16):
        for j in range(3):
            kt = L1_KMAX[lq] - 3 + j
            delta = gq[lq] - kt
            if delta >= 1:
                continue
            if delta < 0:
                m[lq, j, :, :] = NEG_BIG
            elif transposed:
                m[lq, j][q[:, None] > q[None, :]] = NEG_BIG
            else:
                m[lq, j][q[None, :] > q[:, None]] = NEG_BIG
    return m


def host_tri_masks():
    q = np.arange(128)
    tri_p1 = np.where(q[None, :] > q[:, None], NEG_BIG, 0.0).astype(np.float32)
    tri_p2 = np.where(q[:, None] > q[None, :], NEG_BIG, 0.0).astype(np.float32)
    return tri_p1, tri_p2


ENGINES = ("tensor", "vector", "scalar", "gpsimd", "sync")


class Builder:
    def __init__(self, nc):
        self.nc = nc
        self.streams = {e: [] for e in ENGINES}
        self.sems = {}
        self.counts = collections.defaultdict(int)
        self.es = contextlib.ExitStack()
        self._uid = 0

    def uid(self):
        self._uid += 1
        return self._uid

    def sem(self, name):
        if name not in self.sems:
            self.sems[name] = self.es.enter_context(self.nc.semaphore(name))
            self.counts[name] = 0
        return self.sems[name]

    def emit(self, eng, fn):
        self.streams[eng].append(fn)

    def wait(self, eng, sem_name, val):
        if val is None or val <= 0:
            return
        s = self.sem(sem_name)
        self.emit(eng, lambda E: E.wait_ge(s, val))

    def inc(self, sem_name, n=1):
        s = self.sem(sem_name)
        self.counts[sem_name] += n
        return s, self.counts[sem_name]

    def op(self, eng, fn, inc=None):
        if inc is None:
            self.emit(eng, fn)
            return None
        s, tot = self.inc(inc, 1)
        if eng in ("vector", "scalar", "gpsimd"):
            def wrapped(E):
                fn(E)
                E.drain(fusable=True).then_inc(s, 1)
            self.emit(eng, wrapped)
        else:
            self.emit(eng, lambda E: fn(E).then_inc(s, 1))
        return tot

    def dma(self, eng, sem_name, out, in_, **kw):
        s, tot = self.inc(sem_name, 16)
        self.emit(eng, lambda E: E.dma_start(out=out, in_=in_, **kw).then_inc(s, 16))
        return tot

    def sbuf(self, name, shape, dtype):
        return self.es.enter_context(self.nc.sbuf_tensor(name, shape, dtype))

    def psum(self, name, shape=(128, 512), dtype=F32):
        return self.es.enter_context(self.nc.psum_tensor(name, list(shape), dtype))

    def barrier(self, name=None):
        nm = name or f"bar{self.uid()}"
        s = self.sem(nm)
        for e in ("tensor", "vector", "scalar", "gpsimd"):
            _, tot = self.inc(nm, 1)
            self.emit(e, lambda E: E.nop().then_inc(s, 1))
        tot = self.counts[nm]
        for e in ("tensor", "vector", "scalar", "gpsimd", "sync"):
            self.emit(e, lambda E: E.wait_ge(s, tot))

    def finalize(self):
        block = self.es.enter_context(self.nc.Block())
        streams = self.streams

        @block.gpsimd
        def _(E):
            for fn in streams["gpsimd"]:
                fn(E)

        @block.tensor
        def _(E):
            for fn in streams["tensor"]:
                fn(E)

        @block.vector
        def _(E):
            for fn in streams["vector"]:
                fn(E)

        @block.scalar
        def _(E):
            for fn in streams["scalar"]:
                fn(E)

        @block.sync
        def _(E):
            for fn in streams["sync"]:
                fn(E)

        self.es.close()
        return self.nc


def emit_layernorm(b, x, xln, banks, sq, rowbuf, ones_col, ones_row, rb, mb_,
                   epsb):
    R_MU, R_R, R_T, R_T2 = 0, 1, 2, 3

    def row(i, n0=0, n1=TOK):
        return rowbuf[0:1, i * TOK + n0: i * TOK + n1]

    chunk_mark = None
    for c in range(8):
        if chunk_mark is not None:
            b.wait("scalar", "t", chunk_mark)

        def fsq(E, c=c):
            return E.activation(sq[:, 0:TOK], x[:, c * TOK:(c + 1) * TOK], AF.Square)
        tot_s = b.op("scalar", fsq, inc="s")
        b.wait("tensor", "s", tot_s)
        for sl in range(4):
            n0 = sl * 512

            def fmm1(E, c=c, sl=sl, n0=n0):
                return E.matmul(banks[sl][0:1, :], ones_col[:],
                                x[:, c * TOK + n0:c * TOK + n0 + 512],
                                start=(c == 0), stop=False)

            b.op("tensor", fmm1)

            def fmm2(E, c=c, sl=sl, n0=n0):
                return E.matmul(banks[sl][32:33, :], ones_col[:], sq[:, n0:n0 + 512],
                                start=(c == 0), stop=(c == 7))

            chunk_mark = b.op("tensor", fmm2, inc="t" if sl == 3 else None)
    t_stats = b.counts["t"]
    b.wait("vector", "t", t_stats)
    for sl in range(4):
        n0, n1 = sl * 512, sl * 512 + 512

        def f1(E, sl=sl, n0=n0, n1=n1):
            return E.tensor_scalar_mul(row(R_MU, n0, n1), banks[sl][0:1, :], 1.0 / D)

        b.op("vector", f1)

        def f2(E, sl=sl, n0=n0, n1=n1):
            return E.tensor_scalar_mul(row(R_T, n0, n1), banks[sl][32:33, :], 1.0 / D)

        b.op("vector", f2)
    b.op("vector", lambda E: E.tensor_mul(row(R_T2), row(R_MU), row(R_MU)))
    tot_v = b.op("vector", lambda E: E.tensor_sub(row(R_T), row(R_T), row(R_T2)),
                 inc="v")
    b.wait("scalar", "v", tot_v)
    tot_s = b.op("scalar", lambda E: E.activation(row(R_T2), row(R_T), AF.Ln,
                                                  bias=epsb[0:1, 0:1]), inc="s")
    b.wait("scalar", "s", tot_s)
    tot_s = b.op("scalar", lambda E: E.activation(row(R_R), row(R_T2), AF.Exp, scale=-0.5),
                 inc="s")
    b.wait("vector", "s", tot_s)
    tot_v = b.op("vector", lambda E: E.tensor_mul(row(R_T), row(R_MU), row(R_R)),
                 inc="v")
    b.wait("tensor", "v", tot_v)
    for sl in range(4):
        n0 = sl * 512
        bk_r, bk_m = banks[4], banks[5]

        def f1(E, n0=n0, bk=bk_r):
            return E.matmul(bk[:, 0:512], ones_row[:], row(R_R, n0, n0 + 512),
                            start=True, stop=True)

        t1 = b.op("tensor", f1, inc="t")
        b.wait("vector", "t", t1)

        def c1(E, n0=n0, bk=bk_r):
            return E.tensor_copy(rb[:, n0:n0 + 512], bk[:, 0:512])

        b.op("vector", c1)

        def f2(E, n0=n0, bk=bk_m):
            return E.matmul(bk[:, 0:512], ones_row[:], row(R_T, n0, n0 + 512),
                            start=True, stop=True)

        t2 = b.op("tensor", f2, inc="t")
        b.wait("vector", "t", t2)

        def c2(E, n0=n0, bk=bk_m):
            return E.tensor_copy(mb_[:, n0:n0 + 512], bk[:, 0:512])

        v2 = b.op("vector", c2, inc="v")
        b.wait("tensor", "v", v2)
    for c in range(8):
        def fa(E, c=c):
            return E.tensor_mul(sq[:, 0:TOK], x[:, c * TOK:(c + 1) * TOK], rb[:, 0:TOK])

        b.op("vector", fa)

        def fb(E, c=c):
            return E.tensor_sub(xln[:, c * TOK:(c + 1) * TOK], sq[:, 0:TOK], mb_[:, 0:TOK])

        b.op("vector", fb, inc="v" if c == 7 else None)
    return b.counts["v"]


if __name__ == "__main__":
    print("import ok; L1_KMAX =", L1_KMAX)



DBG_SC = None


def alloc_consts(b):
    c = {}
    c["ones_col"] = b.sbuf("c_ones_col", [128, 1], F32)
    c["ones_row"] = b.sbuf("c_ones_row", [1, 128], F32)
    c["epsb"] = b.sbuf("c_epsb", [1, 1], F32)
    b.op("vector", lambda E: E.memset(c["ones_col"][:], 1.0))
    b.op("vector", lambda E: E.memset(c["ones_row"][:], 1.0))
    b.op("vector", lambda E: E.memset(c["epsb"][:], 1e-5), inc="v")
    return c


def tok_ap(t, c, n0, n1):
    return t[:, c * TOK + n0: c * TOK + n1]


def emit_proj(b, w_sb, n_oc, n_kc, xln, n0, banks, bank_ids, evac, state):
    outw = n_oc * 128
    for oc in range(n_oc):
        bid = bank_ids[oc % len(bank_ids)]
        bk = banks[bid]
        if bid in state:
            sem_name, val = state[bid]
            b.wait("tensor", sem_name, val)
        for kc in range(n_kc):
            def fmm(E, oc=oc, kc=kc, n0=n0, bk=bk):
                return E.matmul(bk[:, 0:512],
                                w_sb[:, kc * outw + oc * 128: kc * outw + (oc + 1) * 128],
                                tok_ap(xln, kc, n0, n0 + 512),
                                start=(kc == 0), stop=(kc == n_kc - 1))
            tmm = b.op("tensor", fmm, inc="t" if kc == n_kc - 1 else None)
        state[bid] = evac(oc, n0, bk, tmm)


def emit_l0_attn_instance(b, qkv_g, icol0, x, xcol0, bufs, consts):
    Qt, Kt, Vt = bufs["Qt"], bufs["Kt"], bufs["Vt"]
    Vtok, Pb, sink = bufs["Vtok"], bufs["Pb"], bufs["sink"]
    mz, bpack, banks = bufs["mz"], bufs["bpack"], bufs["banks"]
    idb, tri1, tri2 = bufs["idb"], bufs["tri1"], bufs["tri2"]
    SCALE = 1.0 / (2.0 * D) ** 0.5
    GW = 512

    mneg = lambda q0, q1: mz[:, q0:q1]
    zall = lambda q0, q1: mz[:, 32 + q0:32 + q1]

    b.wait("tensor", "v", b.counts["v"])

    for j in range(16):
        fq, fk, fv = 192 * j, 192 * j + 64, 192 * j + 128
        for (src_f, dst, scale) in ((fq, Qt, SCALE), (fk, Kt, None), (fv, Vt, None)):
            cj, off = src_f // 128, src_f % 128
            src = qkv_g[off:off + 64, cj * GW + icol0: cj * GW + icol0 + 256]
            d = dst[0:64, j:4096:16]
            if scale is None:
                def fcp(E, src=src, d=d):
                    return E.tensor_copy(d, src)
            else:
                def fcp(E, src=src, d=d, scale=scale):
                    return E.tensor_scalar_mul(d, src, scale)
            b.op("vector", fcp)
    vbuild = b.op("vector", lambda E: E.engine_nop(), inc="v")
    b.wait("tensor", "v", vbuild)
    pT = banks[7].ap().bitcast(BF16)
    for kt in range(32):
        def ftr(E, kt=kt):
            return E.transpose(pT[0:128, 0:64], Vt[0:64, kt * 128:(kt + 1) * 128], idb[0:64, 0:64])
        ttr = b.op("tensor", ftr, inc="t")
        b.wait("vector", "t", ttr)
        def fev(E, kt=kt):
            return E.tensor_copy(Vtok[:, kt * 64:(kt + 1) * 64], pT[0:128, 0:64])
        vev = b.op("vector", fev, inc="v")
        b.wait("tensor", "v", vev)
    for qt in range(32):
        nk = (qt + 1) * 128
        nsl = (nk + 511) // 512
        for s in range(nsl):
            k0, k1 = s * 512, min(nk, s * 512 + 512)
            def fmm(E, qt=qt, s=s, k0=k0, k1=k1):
                return E.matmul(banks[s][:, 0:k1 - k0], Qt[0:64, qt * 128:(qt + 1) * 128],
                                Kt[0:64, k0:k1], start=True, stop=True)
            tp1 = b.op("tensor", fmm, inc="t" if s == nsl - 1 else None)
        b.wait("vector", "t", tp1)
        ds, dc = qt // 4, (qt % 4) * 128
        def fmask(E, ds=ds, dc=dc):
            return E.tensor_add(banks[ds][:, dc:dc + 128], banks[ds][:, dc:dc + 128], tri1[:])
        b.op("vector", fmask)
        for s in range(nsl):
            k0, k1 = s * 512, min(nk, s * 512 + 512)
            def fred(E, s=s, k0=k0, k1=k1):
                return E.reduce_max(mz[:, 96 + s:97 + s], banks[s][:, 0:k1 - k0], axis=AX)
            vpre = b.op("vector", fred, inc="v" if s == nsl - 1 else None)
        b.wait("vector", "v", vpre)
        def fcomb(E, qt=qt, nsl=nsl):
            return E.reduce_max(mneg(qt, qt + 1), mz[:, 96:96 + nsl], axis=AX, negate=True)
        vred = b.op("vector", fcomb, inc="v")
        if DBG_SC is not None and qt in DBG_SC:
            dst = DBG_SC[qt]
            w = min(nk, dst.shape[-1] if hasattr(dst, 'shape') else 1024)
            for s in range((w + 511) // 512):
                k0, k1 = s * 512, min(w, s * 512 + 512)
                def fdbg(E, s=s, k0=k0, k1=k1, dst=dst):
                    return E.tensor_copy(dst[:, k0:k1], banks[s][:, 0:k1 - k0])
                vred = b.op("vector", fdbg, inc="v")
        b.wait("scalar", "v", vred)
        for s in range(nsl):
            k0, k1 = s * 512, min(nk, s * 512 + 512)
            def fexp(E, qt=qt, s=s, k0=k0, k1=k1):
                return E.activation(sink[:, 0:k1 - k0], banks[s][:, 0:k1 - k0], AF.Exp,
                                    bias=mneg(qt, qt + 1), accum_out=mz[:, 96 + s:97 + s])
            sexp = b.op("scalar", fexp, inc="s" if s == nsl - 1 else None)
        b.wait("vector", "s", sexp)
        def fz(E, qt=qt, nsl=nsl):
            return E.reduce_sum(zall(qt, qt + 1), mz[:, 96:96 + nsl], axis=AX)
        vz = b.op("vector", fz, inc="v")
        b.wait("tensor", "v", vz)
        b.wait("tensor", "s", sexp)
    b.wait("scalar", "v", vz)
    sln = b.op("scalar", lambda E: E.activation(mz[:, 96:128], mz[:, 32:64], AF.Ln), inc="s")
    b.wait("vector", "s", sln)

    def vstep(fn):
        t = b.op("vector", fn, inc="v")
        b.wait("vector", "v", t)
        return t

    vstep(lambda E: E.tensor_sub(mz[:, 64:96], mz[:, 0:32], mz[:, 96:128]))
    bp = bpack
    vstep(lambda E: E.tensor_copy(bp[:, 0:96:3], mz[:, 64:96]))
    vstep(lambda E: E.tensor_copy(mz[:, 96:128], bp[:, 0:96:3]))
    vstep(lambda E: E.tensor_sub(mz[:, 128:160], mz[:, 64:96], mz[:, 96:128]))
    vstep(lambda E: E.tensor_copy(bp[:, 1:96:3], mz[:, 128:160]))
    vstep(lambda E: E.tensor_copy(mz[:, 96:128], bp[:, 1:96:3]))
    vstep(lambda E: E.tensor_sub(mz[:, 128:160], mz[:, 128:160], mz[:, 96:128]))
    vsp = vstep(lambda E: E.tensor_copy(bp[:, 2:96:3], mz[:, 128:160]))
    b.wait("tensor", "v", vsp)
    for qt in range(32):
        def ftr(E, qt=qt):
            return E.transpose(pT[0:3, 0:128], bp[:, qt * 3:qt * 3 + 3], idb[:])
        ttr = b.op("tensor", ftr, inc="t")
        b.wait("vector", "t", ttr)
        def fev(E, qt=qt):
            return E.tensor_copy(Qt[64:67, qt * 128:(qt + 1) * 128], pT[0:3, 0:128])
        vev = b.op("vector", fev, inc="v")
        b.wait("tensor", "v", vev)
    p2b = [banks[0], banks[1]]
    avb = [banks[2], banks[3]]
    last_av_evac = 0
    prev_tav = prev_sexp = 0
    for g in range(8):
        av = avb[g % 2]
        if last_av_evac:
            b.wait("tensor", "v", last_av_evac)
        if g > 0:
            b.wait("scalar", "t", prev_tav)
            b.wait("tensor", "s", prev_sexp)
        ktmax = 4 * g + 3
        sexp_hist = {}
        tav_hist = {}
        for kt in range(ktmax + 1):
            co = max(0, (kt - 4 * g)) * 128
            p2 = p2b[kt % 2]
            if kt >= 2:
                b.wait("tensor", "s", sexp_hist[kt - 2])
            def fmm(E, g=g, kt=kt, co=co, p2=p2):
                return E.matmul(p2[:, co:512], Kt[0:67, kt * 128:(kt + 1) * 128],
                                Qt[0:67, g * 512 + co:(g + 1) * 512], start=True, stop=True)
            tp2 = b.op("tensor", fmm, inc="t")
            if kt >= 4 * g:
                b.wait("vector", "t", tp2)
                def fmask(E, co=co, p2=p2):
                    return E.tensor_add(p2[:, co:co + 128], p2[:, co:co + 128], tri2[:])
                vm = b.op("vector", fmask, inc="v")
                b.wait("scalar", "v", vm)
            else:
                b.wait("scalar", "t", tp2)
            if kt >= 2:
                b.wait("scalar", "t", tav_hist[kt - 2])
            pb = Pb[kt % 2]
            def fexp(E, kt=kt, co=co, p2=p2, pb=pb):
                return E.activation(pb[:, co:512], p2[:, co:512], AF.Exp)
            sexp = b.op("scalar", fexp, inc="s")
            sexp_hist[kt] = sexp
            b.wait("tensor", "s", sexp)
            def fav(E, g=g, kt=kt, co=co, pb=pb, av=av):
                return E.matmul(av[0:64, co:512], Vtok[:, kt * 64:(kt + 1) * 64],
                                pb[:, co:512], start=(kt == 0), stop=(kt == ktmax))
            tav_hist[kt] = b.op("tensor", fav, inc="t")
        tav = b.counts["t"]
        prev_tav, prev_sexp = tav, sexp_hist[ktmax]
        b.wait("vector", "t", tav)
        for j in range(16):
            cj, off = (64 * j) // 128, (64 * j) % 128
            xs = x[off:off + 64, cj * TOK + xcol0 + 32 * g: cj * TOK + xcol0 + 32 * g + 32]
            def fadd(E, j=j, xs=xs, av=av):
                return E.tensor_add(xs, xs, av[0:64, j:512:16])
            last_av_evac = b.op("vector", fadd, inc="v" if j == 15 else None)
    return last_av_evac


def build_l0_qkv():
    nc = bass.Bass("TRN2", target_bir_lowering=False)
    xin = nc.dram_tensor("xin", [D, TOK], F32, kind="ExternalInput").ap()
    wq = nc.dram_tensor("wq", [D, 3072], F32, kind="ExternalInput").ap()
    bq = nc.dram_tensor("bq", [1, 3072], F32, kind="ExternalInput").ap()
    qkvout = nc.dram_tensor("qkvout", [3072, TOK], BF16, kind="ExternalOutput").ap()

    b = Builder(nc)
    x = b.sbuf("x", [128, 8 * TOK], F32)
    xln = b.sbuf("xln", [128, 8 * TOK], BF16)
    w_sb = b.sbuf("w_sb", [128, 8 * 3072], BF16)
    bias_sb = b.sbuf("bias_sb", [128, 24], F32)
    sq = b.sbuf("sq", [128, TOK], F32)
    rowbuf = b.sbuf("rowbuf", [1, 4 * TOK], F32)
    rb = b.sbuf("rb", [128, TOK], F32)
    mb_ = b.sbuf("mb", [128, TOK], F32)
    sq_b = sq.ap().bitcast(BF16)
    stg = [sq_b[:, 0:TOK], sq_b[:, TOK:2 * TOK]]
    banks = [b.psum(f"bk{i}") for i in range(8)]
    c = alloc_consts(b)

    ldx = b.dma("sync", "ds", x[:].rearrange("p (c n) -> p c n", c=8),
                xin.rearrange("(c p) n -> p c n", p=128))
    ldw = b.dma("gpsimd", "dg", w_sb[:].rearrange("p (c n) -> p c n", c=8),
                wq.rearrange("(c p) n -> p c n", p=128))
    ldb = b.dma("gpsimd", "dg", bias_sb[:], bq.rearrange("o (c p) -> p (o c)", p=128),
                allow_slow_non_contiguous=True)
    for e in ("scalar", "tensor", "vector"):
        b.wait(e, "ds", ldx)
        b.wait(e, "v", 1)
    vln = emit_layernorm(b, x, xln, banks, sq, rowbuf, c["ones_col"], c["ones_row"],
                         rb, mb_, c["epsb"])
    b.wait("tensor", "v", vln)
    b.wait("tensor", "dg", ldw)
    b.wait("scalar", "dg", ldb)
    state = {}
    stg_war = [0, 0]
    for oc in range(24):
        sg = stg[oc % 2]
        b.wait("scalar", "st", stg_war[oc % 2])
        for sl in range(4):
            n0 = sl * 512
            bid = sl % 2
            bk = banks[bid]
            if bid in state:
                b.wait("tensor", "s", state[bid])
            for kc in range(8):
                def fmm(E, oc=oc, kc=kc, n0=n0, bk=bk):
                    return E.matmul(bk[:, 0:512],
                                    w_sb[:, kc * 3072 + oc * 128: kc * 3072 + (oc + 1) * 128],
                                    xln[:, kc * TOK + n0: kc * TOK + n0 + 512],
                                    start=(kc == 0), stop=(kc == 7))
                tmm = b.op("tensor", fmm, inc="t" if kc == 7 else None)
            b.wait("scalar", "t", tmm)

            def fev(E, oc=oc, n0=n0, bk=bk, sg=sg):
                return E.activation(sg[:, n0:n0 + 512], bk[:, 0:512], AF.Identity,
                                    bias=bias_sb[:, oc:oc + 1])
            state[bid] = b.op("scalar", fev, inc="s")
        b.wait("sync", "s", b.counts["s"])
        stg_war[oc % 2] = b.dma("sync", "st", qkvout[oc * 128:(oc + 1) * 128, :], sg)
    b.wait("sync", "st", b.counts["st"])
    return b.finalize()


def build_l0_attn():
    nc = bass.Bass("TRN2", target_bir_lowering=False)
    xin = nc.dram_tensor("xin", [D, TOK], F32, kind="ExternalInput").ap()
    qkv = nc.dram_tensor("qkv", [3072, TOK], BF16, kind="ExternalInput").ap()
    tri1d = nc.dram_tensor("tri1", [128, 128], F32, kind="ExternalInput").ap()
    tri2d = nc.dram_tensor("tri2", [128, 128], F32, kind="ExternalInput").ap()
    identd = nc.dram_tensor("ident", [128, 128], BF16, kind="ExternalInput").ap()
    xout = nc.dram_tensor("xout", [D, TOK], F32, kind="ExternalOutput").ap()

    b = Builder(nc)
    x = b.sbuf("x", [128, 8 * TOK], F32)
    qkv_g = [b.sbuf(f"qkv_g{i}", [128, 24 * 512], BF16) for i in range(2)]
    bufs = {
        "Qt": b.sbuf("Qt", [128, 4096], BF16),
        "Kt": b.sbuf("Kt", [128, 4096], BF16),
        "Vt": b.sbuf("Vt", [64, 4096], BF16),
        "Vtok": b.sbuf("Vtok", [128, 2048], BF16),
        "Pb": [b.sbuf(f"Pb{i}", [128, 512], BF16) for i in range(2)],
        "sink": b.sbuf("sink", [128, 512], BF16),
        "mz": b.sbuf("mz", [128, 160], F32),
        "bpack": b.sbuf("bpack", [128, 96], BF16),
        "tri1": b.sbuf("tri1_sb", [128, 128], F32),
        "tri2": b.sbuf("tri2_sb", [128, 128], F32),
        "idb": b.sbuf("idb_sb", [128, 128], BF16),
    }
    banks = [b.psum(f"bk{i}") for i in range(8)]
    bufs["banks"] = banks

    ldx = b.dma("sync", "ds", x[:].rearrange("p (c n) -> p c n", c=8),
                xin.rearrange("(c p) n -> p c n", p=128))
    b.dma("sync", "ds", bufs["tri1"][:], tri1d)
    b.dma("sync", "ds", bufs["tri2"][:], tri2d)
    ldc = b.dma("sync", "ds", bufs["idb"][:], identd)
    b.op("vector", lambda E: E.memset(bufs["Kt"][64:67, :], 1.0), inc="v")
    for e in ("scalar", "tensor", "vector"):
        b.wait(e, "ds", ldc)
    b.wait("tensor", "v", 1)

    qkv_r = qkv.rearrange("(c p) n -> p c n", p=128)
    prev_g_dma = [0, 0]
    for g in range(4):
        gb = qkv_g[g % 2]
        b.wait("sync", "v", b.counts["v"])
        ldg = b.dma("sync", "ds", gb[:].rearrange("p (c n) -> p c n", c=24),
                    qkv_r[:, :, g * 512:(g + 1) * 512])
        b.wait("vector", "ds", ldg)
        for i in range(2):
            li = 2 * g + i
            emit_l0_attn_instance(b, gb, i * 256, x, li * 256, bufs, None)
    vfin = b.counts["v"]
    b.wait("sync", "v", vfin)
    st = b.dma("sync", "st", xout.rearrange("(c p) n -> p c n", p=128),
               x[:].rearrange("p (c n) -> p c n", c=8))
    b.wait("sync", "st", st)
    return b.finalize()


def build_l0_ffn():
    nc = bass.Bass("TRN2", target_bir_lowering=False)
    xin = nc.dram_tensor("xin", [D, TOK], F32, kind="ExternalInput").ap()
    rwd = nc.dram_tensor("rw", [D, 16], F32, kind="ExternalInput").ap()
    rbd = nc.dram_tensor("rb_in", [16, 1], F32, kind="ExternalInput").ap()
    w1d = nc.dram_tensor("w1", [16, D, 2048], F32, kind="ExternalInput").ap()
    b1d = nc.dram_tensor("b1t", [2048, 16], F32, kind="ExternalInput").ap()
    w2d = nc.dram_tensor("w2", [16, 2048, D], F32, kind="ExternalInput").ap()
    b2d = nc.dram_tensor("b2", [16, D], F32, kind="ExternalInput").ap()
    idfd = nc.dram_tensor("idf", [128, 128], F32, kind="ExternalInput").ap()
    eyed = nc.dram_tensor("eyeb", [16, 16 * 128], BF16, kind="ExternalInput").ap()
    xout = nc.dram_tensor("xout", [D, TOK], F32, kind="ExternalOutput").ap()

    b = Builder(nc)
    x = b.sbuf("x", [128, 8 * TOK], F32)
    xln = b.sbuf("xln", [128, 8 * TOK], BF16)
    arA = b.sbuf("arA", [128, 2 * 4096], F32)
    rowbuf = arA.ap()[0:1, 0:4 * TOK]
    w2sb = arA.ap().bitcast(BF16)
    arB = b.sbuf("arB", [128, 6144], F32)
    sq = arB.ap()[:, 0:TOK]
    rb = arB.ap()[:, TOK:2 * TOK]
    mb_ = arB.ap()[:, 2 * TOK:3 * TOK]
    hm = arB.ap().bitcast(BF16)[:, 0:16 * 512]
    gtmp = arB.ap().bitcast(BF16)[:, 8192:8704]
    cbc = arB.ap().bitcast(BF16)[:, 8704:10752]
    w1sb = b.sbuf("w1sb", [128, 8 * 2048], BF16)
    rwsb = b.sbuf("rwsb", [128, 8 * 16], BF16)
    rbsb = b.sbuf("rbsb", [16, 1], F32)
    b1sb = b.sbuf("b1sb", [128, 256], F32)
    b2sb = b.sbuf("b2sb", [16, 1024], BF16)
    logits = arB.ap()[0:16, 0:TOK]
    ltok = arB.ap()[:, 2048:2304]
    et = arB.ap()[:, 2304:2320]
    tsc = arB.ap()[:, 2320:2336]
    tmsk = arB.ap()[:, 2336:2352]
    ctok = arB.ap()[:, 2352:2608]
    cTb = b.sbuf("cTb", [16, TOK], BF16)
    idf = b.sbuf("idf_sb", [128, 128], F32)
    eyeb = b.sbuf("eyeb_sb", [16, 16 * 128], BF16)
    sml = b.sbuf("sml", [128, 8], F32)
    banks = [b.psum(f"bk{i}") for i in range(8)]
    c = alloc_consts(b)

    ldx = b.dma("sync", "ds", x[:].rearrange("p (c n) -> p c n", c=8),
                xin.rearrange("(c p) n -> p c n", p=128))
    b.dma("sync", "ds", idf[:], idfd)
    b.dma("sync", "ds", eyeb[:], eyed)
    b.dma("sync", "ds", rbsb[:], rbd)
    ldc = b.dma("gpsimd", "dg", rwsb[:].rearrange("p (c n) -> p c n", c=8),
                rwd.rearrange("(c p) n -> p c n", p=128))
    b.dma("gpsimd", "dg", b1sb[:].rearrange("p (c e) -> p c e", c=16),
          b1d.rearrange("(c p) e -> p c e", p=128))
    ldg = b.dma("gpsimd", "dg", b2sb[:], b2d)
    for e in ("scalar", "tensor", "vector"):
        b.wait(e, "ds", ldx)
        b.wait(e, "v", 1)
    vln = emit_layernorm(b, x, xln, banks, sq, rowbuf, c["ones_col"], c["ones_row"],
                         rb, mb_, c["epsb"])
    b.wait("tensor", "v", vln)
    b.wait("tensor", "dg", ldc)
    b.wait("tensor", "ds", b.counts["ds"])

    for sl in range(4):
        n0 = sl * 512
        for kc in range(8):
            def fmm(E, sl=sl, kc=kc, n0=n0):
                return E.matmul(banks[0][0:16, 0:512],
                                rwsb[:, kc * 16:(kc + 1) * 16],
                                xln[:, kc * TOK + n0: kc * TOK + n0 + 512],
                                start=(kc == 0), stop=(kc == 7))
            tmm = b.op("tensor", fmm, inc="t" if kc == 7 else None)
        b.wait("scalar", "t", tmm)
        b.wait("scalar", "dg", b.counts["dg"])

        def fev(E, n0=n0):
            return E.activation(logits[0:16, n0:n0 + 512], banks[0][0:16, 0:512],
                                AF.Identity, bias=rbsb[:])
        sev = b.op("scalar", fev, inc="s")
        b.wait("tensor", "s", sev)
    for tt in range(16):
        def ftr(E, tt=tt):
            return E.transpose(banks[1][0:128, 0:16], logits[0:16, tt * 128:(tt + 1) * 128],
                               idf[0:16, 0:16])
        ttr = b.op("tensor", ftr, inc="t")
        b.wait("vector", "t", ttr)
        vv = b.op("vector", lambda E, tt=tt: E.tensor_copy(ltok[:, tt * 16:(tt + 1) * 16],
                                                           banks[1][0:128, 0:16]), inc="v")
        b.wait("tensor", "v", vv)
    for tt in range(16):
        lt = ltok[:, tt * 16:(tt + 1) * 16]

        def vstep(fn):
            t = b.op("vector", fn, inc="v")
            b.wait("vector", "v", t)
            return t

        vstep(lambda E, lt=lt: E.reduce_max(sml[:, 0:1], lt, axis=AX, negate=True))
        vm = b.counts["v"]
        b.wait("scalar", "v", vm)
        sx = b.op("scalar", lambda E, lt=lt: E.activation(et, lt, AF.Exp, bias=sml[:, 0:1]),
                  inc="s")
        b.wait("vector", "s", sx)
        vstep(lambda E: E.tensor_copy(tsc, et))
        for r_ in range(3):
            vstep(lambda E: E.reduce_max(sml[:, 1:2], tsc, axis=AX))
            vstep(lambda E: E.tensor_scalar(tmsk, tsc, sml[:, 1:2], NEG_BIG,
                                            ALU.is_ge, ALU.mult))
            vstep(lambda E: E.tensor_add(tsc, tsc, tmsk))
        vstep(lambda E: E.reduce_max(sml[:, 1:2], tsc, axis=AX))
        vstep(lambda E: E.scalar_tensor_tensor(tmsk, et, sml[:, 1:2], et,
                                               ALU.is_ge, ALU.mult, accum_out=sml[:, 2:3]))
        vstep(lambda E: E.reciprocal(sml[:, 3:4], sml[:, 2:3]))
        vstep(lambda E, tt=tt: E.tensor_scalar_mul(ctok[:, tt * 16:(tt + 1) * 16],
                                                   tmsk, sml[:, 3:4]))
    vct = b.counts["v"]
    b.wait("tensor", "v", vct)
    for tt in range(16):
        def ftr(E, tt=tt):
            return E.transpose(banks[1][0:16, 0:128], ctok[:, tt * 16:(tt + 1) * 16], idf[:])
        ttr = b.op("tensor", ftr, inc="t")
        b.wait("vector", "t", ttr)
        vv = b.op("vector", lambda E, tt=tt: E.tensor_copy(cTb[0:16, tt * 128:(tt + 1) * 128],
                                                           banks[1][0:16, 0:128]), inc="v")
        b.wait("tensor", "v", vv)

    for e in range(16):
        b.wait("gpsimd", "t", b.counts["t"])
        lw1 = b.dma("gpsimd", "dg",
                    w1sb[:].rearrange("p (c n) -> p c n", c=8),
                    w1d[e].rearrange("(c p) n -> p c n", p=128))
        lw2 = b.dma("gpsimd", "dg",
                    w2sb[:].rearrange("p (c n) -> p c n", c=16),
                    w2d[e].rearrange("(c p) n -> p c n", p=128))
        b.wait("tensor", "v", b.counts["v"])
        for sl in range(4):
            n0 = sl * 512
            def fbc(E, e=e, n0=n0):
                return E.matmul(banks[4][:, 0:512], eyeb[0:16, e * 128:(e + 1) * 128],
                                cTb[0:16, n0:n0 + 512], start=True, stop=True)
            tb = b.op("tensor", fbc, inc="t")
            b.wait("vector", "t", tb)
            vv = b.op("vector", lambda E, n0=n0: E.tensor_copy(cbc[:, n0:n0 + 512],
                                                               banks[4][:, 0:512]), inc="v")
            b.wait("tensor", "v", vv)
        b.wait("tensor", "dg", lw1)
        for sl in range(4):
            n0 = sl * 512
            b.wait("vector", "t", b.counts["t"])
            ev_war = {}
            for ic in range(16):
                bk = banks[ic % 2]
                if (ic % 2) in ev_war:
                    b.wait("tensor", "s", ev_war[ic % 2])
                for kc in range(8):
                    def fmm(E, e=e, sl=sl, ic=ic, kc=kc, n0=n0, bk=bk):
                        return E.matmul(bk[:, 0:512],
                                        w1sb[:, kc * 2048 + ic * 128:kc * 2048 + (ic + 1) * 128],
                                        xln[:, kc * TOK + n0:kc * TOK + n0 + 512],
                                        start=(kc == 0), stop=(kc == 7))
                    tmm = b.op("tensor", fmm, inc="t" if kc == 7 else None)
                b.wait("scalar", "t", tmm)

                def fge(E, e=e, ic=ic, bk=bk):
                    return E.activation(gtmp, bk[:, 0:512], AF.Gelu,
                                        bias=b1sb[:, ic * 16 + e:ic * 16 + e + 1])
                sge = b.op("scalar", fge, inc="s")
                ev_war[ic % 2] = sge
                b.wait("vector", "s", sge)

                def fmu(E, ic=ic, n0=n0):
                    return E.tensor_mul(hm[:, ic * 512:(ic + 1) * 512], gtmp,
                                        cbc[:, n0:n0 + 512])
                vmu = b.op("vector", fmu, inc="v")
                b.wait("scalar", "v", vmu)
            vhm = b.counts["v"]
            b.wait("tensor", "v", vhm)
            if e == 0 and sl == 0:
                b.wait("tensor", "dg", lw2)
            ev2_war = {}
            for oc in range(8):
                bk = banks[2 + oc % 2]
                if (oc % 2) in ev2_war:
                    b.wait("tensor", "v", ev2_war[oc % 2])
                for ic in range(16):
                    def fm2(E, e=e, sl=sl, oc=oc, ic=ic, n0=n0, bk=bk):
                        return E.matmul(bk[:, 0:512],
                                        w2sb[:, ic * 1024 + oc * 128:ic * 1024 + (oc + 1) * 128],
                                        hm[:, ic * 512:(ic + 1) * 512],
                                        start=(ic == 0), stop=(e > 0 and ic == 15))
                    tm2 = b.op("tensor", fm2,
                               inc="t" if (e > 0 and ic == 15) else None)
                if e == 0:
                    def fb2(E, oc=oc, n0=n0, bk=bk):
                        return E.matmul(bk[:, 0:512], b2sb[0:16, oc * 128:(oc + 1) * 128],
                                        cTb[0:16, n0:n0 + 512], start=False, stop=True)
                    tm2 = b.op("tensor", fb2, inc="t")
                b.wait("vector", "t", tm2)

                def fad(E, oc=oc, n0=n0, bk=bk):
                    xs = x[:, oc * TOK + n0:oc * TOK + n0 + 512]
                    return E.tensor_add(xs, xs, bk[:, 0:512])
                ev2_war[oc % 2] = b.op("vector", fad, inc="v")
    vfin = b.counts["v"]
    b.wait("sync", "v", vfin)
    st = b.dma("sync", "st", xout.rearrange("(c p) n -> p c n", p=128),
               x[:].rearrange("p (c n) -> p c n", c=8))
    b.wait("sync", "st", st)
    return b.finalize()


def build_l1_qkv():
    nc = bass.Bass("TRN2", target_bir_lowering=False)
    xin = nc.dram_tensor("xin", [D, TOK], F32, kind="ExternalInput").ap()
    wds = [nc.dram_tensor(n, [D, D], F32, kind="ExternalInput").ap()
           for n in ("qw", "kw", "vw")]
    outs = [nc.dram_tensor(n, [D, TOK], BF16, kind="ExternalOutput").ap()
            for n in ("qout", "kout", "vout")]

    b = Builder(nc)
    x = b.sbuf("x", [128, 8 * TOK], F32)
    xln = b.sbuf("xln", [128, 8 * TOK], BF16)
    arB = b.sbuf("arB", [128, 6144], F32)
    sq = arB.ap()[:, 0:TOK]
    rb = arB.ap()[:, TOK:2 * TOK]
    mb_ = arB.ap()[:, 2 * TOK:3 * TOK]
    stg = [arB.ap().bitcast(BF16)[:, 0:TOK], arB.ap().bitcast(BF16)[:, TOK:2 * TOK]]
    rowbuf = b.sbuf("rowbuf", [1, 4 * TOK], F32)
    w_sb = [b.sbuf(f"w_sb{i}", [128, 8 * 1024], BF16) for i in range(2)]
    banks = [b.psum(f"bk{i}") for i in range(8)]
    c = alloc_consts(b)

    ldx = b.dma("sync", "ds", x[:].rearrange("p (c n) -> p c n", c=8),
                xin.rearrange("(c p) n -> p c n", p=128))
    for e in ("scalar", "tensor", "vector"):
        b.wait(e, "ds", ldx)
        b.wait(e, "v", 1)
    vln = emit_layernorm(b, x, xln, banks, sq, rowbuf, c["ones_col"], c["ones_row"],
                         rb, mb_, c["epsb"])
    b.wait("tensor", "v", vln)
    wdma = [0, 0]
    for wi in range(3):
        wb = w_sb[wi % 2]
        b.wait("gpsimd", "t", b.counts["t"])
        wdma[wi % 2] = b.dma("gpsimd", "dg", wb[:].rearrange("p (c n) -> p c n", c=8),
                             wds[wi].rearrange("(c p) n -> p c n", p=128))
        b.wait("tensor", "dg", wdma[wi % 2])
        scale = 0.125 if wi == 0 else 1.0
        state = {}
        stg_war = [0, 0]
        for oc in range(8):
            sg = stg[oc % 2]
            b.wait("scalar", "st", stg_war[oc % 2])
            for sl in range(4):
                n0 = sl * 512
                bid = sl % 2
                bk = banks[bid]
                if bid in state:
                    b.wait("tensor", "s", state[bid])
                for kc in range(8):
                    def fmm(E, oc=oc, kc=kc, n0=n0, bk=bk, wb=wb):
                        return E.matmul(bk[:, 0:512],
                                        wb[:, kc * 1024 + oc * 128:kc * 1024 + (oc + 1) * 128],
                                        xln[:, kc * TOK + n0:kc * TOK + n0 + 512],
                                        start=(kc == 0), stop=(kc == 7))
                    tmm = b.op("tensor", fmm, inc="t" if kc == 7 else None)
                b.wait("scalar", "t", tmm)

                def fev(E, n0=n0, bk=bk, sg=sg, scale=scale):
                    return E.activation(sg[:, n0:n0 + 512], bk[:, 0:512], AF.Identity,
                                        scale=scale)
                state[bid] = b.op("scalar", fev, inc="s")
            b.wait("sync", "s", b.counts["s"])
            stg_war[oc % 2] = b.dma("sync", "st", outs[wi][oc * 128:(oc + 1) * 128, :], sg)
    b.wait("sync", "st", b.counts["st"])
    return b.finalize()


def emit_l1_attn_head(b, h, Qe, Ke, Vh, Vtok, bufs, is_first):
    Pb, sink = bufs["Pb"], bufs["sink"]
    mz, bpack, banks = bufs["mz"], bufs["bpack"], bufs["banks"]
    idb, m1, m2 = bufs["idb"], bufs["m1"], bufs["m2"]
    bts = bufs["bts"]
    O = bufs["O"]

    b.wait("tensor", "v", b.counts["v"])
    pT = banks[7].ap().bitcast(BF16)
    for kt in range(32):
        def ftr(E, kt=kt):
            return E.transpose(pT[0:128, 0:64], Vh[0:64, kt * 128:(kt + 1) * 128],
                               idb[0:64, 0:64])
        ttr = b.op("tensor", ftr, inc="t")
        b.wait("vector", "t", ttr)
        def fev(E, kt=kt):
            return E.tensor_copy(Vtok[:, kt * 64:(kt + 1) * 64], pT[0:128, 0:64])
        vev = b.op("vector", fev, inc="v")
        b.wait("tensor", "v", vev)
    for lq in range(16):
        km = L1_KMAX[lq]
        nk = km * 128
        nsl = (nk + 511) // 512
        for s in range(nsl):
            k0, k1 = s * 512, min(nk, s * 512 + 512)
            def fmm(E, lq=lq, s=s, k0=k0, k1=k1):
                return E.matmul(banks[s][:, 0:k1 - k0], Qe[0:67, lq * 128:(lq + 1) * 128],
                                Ke[0:67, k0:k1], start=True, stop=True)
            tp1 = b.op("tensor", fmm, inc="t" if s == nsl - 1 else None)
        b.wait("vector", "t", tp1)
        for j in range(3):
            kt = km - 3 + j
            ds, dc = kt // 4, (kt % 4) * 128
            def fmask(E, lq=lq, j=j, ds=ds, dc=dc):
                return E.tensor_add(banks[ds][:, dc:dc + 128], banks[ds][:, dc:dc + 128],
                                    m1[:, (lq * 3 + j) * 128:(lq * 3 + j + 1) * 128])
            b.op("vector", fmask)
        for s in range(nsl):
            k0, k1 = s * 512, min(nk, s * 512 + 512)
            def fred(E, s=s, k0=k0, k1=k1):
                return E.reduce_max(mz[:, 96 + s:97 + s], banks[s][:, 0:k1 - k0], axis=AX)
            vpre = b.op("vector", fred, inc="v" if s == nsl - 1 else None)
        b.wait("vector", "v", vpre)
        def fcomb(E, lq=lq, nsl=nsl):
            return E.reduce_max(mz[:, lq:lq + 1], mz[:, 96:96 + nsl], axis=AX, negate=True)
        vred = b.op("vector", fcomb, inc="v")
        b.wait("scalar", "v", vred)
        for s in range(nsl):
            k0, k1 = s * 512, min(nk, s * 512 + 512)
            def fexp(E, lq=lq, s=s, k0=k0, k1=k1):
                return E.activation(sink[:, 0:k1 - k0], banks[s][:, 0:k1 - k0], AF.Exp,
                                    bias=mz[:, lq:lq + 1], accum_out=mz[:, 96 + s:97 + s])
            sexp = b.op("scalar", fexp, inc="s" if s == nsl - 1 else None)
        b.wait("vector", "s", sexp)
        def fz(E, lq=lq, nsl=nsl):
            return E.reduce_sum(mz[:, 32 + lq:33 + lq], mz[:, 96:96 + nsl], axis=AX)
        vz = b.op("vector", fz, inc="v")
        b.wait("tensor", "s", sexp)
        b.wait("tensor", "v", vz)
    b.wait("scalar", "v", vz)
    sln = b.op("scalar", lambda E: E.activation(mz[:, 96:112], mz[:, 32:48], AF.Ln), inc="s")
    b.wait("vector", "s", sln)

    def vstep(fn):
        t = b.op("vector", fn, inc="v")
        b.wait("vector", "v", t)
        return t

    vstep(lambda E: E.tensor_sub(mz[:, 64:80], mz[:, 0:16], mz[:, 96:112]))
    bp = bpack
    vstep(lambda E: E.tensor_copy(bp[:, 0:48:3], mz[:, 64:80]))
    vstep(lambda E: E.tensor_copy(mz[:, 96:112], bp[:, 0:48:3]))
    vstep(lambda E: E.tensor_sub(mz[:, 112:128], mz[:, 64:80], mz[:, 96:112]))
    vstep(lambda E: E.tensor_copy(bp[:, 1:48:3], mz[:, 112:128]))
    vstep(lambda E: E.tensor_copy(mz[:, 96:112], bp[:, 1:48:3]))
    vstep(lambda E: E.tensor_sub(mz[:, 112:128], mz[:, 112:128], mz[:, 96:112]))
    vsp = vstep(lambda E: E.tensor_copy(bp[:, 2:48:3], mz[:, 112:128]))
    b.wait("tensor", "v", vsp)
    for lq in range(16):
        def ftr(E, lq=lq):
            return E.transpose(pT[0:3, 0:128], bp[:, lq * 3:lq * 3 + 3], idb[:])
        ttr = b.op("tensor", ftr, inc="t")
        b.wait("vector", "t", ttr)
        def fev(E, lq=lq):
            return E.tensor_copy(bts[0:3, lq * 128:(lq + 1) * 128], pT[0:3, 0:128])
        vev = b.op("vector", fev, inc="v")
        b.wait("tensor", "v", vev)
    b.wait("sync", "v", b.counts["v"])
    bdma = b.dma("sync", "ds", Qe[67:70, :], bts[0:3, :])
    b.wait("tensor", "ds", bdma)
    p2b = [banks[0], banks[1]]
    avb = [banks[2], banks[3]]
    last_av_evac = 0
    prev_tav = prev_sexp = 0
    for g in range(4):
        av = avb[g % 2]
        if last_av_evac:
            b.wait("tensor", "v", last_av_evac)
        if g > 0:
            b.wait("scalar", "t", prev_tav)
            b.wait("tensor", "s", prev_sexp)
        kms = [L1_KMAX[4 * g + i] for i in range(4)]
        ktmax = kms[3] - 1
        sexp_hist = {}
        tav_hist = {}
        for kt in range(ktmax + 1):
            nvalid = sum(1 for km in kms if kt < km)
            co = (4 - nvalid) * 128
            p2 = p2b[kt % 2]
            if kt >= 2:
                b.wait("tensor", "s", sexp_hist[kt - 2])
            def fmm(E, g=g, kt=kt, co=co, p2=p2):
                return E.matmul(p2[:, co:512], Ke[0:70, kt * 128:(kt + 1) * 128],
                                Qe[0:70, g * 512 + co:(g + 1) * 512], start=True, stop=True)
            tp2 = b.op("tensor", fmm, inc="t")
            nmask = 0
            for i in range(4):
                lq = 4 * g + i
                j = kt - (kms[i] - 3)
                if 0 <= j < 3 and kt < kms[i]:
                    def fmask(E, lq=lq, j=j, i=i, p2=p2):
                        return E.tensor_add(p2[:, i * 128:(i + 1) * 128],
                                            p2[:, i * 128:(i + 1) * 128],
                                            m2[:, (lq * 3 + j) * 128:(lq * 3 + j + 1) * 128])
                    if nmask == 0:
                        b.wait("vector", "t", tp2)
                    vm = b.op("vector", fmask, inc="v")
                    nmask += 1
            if nmask:
                b.wait("scalar", "v", vm)
            else:
                b.wait("scalar", "t", tp2)
            if kt >= 2:
                b.wait("scalar", "t", tav_hist[kt - 2])
            pb = Pb[kt % 2]
            def fexp(E, kt=kt, co=co, p2=p2, pb=pb):
                return E.activation(pb[:, co:512], p2[:, co:512], AF.Exp)
            sexp = b.op("scalar", fexp, inc="s")
            sexp_hist[kt] = sexp
            b.wait("tensor", "s", sexp)
            def fav(E, g=g, kt=kt, co=co, pb=pb, av=av):
                return E.matmul(av[0:64, co:512], Vtok[:, kt * 64:(kt + 1) * 64],
                                pb[:, co:512], start=(kt == 0), stop=(kt == ktmax))
            tav_hist[kt] = b.op("tensor", fav, inc="t")
        tav = b.counts["t"]
        prev_tav, prev_sexp = tav, sexp_hist[ktmax]
        b.wait("vector", "t", tav)
        cj, off = (64 * h) // 128, (64 * h) % 128
        def fevac(E, g=g, cj=cj, off=off, av=av):
            return E.tensor_copy(O[off:off + 64, cj * TOK + g * 512: cj * TOK + (g + 1) * 512],
                                 av[0:64, 0:512])
        last_av_evac = b.op("vector", fevac, inc="v")
    return last_av_evac


def build_l1_attn():
    nc = bass.Bass("TRN2", target_bir_lowering=False)
    xin = nc.dram_tensor("xin", [D, TOK], F32, kind="ExternalInput").ap()
    qin = nc.dram_tensor("qin", [D, TOK], BF16, kind="ExternalInput").ap()
    kgd = nc.dram_tensor("kg", [D, S_LEN], BF16, kind="ExternalInput").ap()
    vgd = nc.dram_tensor("vg", [D, S_LEN], BF16, kind="ExternalInput").ap()
    owd = nc.dram_tensor("ow", [D, D], F32, kind="ExternalInput").ap()
    al3d = nc.dram_tensor("al3", [16, 3, S_LEN], BF16, kind="ExternalInput").ap()
    ones3d = nc.dram_tensor("ones3", [3, S_LEN], BF16, kind="ExternalInput").ap()
    m1d = nc.dram_tensor("m1", [128, 6144], BF16, kind="ExternalInput").ap()
    m2d = nc.dram_tensor("m2", [128, 6144], BF16, kind="ExternalInput").ap()
    identd = nc.dram_tensor("ident", [128, 128], BF16, kind="ExternalInput").ap()
    xout = nc.dram_tensor("xout", [D, TOK], F32, kind="ExternalOutput").ap()

    b = Builder(nc)
    x = b.sbuf("x", [128, 8 * TOK], F32)
    O = b.sbuf("O", [128, 8 * TOK], BF16)
    Qes = [b.sbuf(f"Qe{i}", [128, TOK], BF16) for i in range(2)]
    Kes = [b.sbuf(f"Ke{i}", [128, S_LEN], BF16) for i in range(2)]
    Vhs = [b.sbuf(f"Vh{i}", [64, S_LEN], BF16) for i in range(2)]
    Vtoks = [b.sbuf(f"Vtok{i}", [128, 2048], BF16) for i in range(2)]
    ow_sb = b.sbuf("ow_sb", [128, 8 * 1024], BF16)
    bufs = {
        "Pb": [b.sbuf(f"Pb{i}", [128, 512], BF16) for i in range(2)],
        "sink": b.sbuf("sink", [128, 512], BF16),
        "mz": b.sbuf("mz", [128, 128], F32),
        "bpack": b.sbuf("bpack", [128, 48], BF16),
        "bts": b.sbuf("bts", [3, TOK], BF16),
        "m1": b.sbuf("m1_sb", [128, 6144], BF16),
        "m2": b.sbuf("m2_sb", [128, 6144], BF16),
        "idb": b.sbuf("idb_sb", [128, 128], BF16),
        "O": O.ap(),
    }
    banks = [b.psum(f"bk{i}") for i in range(8)]
    bufs["banks"] = banks

    ldx = b.dma("sync", "ds", x[:].rearrange("p (c n) -> p c n", c=8),
                xin.rearrange("(c p) n -> p c n", p=128))
    b.dma("sync", "ds", bufs["m1"][:], m1d)
    b.dma("sync", "ds", bufs["m2"][:], m2d)
    b.dma("sync", "ds", bufs["idb"][:], identd)
    for i in range(2):
        b.dma("sync", "ds", Qes[i][64:67, :], ones3d[:, 0:TOK])
        b.dma("sync", "ds", Kes[i][67:70, :], ones3d)
    ldow = b.dma("gpsimd", "dg", ow_sb[:].rearrange("p (c n) -> p c n", c=8),
                 owd.rearrange("(c p) n -> p c n", p=128))
    ldc = b.counts["ds"]
    for e in ("scalar", "tensor", "vector"):
        b.wait(e, "ds", ldc)

    hdma = [0, 0]
    for h in range(16):
        sl_ = h % 2
        Qe, Ke, Vh, Vtok = Qes[sl_], Kes[sl_], Vhs[sl_], Vtoks[sl_]
        b.wait("sync", "t", b.counts["t"])
        b.dma("sync", "ds", Qe[0:64, :], qin[64 * h:64 * h + 64, :])
        b.dma("sync", "ds", Ke[0:64, :], kgd[64 * h:64 * h + 64, :])
        b.dma("sync", "ds", Ke[64:67, :], al3d[h])
        hdma[sl_] = b.dma("sync", "ds", Vh[0:64, :], vgd[64 * h:64 * h + 64, :])
        for e in ("tensor", "vector", "scalar"):
            b.wait(e, "ds", hdma[sl_])
        emit_l1_attn_head(b, h, Qe, Ke, Vh, Vtok, bufs, h == 0)
    vO = b.counts["v"]
    b.wait("tensor", "v", vO)
    b.wait("tensor", "dg", ldow)
    ev_war = {}
    for oc in range(8):
        for sl in range(4):
            n0 = sl * 512
            bid = 4 + (oc * 4 + sl) % 2
            bk = banks[bid]
            if bid in ev_war:
                b.wait("tensor", "v", ev_war[bid])
            for kc in range(8):
                def fmm(E, oc=oc, kc=kc, n0=n0, bk=bk):
                    return E.matmul(bk[:, 0:512],
                                    ow_sb[:, kc * 1024 + oc * 128:kc * 1024 + (oc + 1) * 128],
                                    O[:, kc * TOK + n0:kc * TOK + n0 + 512],
                                    start=(kc == 0), stop=(kc == 7))
                tmm = b.op("tensor", fmm, inc="t" if kc == 7 else None)
            b.wait("vector", "t", tmm)

            def fad(E, oc=oc, n0=n0, bk=bk):
                xs = x[:, oc * TOK + n0:oc * TOK + n0 + 512]
                return E.tensor_add(xs, xs, bk[:, 0:512])
            ev_war[bid] = b.op("vector", fad, inc="v")
    vfin = b.counts["v"]
    b.wait("sync", "v", vfin)
    st = b.dma("sync", "st", xout.rearrange("(c p) n -> p c n", p=128),
               x[:].rearrange("p (c n) -> p c n", c=8))
    b.wait("sync", "st", st)
    return b.finalize()


def host_l1_consts():
    slopes = np.asarray([(2.0 ** (-8.0 / 16)) ** (i + 1) for i in range(16)], np.float64)
    k = np.arange(S_LEN, dtype=np.float64)
    al3 = np.zeros((16, 3, S_LEN), BF16NP)
    for h in range(16):
        v = slopes[h] * k
        h1 = v.astype(BF16NP)
        r1 = v - h1.astype(np.float64)
        h2 = r1.astype(BF16NP)
        r2 = r1 - h2.astype(np.float64)
        h3 = r2.astype(BF16NP)
        al3[h, 0], al3[h, 1], al3[h, 2] = h1, h2, h3
    ones3 = np.ones((3, S_LEN), BF16NP)
    ident = np.eye(128, dtype=BF16NP)
    return al3, ones3, ident


def build_l1_ffn():
    nc = bass.Bass("TRN2", target_bir_lowering=False)
    xin = nc.dram_tensor("xin", [D, TOK], F32, kind="ExternalInput").ap()
    seld = nc.dram_tensor("selt", [D, 32], F32, kind="ExternalInput").ap()
    keyd = nc.dram_tensor("keys", [32, D, 512], F32, kind="ExternalInput").ap()
    vald = nc.dram_tensor("values", [32, 512, D], F32, kind="ExternalInput").ap()
    idfd = nc.dram_tensor("idf", [128, 128], F32, kind="ExternalInput").ap()
    eyed = nc.dram_tensor("eyeb32", [32, 32 * 128], BF16, kind="ExternalInput").ap()
    xout = nc.dram_tensor("xout", [D, TOK], F32, kind="ExternalOutput").ap()

    b = Builder(nc)
    x = b.sbuf("x", [128, 8 * TOK], F32)
    xln = b.sbuf("xln", [128, 8 * TOK], BF16)
    arB = b.sbuf("arB", [128, 6144], F32)
    sq = arB.ap()[:, 0:TOK]
    rb = arB.ap()[:, TOK:2 * TOK]
    mb_ = arB.ap()[:, 2 * TOK:3 * TOK]
    gates = arB.ap()[0:32, 0:TOK]
    gtok = arB.ap()[:, 2048:2080]
    et = arB.ap()[:, 2560:2592]
    tsc = arB.ap()[:, 2592:2624]
    tmsk = arB.ap()[:, 2624:2656]
    gftok = arB.ap()[:, 2656:3168]
    gbc = arB.ap().bitcast(BF16)[:, 6336:8384]
    scsc = arB.ap().bitcast(BF16)[:, 8384:10432]
    sctmp = arB.ap().bitcast(BF16)[:, 10432:10944]
    rowbuf = b.sbuf("rowbuf", [1, 4 * TOK], F32)
    k_sb = b.sbuf("k_sb", [128, 8 * 512], BF16)
    v_sb = b.sbuf("v_sb", [128, 4 * 1024], BF16)
    selw = b.sbuf("selw", [128, 8 * 32], BF16)
    gT = b.sbuf("gT", [32, TOK], BF16)
    idf = b.sbuf("idf_sb", [128, 128], F32)
    eyeb = b.sbuf("eyeb_sb", [32, 32 * 128], BF16)
    sml = b.sbuf("sml", [128, 8], F32)
    banks = [b.psum(f"bk{i}") for i in range(8)]
    c = alloc_consts(b)

    ldx = b.dma("sync", "ds", x[:].rearrange("p (c n) -> p c n", c=8),
                xin.rearrange("(c p) n -> p c n", p=128))
    b.dma("sync", "ds", idf[:], idfd)
    b.dma("sync", "ds", eyeb[:], eyed)
    ldsel = b.dma("gpsimd", "dg", selw[:].rearrange("p (c n) -> p c n", c=8),
                  seld.rearrange("(c p) n -> p c n", p=128))
    for e in ("scalar", "tensor", "vector"):
        b.wait(e, "ds", ldx)
        b.wait(e, "v", 1)
    vln = emit_layernorm(b, x, xln, banks, sq, rowbuf, c["ones_col"], c["ones_row"],
                         rb, mb_, c["epsb"])
    b.wait("tensor", "v", vln)
    b.wait("tensor", "dg", ldsel)
    b.wait("tensor", "ds", b.counts["ds"])

    for sl in range(4):
        n0 = sl * 512
        for kc in range(8):
            def fmm(E, sl=sl, kc=kc, n0=n0):
                return E.matmul(banks[0][0:32, 0:512],
                                selw[:, kc * 32:(kc + 1) * 32],
                                xln[:, kc * TOK + n0: kc * TOK + n0 + 512],
                                start=(kc == 0), stop=(kc == 7))
            tmm = b.op("tensor", fmm, inc="t" if kc == 7 else None)
        b.wait("scalar", "t", tmm)

        def fev(E, n0=n0):
            return E.activation(gates[:, n0:n0 + 512], banks[0][0:32, 0:512], AF.Sigmoid)
        sev = b.op("scalar", fev, inc="s")
        b.wait("tensor", "s", sev)
    for tt in range(16):
        def ftr(E, tt=tt):
            return E.transpose(banks[1][0:128, 0:32], gates[:, tt * 128:(tt + 1) * 128],
                               idf[0:32, 0:32])
        ttr = b.op("tensor", ftr, inc="t")
        b.wait("vector", "t", ttr)
        vv = b.op("vector", lambda E, tt=tt: E.tensor_copy(gtok, banks[1][0:128, 0:32]),
                  inc="v")
        b.wait("tensor", "v", vv)

        def vstep(fn):
            t = b.op("vector", fn, inc="v")
            b.wait("vector", "v", t)
            return t

        vstep(lambda E: E.tensor_copy(tsc, gtok))
        for r_ in range(3):
            vstep(lambda E: E.reduce_max(sml[:, 1:2], tsc, axis=AX))
            vstep(lambda E: E.tensor_scalar(tmsk, tsc, sml[:, 1:2], NEG_BIG,
                                            ALU.is_ge, ALU.mult))
            vstep(lambda E: E.tensor_add(tsc, tsc, tmsk))
        vstep(lambda E: E.reduce_max(sml[:, 1:2], tsc, axis=AX))
        vstep(lambda E: E.scalar_tensor_tensor(gftok[:, 0:32], gtok, sml[:, 1:2], gtok,
                                               ALU.is_ge, ALU.mult))
        vgf = b.counts["v"]
        b.wait("tensor", "v", vgf)
        def ftr2(E, tt=tt):
            return E.transpose(banks[1][0:32, 0:128], gftok[:, 0:32], idf[:])
        ttr2 = b.op("tensor", ftr2, inc="t")
        b.wait("vector", "t", ttr2)
        vv2 = b.op("vector", lambda E, tt=tt: E.tensor_copy(gT[0:32, tt * 128:(tt + 1) * 128],
                                                            banks[1][0:32, 0:128]), inc="v")
        b.wait("tensor", "v", vv2)

    for e in range(32):
        b.wait("gpsimd", "t", b.counts["t"])
        lk = b.dma("gpsimd", "dg", k_sb[:].rearrange("p (c n) -> p c n", c=8),
                   keyd[e].rearrange("(c p) n -> p c n", p=128))
        lv = b.dma("gpsimd", "dg", v_sb[:].rearrange("p (c n) -> p c n", c=4),
                   vald[e].rearrange("(c p) n -> p c n", p=128))
        b.wait("tensor", "v", b.counts["v"])
        for sl in range(4):
            n0 = sl * 512
            def fbc(E, e=e, n0=n0):
                return E.matmul(banks[4][:, 0:512], eyeb[0:32, e * 128:(e + 1) * 128],
                                gT[0:32, n0:n0 + 512], start=True, stop=True)
            tb = b.op("tensor", fbc, inc="t")
            b.wait("vector", "t", tb)
            vv = b.op("vector", lambda E, n0=n0: E.tensor_copy(gbc[:, n0:n0 + 512],
                                                               banks[4][:, 0:512]), inc="v")
            b.wait("tensor", "v", vv)
        b.wait("tensor", "dg", lk)
        for sl in range(4):
            n0 = sl * 512
            b.wait("vector", "t", b.counts["t"])
            ev_war = {}
            for ic in range(4):
                bk = banks[ic % 2]
                if (ic % 2) in ev_war:
                    b.wait("tensor", "s", ev_war[ic % 2])
                for kc in range(8):
                    def fmm(E, e=e, sl=sl, ic=ic, kc=kc, n0=n0, bk=bk):
                        return E.matmul(bk[:, 0:512],
                                        k_sb[:, kc * 512 + ic * 128:kc * 512 + (ic + 1) * 128],
                                        xln[:, kc * TOK + n0:kc * TOK + n0 + 512],
                                        start=(kc == 0), stop=(kc == 7))
                    tmm = b.op("tensor", fmm, inc="t" if kc == 7 else None)
                b.wait("scalar", "t", tmm)

                def fre(E, ic=ic, bk=bk):
                    return E.activation(sctmp, bk[:, 0:512], AF.Relu)
                sre = b.op("scalar", fre, inc="s")
                ev_war[ic % 2] = sre
                b.wait("vector", "s", sre)

                def fmu(E, ic=ic, n0=n0):
                    return E.tensor_mul(scsc[:, ic * 512:(ic + 1) * 512], sctmp,
                                        gbc[:, n0:n0 + 512])
                vmu = b.op("vector", fmu, inc="v")
                b.wait("scalar", "v", vmu)
            vhm = b.counts["v"]
            b.wait("tensor", "v", vhm)
            if e == 0 and sl == 0:
                b.wait("tensor", "dg", lv)
            ev2_war = {}
            for oc in range(8):
                bk = banks[2 + oc % 2]
                if (oc % 2) in ev2_war:
                    b.wait("tensor", "v", ev2_war[oc % 2])
                for ic in range(4):
                    def fm2(E, oc=oc, ic=ic, bk=bk):
                        return E.matmul(bk[:, 0:512],
                                        v_sb[:, ic * 1024 + oc * 128:ic * 1024 + (oc + 1) * 128],
                                        scsc[:, ic * 512:(ic + 1) * 512],
                                        start=(ic == 0), stop=(ic == 3))
                    tm2 = b.op("tensor", fm2, inc="t" if ic == 3 else None)
                b.wait("vector", "t", tm2)

                def fad(E, oc=oc, n0=n0, bk=bk):
                    xs = x[:, oc * TOK + n0:oc * TOK + n0 + 512]
                    return E.tensor_add(xs, xs, bk[:, 0:512])
                ev2_war[oc % 2] = b.op("vector", fad, inc="v")
    vfin = b.counts["v"]
    b.wait("sync", "v", vfin)
    st = b.dma("sync", "st", xout.rearrange("(c p) n -> p c n", p=128),
               x[:].rearrange("p (c n) -> p c n", c=8))
    b.wait("sync", "st", st)
    return b.finalize()


def _host_consts():
    tri1, tri2 = host_tri_masks()
    ident = np.eye(128, dtype=BF16NP)
    idf = np.eye(128, dtype=np.float32)
    eyeb16 = np.zeros((16, 16 * 128), BF16NP)
    for e in range(16):
        eyeb16[e, e * 128:(e + 1) * 128] = 1.0
    eyeb32 = np.zeros((32, 32 * 128), BF16NP)
    for e in range(32):
        eyeb32[e, e * 128:(e + 1) * 128] = 1.0
    al3, ones3, _ = host_l1_consts()
    m1 = {}
    m2 = {}
    for is_a in (True, False):
        a1 = host_l1_masks(is_a, transposed=False).astype(BF16NP)
        a2 = host_l1_masks(is_a, transposed=True).astype(BF16NP)
        m1[is_a] = np.ascontiguousarray(a1.transpose(2, 0, 1, 3).reshape(128, 6144))
        m2[is_a] = np.ascontiguousarray(a2.transpose(2, 0, 1, 3).reshape(128, 6144))
    return dict(tri1=tri1, tri2=tri2, ident=ident, idf=idf, eyeb16=eyeb16,
                eyeb32=eyeb32, al3=al3, ones3=ones3, m1=m1, m2=m2)


def _assemble_global(parts, bi):
    out = np.empty((D, S_LEN), BF16NP)
    for half, ch in enumerate((CH_A, CH_B)):
        src = parts[2 * bi + half]
        for li, gc in enumerate(ch):
            out[:, 256 * gc:256 * gc + 256] = src[:, 256 * li:256 * li + 256]
    return out


def kernel_multilaunch(L0_attn_inputs, L0_attn_qkv_kernel, L0_attn_qkv_bias,
           L0_ffn_router_w, L0_ffn_router_b, L0_ffn_w1, L0_ffn_b1,
           L0_ffn_w2, L0_ffn_b2, L1_attn_q_w, L1_attn_k_w, L1_attn_v_w,
           L1_attn_o_w, L1_ffn_expert_sel, L1_ffn_keys, L1_ffn_values):
    cst = _host_consts()
    cores = list(range(8))
    shards = shard_tokens(np.asarray(L0_attn_inputs, dtype=np.float32))

    nc = build_l0_qkv()
    res = run_bass_kernel_spmd(
        nc, [{"xin": s_, "wq": np.asarray(L0_attn_qkv_kernel),
              "bq": np.asarray(L0_attn_qkv_bias)} for s_ in shards], core_ids=cores)
    qkvs = [r["qkvout"] for r in res.results]

    nc = build_l0_attn()
    res = run_bass_kernel_spmd(
        nc, [{"xin": s_, "qkv": q_, "tri1": cst["tri1"], "tri2": cst["tri2"],
              "ident": cst["ident"]} for s_, q_ in zip(shards, qkvs)], core_ids=cores)
    shards = [r["xout"] for r in res.results]

    nc = build_l0_ffn()
    b1t = np.ascontiguousarray(np.asarray(L0_ffn_b1).T)
    res = run_bass_kernel_spmd(
        nc, [{"xin": s_, "rw": np.asarray(L0_ffn_router_w),
              "rb_in": np.asarray(L0_ffn_router_b).reshape(16, 1),
              "w1": np.asarray(L0_ffn_w1), "b1t": b1t,
              "w2": np.asarray(L0_ffn_w2), "b2": np.asarray(L0_ffn_b2),
              "idf": cst["idf"], "eyeb": cst["eyeb16"]} for s_ in shards],
        core_ids=cores)
    shards = [r["xout"] for r in res.results]

    nc = build_l1_qkv()
    res = run_bass_kernel_spmd(
        nc, [{"xin": s_, "qw": np.asarray(L1_attn_q_w), "kw": np.asarray(L1_attn_k_w),
              "vw": np.asarray(L1_attn_v_w)} for s_ in shards], core_ids=cores)
    qs = [r["qout"] for r in res.results]
    ks = [r["kout"] for r in res.results]
    vs = [r["vout"] for r in res.results]
    kg = [_assemble_global(ks, bi) for bi in range(4)]
    vg = [_assemble_global(vs, bi) for bi in range(4)]

    nc = build_l1_attn()
    in_maps = []
    for ci in range(8):
        bi, half = ci // 2, ci % 2
        is_a = (half == 0)
        in_maps.append({"xin": shards[ci], "qin": qs[ci], "kg": kg[bi], "vg": vg[bi],
                        "ow": np.asarray(L1_attn_o_w), "al3": cst["al3"],
                        "ones3": cst["ones3"], "m1": cst["m1"][is_a],
                        "m2": cst["m2"][is_a], "ident": cst["ident"]})
    res = run_bass_kernel_spmd(nc, in_maps, core_ids=cores)
    shards = [r["xout"] for r in res.results]

    nc = build_l1_ffn()
    selt = np.ascontiguousarray(np.asarray(L1_ffn_expert_sel).T)
    res = run_bass_kernel_spmd(
        nc, [{"xin": s_, "selt": selt, "keys": np.asarray(L1_ffn_keys),
              "values": np.asarray(L1_ffn_values), "idf": cst["idf"],
              "eyeb32": cst["eyeb32"]} for s_ in shards], core_ids=cores)
    return unshard_tokens([r["xout"] for r in res.results], np.float32)


def build_fused():
    nc = bass.Bass("TRN2", target_bir_lowering=False)
    xin = nc.dram_tensor("xin", [D, TOK], F32, kind="ExternalInput").ap()
    wqkv = nc.dram_tensor("wqkv", [D, 3072], F32, kind="ExternalInput").ap()
    bqkv = nc.dram_tensor("bqkv", [1, 3072], F32, kind="ExternalInput").ap()
    rwd = nc.dram_tensor("rw", [D, 16], F32, kind="ExternalInput").ap()
    rbd = nc.dram_tensor("rb_in", [16, 1], F32, kind="ExternalInput").ap()
    w1d = nc.dram_tensor("w1", [16, D, 2048], F32, kind="ExternalInput").ap()
    b1d = nc.dram_tensor("b1t", [2048, 16], F32, kind="ExternalInput").ap()
    w2d = nc.dram_tensor("w2", [16, 2048, D], F32, kind="ExternalInput").ap()
    b2d = nc.dram_tensor("b2", [16, D], F32, kind="ExternalInput").ap()
    qwd = nc.dram_tensor("qw", [D, D], F32, kind="ExternalInput").ap()
    kwd = nc.dram_tensor("kw", [D, D], F32, kind="ExternalInput").ap()
    vwd = nc.dram_tensor("vw", [D, D], F32, kind="ExternalInput").ap()
    owd = nc.dram_tensor("ow", [D, D], F32, kind="ExternalInput").ap()
    seld = nc.dram_tensor("selt", [D, 32], F32, kind="ExternalInput").ap()
    keyd = nc.dram_tensor("keys", [32, D, 512], F32, kind="ExternalInput").ap()
    vald = nc.dram_tensor("values", [32, 512, D], F32, kind="ExternalInput").ap()
    tri1d = nc.dram_tensor("tri1", [128, 128], F32, kind="ExternalInput").ap()
    tri2d = nc.dram_tensor("tri2", [128, 128], F32, kind="ExternalInput").ap()
    identd = nc.dram_tensor("ident", [128, 128], BF16, kind="ExternalInput").ap()
    idfd = nc.dram_tensor("idf", [128, 128], F32, kind="ExternalInput").ap()
    eye16d = nc.dram_tensor("eyeb16", [16, 16 * 128], BF16, kind="ExternalInput").ap()
    eye32d = nc.dram_tensor("eyeb32", [32, 32 * 128], BF16, kind="ExternalInput").ap()
    al3d = nc.dram_tensor("al3", [16, 3, S_LEN], BF16, kind="ExternalInput").ap()
    ones3d = nc.dram_tensor("ones3", [3, S_LEN], BF16, kind="ExternalInput").ap()
    m1d = nc.dram_tensor("m1", [128, 6144], BF16, kind="ExternalInput").ap()
    m2d = nc.dram_tensor("m2", [128, 6144], BF16, kind="ExternalInput").ap()
    xout = nc.dram_tensor("xout", [D, TOK], F32, kind="ExternalOutput").ap()
    qkv_s = nc.dram_tensor("qkv_s", [3072, TOK], BF16, kind="Internal").ap()
    q_s = nc.dram_tensor("q_s", [D, TOK], BF16, kind="Internal").ap()
    kv_loc = nc.dram_tensor("kv_loc", [2 * D, TOK], BF16, kind="Internal").ap()
    kv_gath = nc.dram_tensor("kv_gath", [4 * D, TOK], BF16, kind="Internal").ap()
    kg_s = nc.dram_tensor("kg_s", [D, S_LEN], BF16, kind="Internal").ap()
    vg_s = nc.dram_tensor("vg_s", [D, S_LEN], BF16, kind="Internal").ap()

    b = Builder(nc)
    x = b.sbuf("x", [128, 8 * TOK], F32)
    xln = b.sbuf("xln", [128, 8 * TOK], BF16)
    AR1 = b.sbuf("AR1", [128, 15680], F32)
    AR2 = b.sbuf("AR2", [128, 12288], F32)
    idb = b.sbuf("idb_sb", [128, 128], BF16)
    idf = b.sbuf("idf_sb", [128, 128], F32)
    mz = b.sbuf("mz", [128, 160], F32)
    bpack = b.sbuf("bpack", [128, 96], BF16)
    sml = b.sbuf("sml", [128, 8], F32)
    qkvbias = b.sbuf("qkvbias", [128, 24], F32)
    rbsb = b.sbuf("rbsb", [16, 1], F32)
    c = alloc_consts(b)
    banks = [b.psum(f"bk{i}") for i in range(8)]

    a1f = AR1.ap()
    a1b = AR1.ap().bitcast(BF16)
    a2f = AR2.ap()
    a2b = AR2.ap().bitcast(BF16)
    tri1 = a2f[:, 7680:7808]
    tri2 = a2f[:, 7808:7936]
    Pb2 = [a2b[:, 14336:14848], a2b[:, 14848:15360]]
    Pb6 = [a2b[:, 18432:18944], a2b[:, 18944:19456]]
    rowbuf = a1f[0:1, 0:4 * TOK]
    ln_sq = a1f[:, 8192:10240]
    ln_rb = a1f[:, 10240:12288]
    ln_mb = a1f[:, 12288:14336]

    def do_ln():
        vln = emit_layernorm(b, x, xln, banks, ln_sq, rowbuf, c["ones_col"],
                             c["ones_row"], ln_rb, ln_mb, c["epsb"])
        return vln

    ldx = b.dma("sync", "ds", x[:].rearrange("p (c n) -> p c n", c=8),
                xin.rearrange("(c p) n -> p c n", p=128))
    b.dma("sync", "ds", idb[:], identd)
    b.dma("sync", "ds", idf[:], idfd)
    b.dma("sync", "ds", rbsb[:], rbd)
    ld0 = b.dma("gpsimd", "dg", qkvbias[:], bqkv.rearrange("o (c p) -> p (o c)", p=128),
                allow_slow_non_contiguous=True)
    w_qkv = a2b[:, 0:24576]
    ldw = b.dma("gpsimd", "dg", w_qkv.rearrange("p (c n) -> p c n", c=8),
                wqkv.rearrange("(c p) n -> p c n", p=128))
    for e in ("scalar", "tensor", "vector"):
        b.wait(e, "ds", ldx)
        b.wait(e, "v", 1)

    vln = do_ln()
    b.wait("tensor", "v", vln)
    b.wait("tensor", "dg", ldw)
    b.wait("scalar", "dg", ld0)
    stg = [a1b[:, 0:2048], a1b[:, 2048:4096]]
    state = {}
    stg_war = [0, 0]
    for oc in range(24):
        sg = stg[oc % 2]
        b.wait("scalar", "st", stg_war[oc % 2])
        for sl in range(4):
            n0 = sl * 512
            bid = sl % 2
            bk = banks[bid]
            if bid in state:
                b.wait("tensor", "s", state[bid])
            for kc in range(8):
                def fmm(E, oc=oc, kc=kc, n0=n0, bk=bk):
                    return E.matmul(bk[:, 0:512],
                                    w_qkv[:, kc * 3072 + oc * 128: kc * 3072 + (oc + 1) * 128],
                                    xln[:, kc * TOK + n0: kc * TOK + n0 + 512],
                                    start=(kc == 0), stop=(kc == 7))
                tmm = b.op("tensor", fmm, inc="t" if kc == 7 else None)
            b.wait("scalar", "t", tmm)

            def fev(E, oc=oc, n0=n0, bk=bk, sg=sg):
                return E.activation(sg[:, n0:n0 + 512], bk[:, 0:512], AF.Identity,
                                    bias=qkvbias[:, oc:oc + 1])
            state[bid] = b.op("scalar", fev, inc="s")
        b.wait("sync", "s", b.counts["s"])
        stg_war[oc % 2] = b.dma("sync", "st", qkv_s[oc * 128:(oc + 1) * 128, :], sg)
    b.barrier()
    qkv_done = b.counts["st"]
    b.wait("tensor", "st", qkv_done)
    b.wait("sync", "st", qkv_done)

    ldtr = b.dma("sync", "ds", tri1, tri1d)
    ldtr = b.dma("sync", "ds", tri2, tri2d)
    for e_ in ("tensor", "vector", "scalar"):
        b.wait(e_, "ds", ldtr)
    abufs = {
        "Qt": a2b[:, 0:4096], "Kt": a2b[:, 4096:8192],
        "Vt": a2b[0:64, 8192:12288], "Vtok": a2b[:, 12288:14336],
        "Pb": Pb2, "sink": Pb2[0], "mz": mz, "bpack": bpack,
        "tri1": tri1, "tri2": tri2, "idb": idb.ap(), "banks": banks,
    }
    b.op("vector", lambda E: E.memset(abufs["Kt"][64:67, :], 1.0), inc="v")
    b.wait("tensor", "v", b.counts["v"])
    qkv_r = qkv_s.rearrange("(c p) n -> p c n", p=128)
    qkv_g = [a1b[:, 4096 + i * 12288: 4096 + (i + 1) * 12288] for i in range(2)]
    for g in range(4):
        gb = qkv_g[g % 2]
        b.wait("sync", "v", b.counts["v"])
        ldg = b.dma("sync", "ds", gb.rearrange("p (c n) -> p c n", c=24),
                    qkv_r[:, :, g * 512:(g + 1) * 512])
        b.wait("vector", "ds", ldg)
        for i in range(2):
            li = 2 * g + i
            emit_l0_attn_instance(b, gb, i * 256, x, li * 256, abufs, None)
    b.barrier()

    w2sb = a1b[:, 0:16384]
    gtmp = a1b[:, 16384:16896]
    cbc = a1b[:, 16896:18944]
    cTb_f = a1b[0:16, 18944:20992]
    eyeb16 = a1b[0:16, 20992:23040]
    b1sb = a1f[:, 11776:12032]
    logits = a1f[0:16, 12032:14080]
    ltok = a1f[:, 14080:14336]
    l_et = a1f[:, 14336:14352]
    l_tsc = a1f[:, 14352:14368]
    l_tmsk = a1f[:, 14368:14384]
    ctok = a1f[:, 14384:14640]
    w1sb = a2b[:, 0:16384]
    hm = a2b[:, 16384:24576]
    rwsb = b.sbuf("rwsb", [128, 8 * 16], BF16)
    b2sb = a1b[0:16, 29280:30304]

    vln = do_ln()
    b.wait("gpsimd", "v", vln)
    ldc = b.dma("gpsimd", "dg", rwsb[:].rearrange("p (c n) -> p c n", c=8),
                rwd.rearrange("(c p) n -> p c n", p=128))
    b.dma("gpsimd", "dg", b1sb.rearrange("p (c e) -> p c e", c=16),
          b1d.rearrange("(c p) e -> p c e", p=128))
    b.dma("gpsimd", "dg", b2sb, b2d)
    ldg2 = b.dma("gpsimd", "dg", eyeb16, eye16d)
    b.wait("tensor", "v", vln)
    b.wait("tensor", "dg", ldc)

    for sl in range(4):
        n0 = sl * 512
        for kc in range(8):
            def fmm(E, sl=sl, kc=kc, n0=n0):
                return E.matmul(banks[0][0:16, 0:512],
                                rwsb[:, kc * 16:(kc + 1) * 16],
                                xln[:, kc * TOK + n0: kc * TOK + n0 + 512],
                                start=(kc == 0), stop=(kc == 7))
            tmm = b.op("tensor", fmm, inc="t" if kc == 7 else None)
        b.wait("scalar", "t", tmm)

        def fev(E, n0=n0):
            return E.activation(logits[0:16, n0:n0 + 512], banks[0][0:16, 0:512],
                                AF.Identity, bias=rbsb[:])
        sev = b.op("scalar", fev, inc="s")
        b.wait("tensor", "s", sev)
    for tt in range(16):
        def ftr(E, tt=tt):
            return E.transpose(banks[1][0:128, 0:16], logits[0:16, tt * 128:(tt + 1) * 128],
                               idf[0:16, 0:16])
        ttr = b.op("tensor", ftr, inc="t")
        b.wait("vector", "t", ttr)
        vv = b.op("vector", lambda E, tt=tt: E.tensor_copy(ltok[:, tt * 16:(tt + 1) * 16],
                                                           banks[1][0:128, 0:16]), inc="v")
        b.wait("tensor", "v", vv)
    for tt in range(16):
        lt = ltok[:, tt * 16:(tt + 1) * 16]

        def vstep(fn):
            t = b.op("vector", fn, inc="v")
            b.wait("vector", "v", t)
            return t

        vstep(lambda E, lt=lt: E.reduce_max(sml[:, 0:1], lt, axis=AX, negate=True))
        vm = b.counts["v"]
        b.wait("scalar", "v", vm)
        sx = b.op("scalar", lambda E, lt=lt: E.activation(l_et, lt, AF.Exp,
                                                          bias=sml[:, 0:1]), inc="s")
        b.wait("vector", "s", sx)
        vstep(lambda E: E.tensor_copy(l_tsc, l_et))
        for r_ in range(3):
            vstep(lambda E: E.reduce_max(sml[:, 1:2], l_tsc, axis=AX))
            vstep(lambda E: E.tensor_scalar(l_tmsk, l_tsc, sml[:, 1:2], NEG_BIG,
                                            ALU.is_ge, ALU.mult))
            vstep(lambda E: E.tensor_add(l_tsc, l_tsc, l_tmsk))
        vstep(lambda E: E.reduce_max(sml[:, 1:2], l_tsc, axis=AX))
        vstep(lambda E: E.scalar_tensor_tensor(l_tmsk, l_et, sml[:, 1:2], l_et,
                                               ALU.is_ge, ALU.mult, accum_out=sml[:, 2:3]))
        vstep(lambda E: E.reciprocal(sml[:, 3:4], sml[:, 2:3]))
        vstep(lambda E, tt=tt: E.tensor_scalar_mul(ctok[:, tt * 16:(tt + 1) * 16],
                                                   l_tmsk, sml[:, 3:4]))
    vct = b.counts["v"]
    b.wait("tensor", "v", vct)
    for tt in range(16):
        def ftr(E, tt=tt):
            return E.transpose(banks[1][0:16, 0:128], ctok[:, tt * 16:(tt + 1) * 16], idf[:])
        ttr = b.op("tensor", ftr, inc="t")
        b.wait("vector", "t", ttr)
        vv = b.op("vector", lambda E, tt=tt: E.tensor_copy(cTb_f[0:16, tt * 128:(tt + 1) * 128],
                                                           banks[1][0:16, 0:128]), inc="v")
        b.wait("tensor", "v", vv)
    b.wait("scalar", "dg", ldg2)
    for e in range(16):
        b.wait("gpsimd", "t", b.counts["t"])
        b.wait("gpsimd", "v", b.counts["v"])
        lw1 = b.dma("gpsimd", "dg",
                    w1sb.rearrange("p (c n) -> p c n", c=8),
                    w1d[e].rearrange("(c p) n -> p c n", p=128))
        lw2 = b.dma("gpsimd", "dg",
                    w2sb.rearrange("p (c n) -> p c n", c=16),
                    w2d[e].rearrange("(c p) n -> p c n", p=128))
        b.wait("tensor", "v", b.counts["v"])
        if e == 0:
            b.wait("tensor", "dg", lw2)
        for sl in range(4):
            n0 = sl * 512
            def fbc(E, e=e, n0=n0):
                return E.matmul(banks[4][:, 0:512], eyeb16[0:16, e * 128:(e + 1) * 128],
                                cTb_f[0:16, n0:n0 + 512], start=True, stop=True)
            tb = b.op("tensor", fbc, inc="t")
            b.wait("vector", "t", tb)
            vv = b.op("vector", lambda E, n0=n0: E.tensor_copy(cbc[:, n0:n0 + 512],
                                                               banks[4][:, 0:512]), inc="v")
            b.wait("tensor", "v", vv)
        b.wait("tensor", "dg", lw1)
        for sl in range(4):
            n0 = sl * 512
            b.wait("vector", "t", b.counts["t"])
            ev_war = {}
            for ic in range(16):
                bk = banks[ic % 2]
                if (ic % 2) in ev_war:
                    b.wait("tensor", "s", ev_war[ic % 2])
                for kc in range(8):
                    def fmm(E, e=e, sl=sl, ic=ic, kc=kc, n0=n0, bk=bk):
                        return E.matmul(bk[:, 0:512],
                                        w1sb[:, kc * 2048 + ic * 128:kc * 2048 + (ic + 1) * 128],
                                        xln[:, kc * TOK + n0:kc * TOK + n0 + 512],
                                        start=(kc == 0), stop=(kc == 7))
                    tmm = b.op("tensor", fmm, inc="t" if kc == 7 else None)
                b.wait("scalar", "t", tmm)

                def fge(E, e=e, ic=ic, bk=bk):
                    return E.activation(gtmp, bk[:, 0:512], AF.Gelu,
                                        bias=b1sb[:, ic * 16 + e:ic * 16 + e + 1])
                sge = b.op("scalar", fge, inc="s")
                ev_war[ic % 2] = sge
                b.wait("vector", "s", sge)

                def fmu(E, ic=ic, n0=n0):
                    return E.tensor_mul(hm[:, ic * 512:(ic + 1) * 512], gtmp,
                                        cbc[:, n0:n0 + 512])
                vmu = b.op("vector", fmu, inc="v")
                b.wait("scalar", "v", vmu)
            vhm = b.counts["v"]
            b.wait("tensor", "v", vhm)
            if e == 0 and sl == 0:
                b.wait("tensor", "dg", lw2)
            ev2_war = {}
            for oc in range(8):
                bk = banks[2 + oc % 2]
                if (oc % 2) in ev2_war:
                    b.wait("tensor", "v", ev2_war[oc % 2])
                for ic in range(16):
                    def fm2(E, e=e, sl=sl, oc=oc, ic=ic, n0=n0, bk=bk):
                        return E.matmul(bk[:, 0:512],
                                        w2sb[:, ic * 1024 + oc * 128:ic * 1024 + (oc + 1) * 128],
                                        hm[:, ic * 512:(ic + 1) * 512],
                                        start=(ic == 0), stop=(e > 0 and ic == 15))
                    tm2 = b.op("tensor", fm2,
                               inc="t" if (e > 0 and ic == 15) else None)
                if e == 0:
                    def fb2(E, oc=oc, n0=n0, bk=bk):
                        return E.matmul(bk[:, 0:512], b2sb[0:16, oc * 128:(oc + 1) * 128],
                                        cTb_f[0:16, n0:n0 + 512], start=False, stop=True)
                    tm2 = b.op("tensor", fb2, inc="t")
                b.wait("vector", "t", tm2)

                def fad(E, oc=oc, n0=n0, bk=bk):
                    xs = x[:, oc * TOK + n0:oc * TOK + n0 + 512]
                    return E.tensor_add(xs, xs, bk[:, 0:512])
                ev2_war[oc % 2] = b.op("vector", fad, inc="v")
    b.barrier()

    vln = do_ln()
    b.wait("tensor", "v", vln)
    w_l1 = [a2b[:, 0:8192], a2b[:, 8192:16384]]
    dsts = [q_s, kv_loc[0:D, :], kv_loc[D:2 * D, :]]
    wsrcs = [qwd, kwd, vwd]
    wdma = [0, 0]
    for wi in range(3):
        wb = w_l1[wi % 2]
        b.wait("gpsimd", "t", b.counts["t"])
        wdma[wi % 2] = b.dma("gpsimd", "dg", wb.rearrange("p (c n) -> p c n", c=8),
                             wsrcs[wi].rearrange("(c p) n -> p c n", p=128))
        b.wait("tensor", "dg", wdma[wi % 2])
        scale = 0.125 if wi == 0 else 1.0
        state = {}
        stg_war = [0, 0]
        for oc in range(8):
            sg = stg[oc % 2]
            b.wait("scalar", "st", stg_war[oc % 2])
            for sl in range(4):
                n0 = sl * 512
                bid = sl % 2
                bk = banks[bid]
                if bid in state:
                    b.wait("tensor", "s", state[bid])
                for kc in range(8):
                    def fmm(E, oc=oc, kc=kc, n0=n0, bk=bk, wb=wb):
                        return E.matmul(bk[:, 0:512],
                                        wb[:, kc * 1024 + oc * 128:kc * 1024 + (oc + 1) * 128],
                                        xln[:, kc * TOK + n0:kc * TOK + n0 + 512],
                                        start=(kc == 0), stop=(kc == 7))
                    tmm = b.op("tensor", fmm, inc="t" if kc == 7 else None)
                b.wait("scalar", "t", tmm)

                def fev(E, n0=n0, bk=bk, sg=sg, scale=scale):
                    return E.activation(sg[:, n0:n0 + 512], bk[:, 0:512], AF.Identity,
                                        scale=scale)
                state[bid] = b.op("scalar", fev, inc="s")
            b.wait("sync", "s", b.counts["s"])
            stg_war[oc % 2] = b.dma("sync", "st", dsts[wi][oc * 128:(oc + 1) * 128, :], sg)
    b.wait("gpsimd", "st", b.counts["st"])
    s_cc, cc_tot = b.inc("cc", 1)
    b.emit("gpsimd", lambda E: E.collective_compute(
        "AllGather", ALU.bypass, replica_groups=[[0, 1], [2, 3], [4, 5], [6, 7]],
        ins=[kv_loc.opt()], outs=[kv_gath.opt()]).then_inc(s_cc))
    b.wait("sync", "cc", cc_tot)
    for half, ch in enumerate((CH_A, CH_B)):
        for li, gc in enumerate(ch):
            roff = half * 2048
            b.dma("sync", "st", kg_s[:, 256 * gc:256 * gc + 256],
                  kv_gath[roff:roff + D, 256 * li:256 * li + 256])
            b.dma("sync", "st", vg_s[:, 256 * gc:256 * gc + 256],
                  kv_gath[roff + D:roff + 2 * D, 256 * li:256 * li + 256])
    relay = b.counts["st"]
    b.barrier()
    b.wait("sync", "st", relay)

    O = a1b[:, 14976:31360]
    lbufs = {
        "Pb": Pb6, "sink": Pb6[0], "mz": mz, "bpack": bpack,
        "m1": a2b[:, 4096:10240],
        "m2": a2b[:, 10240:16384],
        "idb": idb.ap(), "bts": a2b[0:3, 16384:18432], "O": O, "banks": banks,
    }
    Qe = a2b[:, 0:2048]
    Ke = a1b[:, 0:4096]
    Vh = a1b[0:64, 4096:8192]
    Vtok = a2b[:, 2048:4096]
    ldm1 = b.dma("sync", "ds", lbufs["m1"], m1d)
    ldm2 = b.dma("sync", "ds", lbufs["m2"], m2d)
    b.dma("sync", "ds", Qe[64:67, :], ones3d[:, 0:TOK])
    ldst = b.dma("sync", "ds", Ke[67:70, :], ones3d)
    for e in ("tensor", "vector", "scalar"):
        b.wait(e, "ds", ldst)
    for h in range(16):
        b.wait("sync", "t", b.counts["t"])
        b.wait("sync", "s", b.counts["s"])
        b.dma("sync", "ds", Qe[0:64, :], q_s[64 * h:64 * h + 64, :])
        b.dma("sync", "ds", Ke[0:64, :], kg_s[64 * h:64 * h + 64, :])
        b.dma("sync", "ds", Ke[64:67, :], al3d[h])
        hd = b.dma("sync", "ds", Vh[0:64, :], vg_s[64 * h:64 * h + 64, :])
        for e in ("tensor", "vector", "scalar"):
            b.wait(e, "ds", hd)
        emit_l1_attn_head(b, h, Qe, Ke, Vh, Vtok, lbufs, h == 0)
    vO = b.counts["v"]
    b.wait("gpsimd", "v", vO)
    b.wait("gpsimd", "t", b.counts["t"])
    ow_sb = a2b[:, 4096:12288]
    ldow = b.dma("gpsimd", "dg", ow_sb.rearrange("p (c n) -> p c n", c=8),
                 owd.rearrange("(c p) n -> p c n", p=128))
    b.wait("tensor", "v", vO)
    b.wait("tensor", "dg", ldow)
    ev_war = {}
    for oc in range(8):
        for sl in range(4):
            n0 = sl * 512
            bid = 4 + (oc * 4 + sl) % 2
            bk = banks[bid]
            if bid in ev_war:
                b.wait("tensor", "v", ev_war[bid])
            for kc in range(8):
                def fmm(E, oc=oc, kc=kc, n0=n0, bk=bk):
                    return E.matmul(bk[:, 0:512],
                                    ow_sb[:, kc * 1024 + oc * 128:kc * 1024 + (oc + 1) * 128],
                                    O[:, kc * TOK + n0:kc * TOK + n0 + 512],
                                    start=(kc == 0), stop=(kc == 7))
                tmm = b.op("tensor", fmm, inc="t" if kc == 7 else None)
            b.wait("vector", "t", tmm)

            def fad(E, oc=oc, n0=n0, bk=bk):
                xs = x[:, oc * TOK + n0:oc * TOK + n0 + 512]
                return E.tensor_add(xs, xs, bk[:, 0:512])
            ev_war[bid] = b.op("vector", fad, inc="v")
    b.barrier()

    gates = a1f[0:32, 0:TOK]
    gtok = a1f[:, 2048:2080]
    f_et = a1f[:, 2080:2112]
    f_tsc = a1f[:, 2112:2144]
    f_tmsk = a1f[:, 2144:2176]
    gftok = a1f[:, 2176:2688]
    gbc = a1b[:, 5632:7680]
    scsc = a1b[:, 7680:9728]
    sctmp = a1b[:, 9728:10240]
    k_sb = a2b[:, 0:4096]
    v_sb = a2b[:, 4096:8192]
    selw = a2b[:, 8192:8448]
    eyeb32 = a2b[0:32, 8448:12544]
    gT = a2b[0:32, 12544:14592]

    vln = do_ln()
    b.wait("gpsimd", "v", vln)
    ldsel = b.dma("gpsimd", "dg", selw.rearrange("p (c n) -> p c n", c=8),
                  seld.rearrange("(c p) n -> p c n", p=128))
    lde32 = b.dma("gpsimd", "dg", eyeb32, eye32d)
    b.wait("tensor", "v", vln)
    b.wait("tensor", "dg", ldsel)
    for sl in range(4):
        n0 = sl * 512
        for kc in range(8):
            def fmm(E, sl=sl, kc=kc, n0=n0):
                return E.matmul(banks[0][0:32, 0:512],
                                selw[:, kc * 32:(kc + 1) * 32],
                                xln[:, kc * TOK + n0: kc * TOK + n0 + 512],
                                start=(kc == 0), stop=(kc == 7))
            tmm = b.op("tensor", fmm, inc="t" if kc == 7 else None)
        b.wait("scalar", "t", tmm)

        def fev(E, n0=n0):
            return E.activation(gates[:, n0:n0 + 512], banks[0][0:32, 0:512], AF.Sigmoid)
        sev = b.op("scalar", fev, inc="s")
        b.wait("tensor", "s", sev)
    for tt in range(16):
        def ftr(E, tt=tt):
            return E.transpose(banks[1][0:128, 0:32], gates[:, tt * 128:(tt + 1) * 128],
                               idf[0:32, 0:32])
        ttr = b.op("tensor", ftr, inc="t")
        b.wait("vector", "t", ttr)
        vv = b.op("vector", lambda E, tt=tt: E.tensor_copy(gtok, banks[1][0:128, 0:32]),
                  inc="v")
        b.wait("tensor", "v", vv)

        def vstep(fn):
            t = b.op("vector", fn, inc="v")
            b.wait("vector", "v", t)
            return t

        vstep(lambda E: E.tensor_copy(f_tsc, gtok))
        for r_ in range(3):
            vstep(lambda E: E.reduce_max(sml[:, 1:2], f_tsc, axis=AX))
            vstep(lambda E: E.tensor_scalar(f_tmsk, f_tsc, sml[:, 1:2], NEG_BIG,
                                            ALU.is_ge, ALU.mult))
            vstep(lambda E: E.tensor_add(f_tsc, f_tsc, f_tmsk))
        vstep(lambda E: E.reduce_max(sml[:, 1:2], f_tsc, axis=AX))
        vstep(lambda E: E.scalar_tensor_tensor(gftok[:, 0:32], gtok, sml[:, 1:2], gtok,
                                               ALU.is_ge, ALU.mult))
        vgf = b.counts["v"]
        b.wait("tensor", "v", vgf)
        def ftr2(E, tt=tt):
            return E.transpose(banks[1][0:32, 0:128], gftok[:, 0:32], idf[:])
        ttr2 = b.op("tensor", ftr2, inc="t")
        b.wait("vector", "t", ttr2)
        vv2 = b.op("vector", lambda E, tt=tt: E.tensor_copy(gT[0:32, tt * 128:(tt + 1) * 128],
                                                            banks[1][0:32, 0:128]), inc="v")
        b.wait("tensor", "v", vv2)
    b.wait("tensor", "dg", lde32)
    for e in range(32):
        b.wait("gpsimd", "t", b.counts["t"])
        b.wait("gpsimd", "v", b.counts["v"])
        lk = b.dma("gpsimd", "dg", k_sb.rearrange("p (c n) -> p c n", c=8),
                   keyd[e].rearrange("(c p) n -> p c n", p=128))
        lv = b.dma("gpsimd", "dg", v_sb.rearrange("p (c n) -> p c n", c=4),
                   vald[e].rearrange("(c p) n -> p c n", p=128))
        b.wait("tensor", "v", b.counts["v"])
        for sl in range(4):
            n0 = sl * 512
            def fbc(E, e=e, n0=n0):
                return E.matmul(banks[4][:, 0:512], eyeb32[0:32, e * 128:(e + 1) * 128],
                                gT[0:32, n0:n0 + 512], start=True, stop=True)
            tb = b.op("tensor", fbc, inc="t")
            b.wait("vector", "t", tb)
            vv = b.op("vector", lambda E, n0=n0: E.tensor_copy(gbc[:, n0:n0 + 512],
                                                               banks[4][:, 0:512]), inc="v")
            b.wait("tensor", "v", vv)
        b.wait("tensor", "dg", lk)
        for sl in range(4):
            n0 = sl * 512
            b.wait("vector", "t", b.counts["t"])
            ev_war = {}
            for ic in range(4):
                bk = banks[ic % 2]
                if (ic % 2) in ev_war:
                    b.wait("tensor", "s", ev_war[ic % 2])
                for kc in range(8):
                    def fmm(E, e=e, sl=sl, ic=ic, kc=kc, n0=n0, bk=bk):
                        return E.matmul(bk[:, 0:512],
                                        k_sb[:, kc * 512 + ic * 128:kc * 512 + (ic + 1) * 128],
                                        xln[:, kc * TOK + n0:kc * TOK + n0 + 512],
                                        start=(kc == 0), stop=(kc == 7))
                    tmm = b.op("tensor", fmm, inc="t" if kc == 7 else None)
                b.wait("scalar", "t", tmm)

                def fre(E, ic=ic, bk=bk):
                    return E.activation(sctmp, bk[:, 0:512], AF.Relu)
                sre = b.op("scalar", fre, inc="s")
                ev_war[ic % 2] = sre
                b.wait("vector", "s", sre)

                def fmu(E, ic=ic, n0=n0):
                    return E.tensor_mul(scsc[:, ic * 512:(ic + 1) * 512], sctmp,
                                        gbc[:, n0:n0 + 512])
                vmu = b.op("vector", fmu, inc="v")
                b.wait("scalar", "v", vmu)
            vhm = b.counts["v"]
            b.wait("tensor", "v", vhm)
            if e == 0 and sl == 0:
                b.wait("tensor", "dg", lv)
            ev2_war = {}
            for oc in range(8):
                bk = banks[2 + oc % 2]
                if (oc % 2) in ev2_war:
                    b.wait("tensor", "v", ev2_war[oc % 2])
                for ic in range(4):
                    def fm2(E, oc=oc, ic=ic, bk=bk):
                        return E.matmul(bk[:, 0:512],
                                        v_sb[:, ic * 1024 + oc * 128:ic * 1024 + (oc + 1) * 128],
                                        scsc[:, ic * 512:(ic + 1) * 512],
                                        start=(ic == 0), stop=(ic == 3))
                    tm2 = b.op("tensor", fm2, inc="t" if ic == 3 else None)
                b.wait("vector", "t", tm2)

                def fad(E, oc=oc, n0=n0, bk=bk):
                    xs = x[:, oc * TOK + n0:oc * TOK + n0 + 512]
                    return E.tensor_add(xs, xs, bk[:, 0:512])
                ev2_war[oc % 2] = b.op("vector", fad, inc="v")
    vfin = b.counts["v"]
    b.wait("sync", "v", vfin)
    st = b.dma("sync", "st", xout.rearrange("(c p) n -> p c n", p=128),
               x[:].rearrange("p (c n) -> p c n", c=8))
    b.wait("sync", "st", st)
    return b.finalize()


def kernel_fused(**inputs):
    cst = _host_consts()
    cores = list(range(8))
    shards = shard_tokens(np.asarray(inputs["L0_attn_inputs"], dtype=np.float32))
    b1t = np.ascontiguousarray(np.asarray(inputs["L0_ffn_b1"]).T)
    selt = np.ascontiguousarray(np.asarray(inputs["L1_ffn_expert_sel"]).T)
    nc = build_fused()
    in_maps = []
    for ci in range(8):
        is_a = (ci % 2 == 0)
        in_maps.append({
            "xin": shards[ci],
            "wqkv": np.asarray(inputs["L0_attn_qkv_kernel"]),
            "bqkv": np.asarray(inputs["L0_attn_qkv_bias"]),
            "rw": np.asarray(inputs["L0_ffn_router_w"]),
            "rb_in": np.asarray(inputs["L0_ffn_router_b"]).reshape(16, 1),
            "w1": np.asarray(inputs["L0_ffn_w1"]), "b1t": b1t,
            "w2": np.asarray(inputs["L0_ffn_w2"]), "b2": np.asarray(inputs["L0_ffn_b2"]),
            "qw": np.asarray(inputs["L1_attn_q_w"]), "kw": np.asarray(inputs["L1_attn_k_w"]),
            "vw": np.asarray(inputs["L1_attn_v_w"]), "ow": np.asarray(inputs["L1_attn_o_w"]),
            "selt": selt, "keys": np.asarray(inputs["L1_ffn_keys"]),
            "values": np.asarray(inputs["L1_ffn_values"]),
            "tri1": cst["tri1"], "tri2": cst["tri2"], "ident": cst["ident"],
            "idf": cst["idf"], "eyeb16": cst["eyeb16"], "eyeb32": cst["eyeb32"],
            "al3": cst["al3"], "ones3": cst["ones3"],
            "m1": cst["m1"][is_a], "m2": cst["m2"][is_a],
        })
    res = run_bass_kernel_spmd(nc, in_maps, core_ids=cores)
    return unshard_tokens([r["xout"] for r in res.results], np.float32)


def kernel(**inputs):
    return kernel_fused(**inputs)


# version 03
# speedup vs baseline: 1.0678x; 1.0678x over previous
import collections
import contextlib

import numpy as np
import ml_dtypes

import concourse.bass as bass
import concourse.mybir as mybir
from concourse.bass_utils import run_bass_kernel_spmd

F32 = mybir.dt.float32
BF16 = mybir.dt.bfloat16
AX = mybir.AxisListType.X
AF = mybir.ActivationFunctionType
ALU = mybir.AluOpType
BF16NP = ml_dtypes.bfloat16

B_SZ = 4
S_LEN = 4096
D = 1024
CHUNK = 256
TOK = 2048
CH_A = [0, 3, 4, 7, 8, 11, 12, 15]
CH_B = [1, 2, 5, 6, 9, 10, 13, 14]
NEG_BIG = -1.0e30


def _gq_list(is_a):
    ch = CH_A if is_a else CH_B
    return [2 * ch[l // 2] + (l % 2) for l in range(16)]


L1_KMAX = [max(a, b) + 1 for a, b in zip(_gq_list(True), _gq_list(False))]


def shard_tokens(x):
    shards = []
    for b in range(B_SZ):
        for ch in (CH_A, CH_B):
            rows = np.concatenate([x[b, 256 * c:256 * c + 256] for c in ch], axis=0)
            shards.append(np.ascontiguousarray(rows.T))
    return shards


def unshard_tokens(shards, dtype=np.float32):
    out = np.empty((B_SZ, S_LEN, D), dtype)
    for b in range(B_SZ):
        for k, ch in enumerate((CH_A, CH_B)):
            rows = np.asarray(shards[2 * b + k]).T
            for li, c in enumerate(ch):
                out[b, 256 * c:256 * c + 256] = rows[256 * li:256 * li + 256]
    return out


def host_l1_masks(is_a, transposed):
    gq = _gq_list(is_a)
    m = np.zeros((16, 3, 128, 128), np.float32)
    q = np.arange(128)
    for lq in range(16):
        for j in range(3):
            kt = L1_KMAX[lq] - 3 + j
            delta = gq[lq] - kt
            if delta >= 1:
                continue
            if delta < 0:
                m[lq, j, :, :] = NEG_BIG
            elif transposed:
                m[lq, j][q[:, None] > q[None, :]] = NEG_BIG
            else:
                m[lq, j][q[None, :] > q[:, None]] = NEG_BIG
    return m


def host_tri_masks():
    q = np.arange(128)
    tri_p1 = np.where(q[None, :] > q[:, None], NEG_BIG, 0.0).astype(np.float32)
    tri_p2 = np.where(q[:, None] > q[None, :], NEG_BIG, 0.0).astype(np.float32)
    return tri_p1, tri_p2


ENGINES = ("tensor", "vector", "scalar", "gpsimd", "sync")


class Builder:
    def __init__(self, nc):
        self.nc = nc
        self.streams = {e: [] for e in ENGINES}
        self.sems = {}
        self.counts = collections.defaultdict(int)
        self.es = contextlib.ExitStack()
        self._uid = 0

    def uid(self):
        self._uid += 1
        return self._uid

    def sem(self, name):
        if name not in self.sems:
            self.sems[name] = self.es.enter_context(self.nc.semaphore(name))
            self.counts[name] = 0
        return self.sems[name]

    def emit(self, eng, fn):
        self.streams[eng].append(fn)

    def wait(self, eng, sem_name, val):
        if val is None or val <= 0:
            return
        s = self.sem(sem_name)
        self.emit(eng, lambda E: E.wait_ge(s, val))

    def inc(self, sem_name, n=1):
        s = self.sem(sem_name)
        self.counts[sem_name] += n
        return s, self.counts[sem_name]

    def op(self, eng, fn, inc=None):
        if inc is None:
            self.emit(eng, fn)
            return None
        s, tot = self.inc(inc, 1)
        if eng in ("vector", "scalar", "gpsimd"):
            def wrapped(E):
                fn(E)
                E.drain(fusable=True).then_inc(s, 1)
            self.emit(eng, wrapped)
        else:
            self.emit(eng, lambda E: fn(E).then_inc(s, 1))
        return tot

    def dma(self, eng, sem_name, out, in_, **kw):
        s, tot = self.inc(sem_name, 16)
        self.emit(eng, lambda E: E.dma_start(out=out, in_=in_, **kw).then_inc(s, 16))
        return tot

    def sbuf(self, name, shape, dtype):
        return self.es.enter_context(self.nc.sbuf_tensor(name, shape, dtype))

    def psum(self, name, shape=(128, 512), dtype=F32):
        return self.es.enter_context(self.nc.psum_tensor(name, list(shape), dtype))

    def barrier(self, name=None):
        nm = name or f"bar{self.uid()}"
        s = self.sem(nm)
        for e in ("tensor", "vector", "scalar", "gpsimd"):
            _, tot = self.inc(nm, 1)
            self.emit(e, lambda E: E.nop().then_inc(s, 1))
        tot = self.counts[nm]
        for e in ("tensor", "vector", "scalar", "gpsimd", "sync"):
            self.emit(e, lambda E: E.wait_ge(s, tot))

    def finalize(self):
        block = self.es.enter_context(self.nc.Block())
        streams = self.streams

        @block.gpsimd
        def _(E):
            for fn in streams["gpsimd"]:
                fn(E)

        @block.tensor
        def _(E):
            for fn in streams["tensor"]:
                fn(E)

        @block.vector
        def _(E):
            for fn in streams["vector"]:
                fn(E)

        @block.scalar
        def _(E):
            for fn in streams["scalar"]:
                fn(E)

        @block.sync
        def _(E):
            for fn in streams["sync"]:
                fn(E)

        self.es.close()
        return self.nc


def emit_layernorm(b, x, xln, banks, sq, rowbuf, ones_col, ones_row, rb, mb_,
                   epsb):
    R_MU, R_R, R_T, R_T2 = 0, 1, 2, 3

    def row(i, n0=0, n1=TOK):
        return rowbuf[0:1, i * TOK + n0: i * TOK + n1]

    chunk_mark = None
    for c in range(8):
        if chunk_mark is not None:
            b.wait("scalar", "t", chunk_mark)

        def fsq(E, c=c):
            return E.activation(sq[:, 0:TOK], x[:, c * TOK:(c + 1) * TOK], AF.Square)
        tot_s = b.op("scalar", fsq, inc="s")
        b.wait("tensor", "s", tot_s)
        for sl in range(4):
            n0 = sl * 512

            def fmm1(E, c=c, sl=sl, n0=n0):
                return E.matmul(banks[sl][0:1, :], ones_col[:],
                                x[:, c * TOK + n0:c * TOK + n0 + 512],
                                start=(c == 0), stop=False)

            b.op("tensor", fmm1)

            def fmm2(E, c=c, sl=sl, n0=n0):
                return E.matmul(banks[sl][32:33, :], ones_col[:], sq[:, n0:n0 + 512],
                                start=(c == 0), stop=(c == 7))

            chunk_mark = b.op("tensor", fmm2, inc="t" if sl == 3 else None)
    t_stats = b.counts["t"]
    b.wait("vector", "t", t_stats)
    for sl in range(4):
        n0, n1 = sl * 512, sl * 512 + 512

        def f1(E, sl=sl, n0=n0, n1=n1):
            return E.tensor_scalar_mul(row(R_MU, n0, n1), banks[sl][0:1, :], 1.0 / D)

        b.op("vector", f1)

        def f2(E, sl=sl, n0=n0, n1=n1):
            return E.tensor_scalar_mul(row(R_T, n0, n1), banks[sl][32:33, :], 1.0 / D)

        b.op("vector", f2)
    b.op("vector", lambda E: E.tensor_mul(row(R_T2), row(R_MU), row(R_MU)))
    tot_v = b.op("vector", lambda E: E.tensor_sub(row(R_T), row(R_T), row(R_T2)),
                 inc="v")
    b.wait("scalar", "v", tot_v)
    tot_s = b.op("scalar", lambda E: E.activation(row(R_T2), row(R_T), AF.Ln,
                                                  bias=epsb[0:1, 0:1]), inc="s")
    b.wait("scalar", "s", tot_s)
    tot_s = b.op("scalar", lambda E: E.activation(row(R_R), row(R_T2), AF.Exp, scale=-0.5),
                 inc="s")
    b.wait("vector", "s", tot_s)
    tot_v = b.op("vector", lambda E: E.tensor_mul(row(R_T), row(R_MU), row(R_R)),
                 inc="v")
    b.wait("tensor", "v", tot_v)
    for sl in range(4):
        n0 = sl * 512
        bk_r, bk_m = banks[4], banks[5]

        def f1(E, n0=n0, bk=bk_r):
            return E.matmul(bk[:, 0:512], ones_row[:], row(R_R, n0, n0 + 512),
                            start=True, stop=True)

        t1 = b.op("tensor", f1, inc="t")
        b.wait("vector", "t", t1)

        def c1(E, n0=n0, bk=bk_r):
            return E.tensor_copy(rb[:, n0:n0 + 512], bk[:, 0:512])

        b.op("vector", c1)

        def f2(E, n0=n0, bk=bk_m):
            return E.matmul(bk[:, 0:512], ones_row[:], row(R_T, n0, n0 + 512),
                            start=True, stop=True)

        t2 = b.op("tensor", f2, inc="t")
        b.wait("vector", "t", t2)

        def c2(E, n0=n0, bk=bk_m):
            return E.tensor_copy(mb_[:, n0:n0 + 512], bk[:, 0:512])

        v2 = b.op("vector", c2, inc="v")
        b.wait("tensor", "v", v2)
    for c in range(8):
        def fa(E, c=c):
            return E.tensor_mul(sq[:, 0:TOK], x[:, c * TOK:(c + 1) * TOK], rb[:, 0:TOK])

        b.op("vector", fa)

        def fb(E, c=c):
            return E.tensor_sub(xln[:, c * TOK:(c + 1) * TOK], sq[:, 0:TOK], mb_[:, 0:TOK])

        b.op("vector", fb, inc="v" if c == 7 else None)
    return b.counts["v"]


if __name__ == "__main__":
    print("import ok; L1_KMAX =", L1_KMAX)



DBG_SC = None


def alloc_consts(b):
    c = {}
    c["ones_col"] = b.sbuf("c_ones_col", [128, 1], F32)
    c["ones_row"] = b.sbuf("c_ones_row", [1, 128], F32)
    c["epsb"] = b.sbuf("c_epsb", [1, 1], F32)
    b.op("vector", lambda E: E.memset(c["ones_col"][:], 1.0))
    b.op("vector", lambda E: E.memset(c["ones_row"][:], 1.0))
    b.op("vector", lambda E: E.memset(c["epsb"][:], 1e-5), inc="v")
    return c


def tok_ap(t, c, n0, n1):
    return t[:, c * TOK + n0: c * TOK + n1]


def emit_proj(b, w_sb, n_oc, n_kc, xln, n0, banks, bank_ids, evac, state):
    outw = n_oc * 128
    for oc in range(n_oc):
        bid = bank_ids[oc % len(bank_ids)]
        bk = banks[bid]
        if bid in state:
            sem_name, val = state[bid]
            b.wait("tensor", sem_name, val)
        for kc in range(n_kc):
            def fmm(E, oc=oc, kc=kc, n0=n0, bk=bk):
                return E.matmul(bk[:, 0:512],
                                w_sb[:, kc * outw + oc * 128: kc * outw + (oc + 1) * 128],
                                tok_ap(xln, kc, n0, n0 + 512),
                                start=(kc == 0), stop=(kc == n_kc - 1))
            tmm = b.op("tensor", fmm, inc="t" if kc == n_kc - 1 else None)
        state[bid] = evac(oc, n0, bk, tmm)


def emit_l0_attn_instance(b, qkv_g, icol0, x, xcol0, bufs, consts):
    Qt, Kt, Vt = bufs["Qt"], bufs["Kt"], bufs["Vt"]
    Vtok, Pb, sink = bufs["Vtok"], bufs["Pb"], bufs["sink"]
    mz, bpack, banks = bufs["mz"], bufs["bpack"], bufs["banks"]
    idb, tri1, tri2 = bufs["idb"], bufs["tri1"], bufs["tri2"]
    SCALE = 1.0 / (2.0 * D) ** 0.5
    GW = 512

    mneg = lambda q0, q1: mz[:, q0:q1]
    zall = lambda q0, q1: mz[:, 32 + q0:32 + q1]

    b.wait("tensor", "v", b.counts["v"])

    for j in range(16):
        fq, fk, fv = 192 * j, 192 * j + 64, 192 * j + 128
        for (src_f, dst, scale) in ((fq, Qt, SCALE), (fk, Kt, None), (fv, Vt, None)):
            cj, off = src_f // 128, src_f % 128
            src = qkv_g[off:off + 64, cj * GW + icol0: cj * GW + icol0 + 256]
            d = dst[0:64, j:4096:16]
            if scale is None:
                def fcp(E, src=src, d=d):
                    return E.tensor_copy(d, src)
            else:
                def fcp(E, src=src, d=d, scale=scale):
                    return E.tensor_scalar_mul(d, src, scale)
            b.op("vector", fcp)
    vbuild = b.op("vector", lambda E: E.engine_nop(), inc="v")
    b.wait("tensor", "v", vbuild)
    pT = bufs["pT"]
    big = bufs["big"]
    sink4k = bufs["sink4k"]
    vev = 0
    for g8 in range(4):
        if g8 > 0:
            b.wait("tensor", "v", vev)
        for j8 in range(8):
            kt = g8 * 8 + j8
            def ftr(E, kt=kt, j8=j8):
                return E.transpose(pT[0:128, j8 * 64:(j8 + 1) * 64],
                                   Vt[0:64, kt * 128:(kt + 1) * 128], idb[0:64, 0:64])
            ttr = b.op("tensor", ftr, inc="t" if j8 == 7 else None)
        b.wait("vector", "t", ttr)
        def fev(E, g8=g8):
            return E.tensor_copy(Vtok[:, g8 * 512:(g8 + 1) * 512], pT[0:128, 0:512])
        vev = b.op("vector", fev, inc="v")
    b.wait("tensor", "v", vev)
    sexp = 0
    for qt in range(32):
        nk = (qt + 1) * 128
        nsl = (nk + 511) // 512
        for s in range(nsl):
            k0, k1 = s * 512, min(nk, s * 512 + 512)
            def fmm(E, qt=qt, s=s, k0=k0, k1=k1):
                return E.matmul(banks[s][:, 0:k1 - k0], Qt[0:64, qt * 128:(qt + 1) * 128],
                                Kt[0:64, k0:k1], start=True, stop=True)
            tp1 = b.op("tensor", fmm, inc="t" if s == nsl - 1 else None)
        b.wait("vector", "t", tp1)
        def fmask(E, qt=qt):
            return E.tensor_add(big[:, qt * 128:(qt + 1) * 128],
                                big[:, qt * 128:(qt + 1) * 128], tri1[:])
        b.op("vector", fmask)
        def fred(E, qt=qt, nk=nk):
            return E.reduce_max(mneg(qt, qt + 1), big[:, 0:nk], axis=AX, negate=True)
        vred = b.op("vector", fred, inc="v")
        b.wait("scalar", "v", vred)
        def fexp(E, qt=qt, nk=nk):
            return E.activation(sink4k[:, 0:nk], big[:, 0:nk], AF.Exp,
                                bias=mneg(qt, qt + 1), accum_out=zall(qt, qt + 1))
        sexp = b.op("scalar", fexp, inc="s")
        b.wait("tensor", "s", sexp)
        b.wait("vector", "s", sexp)
    b.wait("vector", "v", vred)
    b.wait("scalar", "s", sexp)
    sln = b.op("scalar", lambda E: E.activation(mz[:, 96:128], mz[:, 32:64], AF.Ln), inc="s")
    b.wait("vector", "s", sln)

    def vstep(fn):
        t = b.op("vector", fn, inc="v")
        b.wait("vector", "v", t)
        return t

    vstep(lambda E: E.tensor_sub(mz[:, 64:96], mz[:, 0:32], mz[:, 96:128]))
    bp = bpack
    vstep(lambda E: E.tensor_copy(bp[:, 0:96:3], mz[:, 64:96]))
    vstep(lambda E: E.tensor_copy(mz[:, 96:128], bp[:, 0:96:3]))
    vstep(lambda E: E.tensor_sub(mz[:, 128:160], mz[:, 64:96], mz[:, 96:128]))
    vstep(lambda E: E.tensor_copy(bp[:, 1:96:3], mz[:, 128:160]))
    vstep(lambda E: E.tensor_copy(mz[:, 96:128], bp[:, 1:96:3]))
    vstep(lambda E: E.tensor_sub(mz[:, 128:160], mz[:, 128:160], mz[:, 96:128]))
    vsp = vstep(lambda E: E.tensor_copy(bp[:, 2:96:3], mz[:, 128:160]))
    b.wait("tensor", "v", vsp)
    for qt in range(32):
        def ftr(E, qt=qt):
            return E.transpose(pT[0:3, 0:128], bp[:, qt * 3:qt * 3 + 3], idb[:])
        ttr = b.op("tensor", ftr, inc="t")
        b.wait("vector", "t", ttr)
        def fev(E, qt=qt):
            return E.tensor_copy(Qt[64:67, qt * 128:(qt + 1) * 128], pT[0:3, 0:128])
        vev = b.op("vector", fev, inc="v")
        b.wait("tensor", "v", vev)
    p2b = [banks[0], banks[1]]
    avb = [banks[2], banks[3]]
    last_av_evac = 0
    prev_tav = prev_sexp = 0
    for g in range(8):
        av = avb[g % 2]
        if last_av_evac:
            b.wait("tensor", "v", last_av_evac)
        if g > 0:
            b.wait("scalar", "t", prev_tav)
            b.wait("tensor", "s", prev_sexp)
        ktmax = 4 * g + 3
        sexp_hist = {}
        tav_hist = {}
        for kt in range(ktmax + 1):
            co = max(0, (kt - 4 * g)) * 128
            p2 = p2b[kt % 2]
            if kt >= 2:
                b.wait("tensor", "s", sexp_hist[kt - 2])
            def fmm(E, g=g, kt=kt, co=co, p2=p2):
                return E.matmul(p2[:, co:512], Kt[0:67, kt * 128:(kt + 1) * 128],
                                Qt[0:67, g * 512 + co:(g + 1) * 512], start=True, stop=True)
            tp2 = b.op("tensor", fmm, inc="t")
            if kt >= 4 * g:
                b.wait("vector", "t", tp2)
                def fmask(E, co=co, p2=p2):
                    return E.tensor_add(p2[:, co:co + 128], p2[:, co:co + 128], tri2[:])
                vm = b.op("vector", fmask, inc="v")
                b.wait("scalar", "v", vm)
            else:
                b.wait("scalar", "t", tp2)
            if kt >= 2:
                b.wait("scalar", "t", tav_hist[kt - 2])
            pb = Pb[kt % 2]
            def fexp(E, kt=kt, co=co, p2=p2, pb=pb):
                return E.activation(pb[:, co:512], p2[:, co:512], AF.Exp)
            sexp = b.op("scalar", fexp, inc="s")
            sexp_hist[kt] = sexp
            b.wait("tensor", "s", sexp)
            def fav(E, g=g, kt=kt, co=co, pb=pb, av=av):
                return E.matmul(av[0:64, co:512], Vtok[:, kt * 64:(kt + 1) * 64],
                                pb[:, co:512], start=(kt == 0), stop=(kt == ktmax))
            tav_hist[kt] = b.op("tensor", fav, inc="t")
        tav = b.counts["t"]
        prev_tav, prev_sexp = tav, sexp_hist[ktmax]
        b.wait("vector", "t", tav)
        for j in range(16):
            cj, off = (64 * j) // 128, (64 * j) % 128
            xs = x[off:off + 64, cj * TOK + xcol0 + 32 * g: cj * TOK + xcol0 + 32 * g + 32]
            def fadd(E, j=j, xs=xs, av=av):
                return E.tensor_add(xs, xs, av[0:64, j:512:16])
            last_av_evac = b.op("vector", fadd, inc="v" if j == 15 else None)
    return last_av_evac


def build_l0_qkv():
    nc = bass.Bass("TRN2", target_bir_lowering=False)
    xin = nc.dram_tensor("xin", [D, TOK], F32, kind="ExternalInput").ap()
    wq = nc.dram_tensor("wq", [D, 3072], F32, kind="ExternalInput").ap()
    bq = nc.dram_tensor("bq", [1, 3072], F32, kind="ExternalInput").ap()
    qkvout = nc.dram_tensor("qkvout", [3072, TOK], BF16, kind="ExternalOutput").ap()

    b = Builder(nc)
    x = b.sbuf("x", [128, 8 * TOK], F32)
    xln = b.sbuf("xln", [128, 8 * TOK], BF16)
    w_sb = b.sbuf("w_sb", [128, 8 * 3072], BF16)
    bias_sb = b.sbuf("bias_sb", [128, 24], F32)
    sq = b.sbuf("sq", [128, TOK], F32)
    rowbuf = b.sbuf("rowbuf", [1, 4 * TOK], F32)
    rb = b.sbuf("rb", [128, TOK], F32)
    mb_ = b.sbuf("mb", [128, TOK], F32)
    sq_b = sq.ap().bitcast(BF16)
    stg = [sq_b[:, 0:TOK], sq_b[:, TOK:2 * TOK]]
    banks = [b.psum(f"bk{i}") for i in range(8)]
    c = alloc_consts(b)

    ldx = b.dma("sync", "ds", x[:].rearrange("p (c n) -> p c n", c=8),
                xin.rearrange("(c p) n -> p c n", p=128))
    ldw = b.dma("gpsimd", "dg", w_sb[:].rearrange("p (c n) -> p c n", c=8),
                wq.rearrange("(c p) n -> p c n", p=128))
    ldb = b.dma("gpsimd", "dg", bias_sb[:], bq.rearrange("o (c p) -> p (o c)", p=128),
                allow_slow_non_contiguous=True)
    for e in ("scalar", "tensor", "vector"):
        b.wait(e, "ds", ldx)
        b.wait(e, "v", 1)
    vln = emit_layernorm(b, x, xln, banks, sq, rowbuf, c["ones_col"], c["ones_row"],
                         rb, mb_, c["epsb"])
    b.wait("tensor", "v", vln)
    b.wait("tensor", "dg", ldw)
    b.wait("scalar", "dg", ldb)
    state = {}
    stg_war = [0, 0]
    for oc in range(24):
        sg = stg[oc % 2]
        b.wait("scalar", "st", stg_war[oc % 2])
        for sl in range(4):
            n0 = sl * 512
            bid = sl % 2
            bk = banks[bid]
            if bid in state:
                b.wait("tensor", "s", state[bid])
            for kc in range(8):
                def fmm(E, oc=oc, kc=kc, n0=n0, bk=bk):
                    return E.matmul(bk[:, 0:512],
                                    w_sb[:, kc * 3072 + oc * 128: kc * 3072 + (oc + 1) * 128],
                                    xln[:, kc * TOK + n0: kc * TOK + n0 + 512],
                                    start=(kc == 0), stop=(kc == 7))
                tmm = b.op("tensor", fmm, inc="t" if kc == 7 else None)
            b.wait("scalar", "t", tmm)

            def fev(E, oc=oc, n0=n0, bk=bk, sg=sg):
                return E.activation(sg[:, n0:n0 + 512], bk[:, 0:512], AF.Identity,
                                    bias=bias_sb[:, oc:oc + 1])
            state[bid] = b.op("scalar", fev, inc="s")
        b.wait("sync", "s", b.counts["s"])
        stg_war[oc % 2] = b.dma("sync", "st", qkvout[oc * 128:(oc + 1) * 128, :], sg)
    b.wait("sync", "st", b.counts["st"])
    return b.finalize()


def build_l0_attn():
    nc = bass.Bass("TRN2", target_bir_lowering=False)
    xin = nc.dram_tensor("xin", [D, TOK], F32, kind="ExternalInput").ap()
    qkv = nc.dram_tensor("qkv", [3072, TOK], BF16, kind="ExternalInput").ap()
    tri1d = nc.dram_tensor("tri1", [128, 128], F32, kind="ExternalInput").ap()
    tri2d = nc.dram_tensor("tri2", [128, 128], F32, kind="ExternalInput").ap()
    identd = nc.dram_tensor("ident", [128, 128], BF16, kind="ExternalInput").ap()
    xout = nc.dram_tensor("xout", [D, TOK], F32, kind="ExternalOutput").ap()

    b = Builder(nc)
    x = b.sbuf("x", [128, 8 * TOK], F32)
    qkv_g = [b.sbuf(f"qkv_g{i}", [128, 24 * 512], BF16) for i in range(2)]
    bufs = {
        "Qt": b.sbuf("Qt", [128, 4096], BF16),
        "Kt": b.sbuf("Kt", [128, 4096], BF16),
        "Vt": b.sbuf("Vt", [64, 4096], BF16),
        "Vtok": b.sbuf("Vtok", [128, 2048], BF16),
        "Pb": [b.sbuf(f"Pb{i}", [128, 512], BF16) for i in range(2)],
        "sink": b.sbuf("sink", [128, 512], BF16),
        "mz": b.sbuf("mz", [128, 160], F32),
        "bpack": b.sbuf("bpack", [128, 96], BF16),
        "tri1": b.sbuf("tri1_sb", [128, 128], F32),
        "tri2": b.sbuf("tri2_sb", [128, 128], F32),
        "idb": b.sbuf("idb_sb", [128, 128], BF16),
    }
    banks = [b.psum(f"bk{i}") for i in range(8)]
    bufs["banks"] = banks

    ldx = b.dma("sync", "ds", x[:].rearrange("p (c n) -> p c n", c=8),
                xin.rearrange("(c p) n -> p c n", p=128))
    b.dma("sync", "ds", bufs["tri1"][:], tri1d)
    b.dma("sync", "ds", bufs["tri2"][:], tri2d)
    ldc = b.dma("sync", "ds", bufs["idb"][:], identd)
    b.op("vector", lambda E: E.memset(bufs["Kt"][64:67, :], 1.0), inc="v")
    for e in ("scalar", "tensor", "vector"):
        b.wait(e, "ds", ldc)
    b.wait("tensor", "v", 1)

    qkv_r = qkv.rearrange("(c p) n -> p c n", p=128)
    prev_g_dma = [0, 0]
    for g in range(4):
        gb = qkv_g[g % 2]
        b.wait("sync", "v", b.counts["v"])
        ldg = b.dma("sync", "ds", gb[:].rearrange("p (c n) -> p c n", c=24),
                    qkv_r[:, :, g * 512:(g + 1) * 512])
        b.wait("vector", "ds", ldg)
        for i in range(2):
            li = 2 * g + i
            emit_l0_attn_instance(b, gb, i * 256, x, li * 256, bufs, None)
    vfin = b.counts["v"]
    b.wait("sync", "v", vfin)
    st = b.dma("sync", "st", xout.rearrange("(c p) n -> p c n", p=128),
               x[:].rearrange("p (c n) -> p c n", c=8))
    b.wait("sync", "st", st)
    return b.finalize()


def build_l0_ffn():
    nc = bass.Bass("TRN2", target_bir_lowering=False)
    xin = nc.dram_tensor("xin", [D, TOK], F32, kind="ExternalInput").ap()
    rwd = nc.dram_tensor("rw", [D, 16], F32, kind="ExternalInput").ap()
    rbd = nc.dram_tensor("rb_in", [16, 1], F32, kind="ExternalInput").ap()
    w1d = nc.dram_tensor("w1", [16, D, 2048], F32, kind="ExternalInput").ap()
    b1d = nc.dram_tensor("b1t", [2048, 16], F32, kind="ExternalInput").ap()
    w2d = nc.dram_tensor("w2", [16, 2048, D], F32, kind="ExternalInput").ap()
    b2d = nc.dram_tensor("b2", [16, D], F32, kind="ExternalInput").ap()
    idfd = nc.dram_tensor("idf", [128, 128], F32, kind="ExternalInput").ap()
    eyed = nc.dram_tensor("eyeb", [16, 16 * 128], BF16, kind="ExternalInput").ap()
    xout = nc.dram_tensor("xout", [D, TOK], F32, kind="ExternalOutput").ap()

    b = Builder(nc)
    x = b.sbuf("x", [128, 8 * TOK], F32)
    xln = b.sbuf("xln", [128, 8 * TOK], BF16)
    arA = b.sbuf("arA", [128, 2 * 4096], F32)
    rowbuf = arA.ap()[0:1, 0:4 * TOK]
    w2sb = arA.ap().bitcast(BF16)
    arB = b.sbuf("arB", [128, 6144], F32)
    sq = arB.ap()[:, 0:TOK]
    rb = arB.ap()[:, TOK:2 * TOK]
    mb_ = arB.ap()[:, 2 * TOK:3 * TOK]
    hm = arB.ap().bitcast(BF16)[:, 0:16 * 512]
    gtmp = arB.ap().bitcast(BF16)[:, 8192:8704]
    cbc = arB.ap().bitcast(BF16)[:, 8704:10752]
    w1sb = b.sbuf("w1sb", [128, 8 * 2048], BF16)
    rwsb = b.sbuf("rwsb", [128, 8 * 16], BF16)
    rbsb = b.sbuf("rbsb", [16, 1], F32)
    b1sb = b.sbuf("b1sb", [128, 256], F32)
    b2sb = b.sbuf("b2sb", [16, 1024], BF16)
    logits = arB.ap()[0:16, 0:TOK]
    ltok = arB.ap()[:, 2048:2304]
    et = arB.ap()[:, 2304:2320]
    tsc = arB.ap()[:, 2320:2336]
    tmsk = arB.ap()[:, 2336:2352]
    ctok = arB.ap()[:, 2352:2608]
    cTb = b.sbuf("cTb", [16, TOK], BF16)
    idf = b.sbuf("idf_sb", [128, 128], F32)
    eyeb = b.sbuf("eyeb_sb", [16, 16 * 128], BF16)
    sml = b.sbuf("sml", [128, 8], F32)
    banks = [b.psum(f"bk{i}") for i in range(8)]
    c = alloc_consts(b)

    ldx = b.dma("sync", "ds", x[:].rearrange("p (c n) -> p c n", c=8),
                xin.rearrange("(c p) n -> p c n", p=128))
    b.dma("sync", "ds", idf[:], idfd)
    b.dma("sync", "ds", eyeb[:], eyed)
    b.dma("sync", "ds", rbsb[:], rbd)
    ldc = b.dma("gpsimd", "dg", rwsb[:].rearrange("p (c n) -> p c n", c=8),
                rwd.rearrange("(c p) n -> p c n", p=128))
    b.dma("gpsimd", "dg", b1sb[:].rearrange("p (c e) -> p c e", c=16),
          b1d.rearrange("(c p) e -> p c e", p=128))
    ldg = b.dma("gpsimd", "dg", b2sb[:], b2d)
    for e in ("scalar", "tensor", "vector"):
        b.wait(e, "ds", ldx)
        b.wait(e, "v", 1)
    vln = emit_layernorm(b, x, xln, banks, sq, rowbuf, c["ones_col"], c["ones_row"],
                         rb, mb_, c["epsb"])
    b.wait("tensor", "v", vln)
    b.wait("tensor", "dg", ldc)
    b.wait("tensor", "ds", b.counts["ds"])

    for sl in range(4):
        n0 = sl * 512
        for kc in range(8):
            def fmm(E, sl=sl, kc=kc, n0=n0):
                return E.matmul(banks[0][0:16, 0:512],
                                rwsb[:, kc * 16:(kc + 1) * 16],
                                xln[:, kc * TOK + n0: kc * TOK + n0 + 512],
                                start=(kc == 0), stop=(kc == 7))
            tmm = b.op("tensor", fmm, inc="t" if kc == 7 else None)
        b.wait("scalar", "t", tmm)
        b.wait("scalar", "dg", b.counts["dg"])

        def fev(E, n0=n0):
            return E.activation(logits[0:16, n0:n0 + 512], banks[0][0:16, 0:512],
                                AF.Identity, bias=rbsb[:])
        sev = b.op("scalar", fev, inc="s")
        b.wait("tensor", "s", sev)
    for tt in range(16):
        def ftr(E, tt=tt):
            return E.transpose(banks[1][0:128, 0:16], logits[0:16, tt * 128:(tt + 1) * 128],
                               idf[0:16, 0:16])
        ttr = b.op("tensor", ftr, inc="t")
        b.wait("vector", "t", ttr)
        vv = b.op("vector", lambda E, tt=tt: E.tensor_copy(ltok[:, tt * 16:(tt + 1) * 16],
                                                           banks[1][0:128, 0:16]), inc="v")
        b.wait("tensor", "v", vv)
    for tt in range(16):
        lt = ltok[:, tt * 16:(tt + 1) * 16]

        def vstep(fn):
            t = b.op("vector", fn, inc="v")
            b.wait("vector", "v", t)
            return t

        vstep(lambda E, lt=lt: E.reduce_max(sml[:, 0:1], lt, axis=AX, negate=True))
        vm = b.counts["v"]
        b.wait("scalar", "v", vm)
        sx = b.op("scalar", lambda E, lt=lt: E.activation(et, lt, AF.Exp, bias=sml[:, 0:1]),
                  inc="s")
        b.wait("vector", "s", sx)
        vstep(lambda E: E.tensor_copy(tsc, et))
        for r_ in range(3):
            vstep(lambda E: E.reduce_max(sml[:, 1:2], tsc, axis=AX))
            vstep(lambda E: E.tensor_scalar(tmsk, tsc, sml[:, 1:2], NEG_BIG,
                                            ALU.is_ge, ALU.mult))
            vstep(lambda E: E.tensor_add(tsc, tsc, tmsk))
        vstep(lambda E: E.reduce_max(sml[:, 1:2], tsc, axis=AX))
        vstep(lambda E: E.scalar_tensor_tensor(tmsk, et, sml[:, 1:2], et,
                                               ALU.is_ge, ALU.mult, accum_out=sml[:, 2:3]))
        vstep(lambda E: E.reciprocal(sml[:, 3:4], sml[:, 2:3]))
        vstep(lambda E, tt=tt: E.tensor_scalar_mul(ctok[:, tt * 16:(tt + 1) * 16],
                                                   tmsk, sml[:, 3:4]))
    vct = b.counts["v"]
    b.wait("tensor", "v", vct)
    for tt in range(16):
        def ftr(E, tt=tt):
            return E.transpose(banks[1][0:16, 0:128], ctok[:, tt * 16:(tt + 1) * 16], idf[:])
        ttr = b.op("tensor", ftr, inc="t")
        b.wait("vector", "t", ttr)
        vv = b.op("vector", lambda E, tt=tt: E.tensor_copy(cTb[0:16, tt * 128:(tt + 1) * 128],
                                                           banks[1][0:16, 0:128]), inc="v")
        b.wait("tensor", "v", vv)

    for e in range(16):
        b.wait("gpsimd", "t", b.counts["t"])
        lw1 = b.dma("gpsimd", "dg",
                    w1sb[:].rearrange("p (c n) -> p c n", c=8),
                    w1d[e].rearrange("(c p) n -> p c n", p=128))
        lw2 = b.dma("gpsimd", "dg",
                    w2sb[:].rearrange("p (c n) -> p c n", c=16),
                    w2d[e].rearrange("(c p) n -> p c n", p=128))
        b.wait("tensor", "v", b.counts["v"])
        for sl in range(4):
            n0 = sl * 512
            def fbc(E, e=e, n0=n0):
                return E.matmul(banks[4][:, 0:512], eyeb[0:16, e * 128:(e + 1) * 128],
                                cTb[0:16, n0:n0 + 512], start=True, stop=True)
            tb = b.op("tensor", fbc, inc="t")
            b.wait("vector", "t", tb)
            vv = b.op("vector", lambda E, n0=n0: E.tensor_copy(cbc[:, n0:n0 + 512],
                                                               banks[4][:, 0:512]), inc="v")
            b.wait("tensor", "v", vv)
        b.wait("tensor", "dg", lw1)
        for sl in range(4):
            n0 = sl * 512
            b.wait("vector", "t", b.counts["t"])
            ev_war = {}
            for ic in range(16):
                bk = banks[ic % 2]
                if (ic % 2) in ev_war:
                    b.wait("tensor", "s", ev_war[ic % 2])
                for kc in range(8):
                    def fmm(E, e=e, sl=sl, ic=ic, kc=kc, n0=n0, bk=bk):
                        return E.matmul(bk[:, 0:512],
                                        w1sb[:, kc * 2048 + ic * 128:kc * 2048 + (ic + 1) * 128],
                                        xln[:, kc * TOK + n0:kc * TOK + n0 + 512],
                                        start=(kc == 0), stop=(kc == 7))
                    tmm = b.op("tensor", fmm, inc="t" if kc == 7 else None)
                b.wait("scalar", "t", tmm)

                def fge(E, e=e, ic=ic, bk=bk):
                    return E.activation(gtmp, bk[:, 0:512], AF.Gelu,
                                        bias=b1sb[:, ic * 16 + e:ic * 16 + e + 1])
                sge = b.op("scalar", fge, inc="s")
                ev_war[ic % 2] = sge
                b.wait("vector", "s", sge)

                def fmu(E, ic=ic, n0=n0):
                    return E.tensor_mul(hm[:, ic * 512:(ic + 1) * 512], gtmp,
                                        cbc[:, n0:n0 + 512])
                vmu = b.op("vector", fmu, inc="v")
                b.wait("scalar", "v", vmu)
            vhm = b.counts["v"]
            b.wait("tensor", "v", vhm)
            if e == 0 and sl == 0:
                b.wait("tensor", "dg", lw2)
            ev2_war = {}
            for oc in range(8):
                bk = banks[2 + oc % 2]
                if (oc % 2) in ev2_war:
                    b.wait("tensor", "v", ev2_war[oc % 2])
                for ic in range(16):
                    def fm2(E, e=e, sl=sl, oc=oc, ic=ic, n0=n0, bk=bk):
                        return E.matmul(bk[:, 0:512],
                                        w2sb[:, ic * 1024 + oc * 128:ic * 1024 + (oc + 1) * 128],
                                        hm[:, ic * 512:(ic + 1) * 512],
                                        start=(ic == 0), stop=(e > 0 and ic == 15))
                    tm2 = b.op("tensor", fm2,
                               inc="t" if (e > 0 and ic == 15) else None)
                if e == 0:
                    def fb2(E, oc=oc, n0=n0, bk=bk):
                        return E.matmul(bk[:, 0:512], b2sb[0:16, oc * 128:(oc + 1) * 128],
                                        cTb[0:16, n0:n0 + 512], start=False, stop=True)
                    tm2 = b.op("tensor", fb2, inc="t")
                b.wait("vector", "t", tm2)

                def fad(E, oc=oc, n0=n0, bk=bk):
                    xs = x[:, oc * TOK + n0:oc * TOK + n0 + 512]
                    return E.tensor_add(xs, xs, bk[:, 0:512])
                ev2_war[oc % 2] = b.op("vector", fad, inc="v")
    vfin = b.counts["v"]
    b.wait("sync", "v", vfin)
    st = b.dma("sync", "st", xout.rearrange("(c p) n -> p c n", p=128),
               x[:].rearrange("p (c n) -> p c n", c=8))
    b.wait("sync", "st", st)
    return b.finalize()


def build_l1_qkv():
    nc = bass.Bass("TRN2", target_bir_lowering=False)
    xin = nc.dram_tensor("xin", [D, TOK], F32, kind="ExternalInput").ap()
    wds = [nc.dram_tensor(n, [D, D], F32, kind="ExternalInput").ap()
           for n in ("qw", "kw", "vw")]
    outs = [nc.dram_tensor(n, [D, TOK], BF16, kind="ExternalOutput").ap()
            for n in ("qout", "kout", "vout")]

    b = Builder(nc)
    x = b.sbuf("x", [128, 8 * TOK], F32)
    xln = b.sbuf("xln", [128, 8 * TOK], BF16)
    arB = b.sbuf("arB", [128, 6144], F32)
    sq = arB.ap()[:, 0:TOK]
    rb = arB.ap()[:, TOK:2 * TOK]
    mb_ = arB.ap()[:, 2 * TOK:3 * TOK]
    stg = [arB.ap().bitcast(BF16)[:, 0:TOK], arB.ap().bitcast(BF16)[:, TOK:2 * TOK]]
    rowbuf = b.sbuf("rowbuf", [1, 4 * TOK], F32)
    w_sb = [b.sbuf(f"w_sb{i}", [128, 8 * 1024], BF16) for i in range(2)]
    banks = [b.psum(f"bk{i}") for i in range(8)]
    c = alloc_consts(b)

    ldx = b.dma("sync", "ds", x[:].rearrange("p (c n) -> p c n", c=8),
                xin.rearrange("(c p) n -> p c n", p=128))
    for e in ("scalar", "tensor", "vector"):
        b.wait(e, "ds", ldx)
        b.wait(e, "v", 1)
    vln = emit_layernorm(b, x, xln, banks, sq, rowbuf, c["ones_col"], c["ones_row"],
                         rb, mb_, c["epsb"])
    b.wait("tensor", "v", vln)
    wdma = [0, 0]
    for wi in range(3):
        wb = w_sb[wi % 2]
        b.wait("gpsimd", "t", b.counts["t"])
        wdma[wi % 2] = b.dma("gpsimd", "dg", wb[:].rearrange("p (c n) -> p c n", c=8),
                             wds[wi].rearrange("(c p) n -> p c n", p=128))
        b.wait("tensor", "dg", wdma[wi % 2])
        scale = 0.125 if wi == 0 else 1.0
        state = {}
        stg_war = [0, 0]
        for oc in range(8):
            sg = stg[oc % 2]
            b.wait("scalar", "st", stg_war[oc % 2])
            for sl in range(4):
                n0 = sl * 512
                bid = sl % 2
                bk = banks[bid]
                if bid in state:
                    b.wait("tensor", "s", state[bid])
                for kc in range(8):
                    def fmm(E, oc=oc, kc=kc, n0=n0, bk=bk, wb=wb):
                        return E.matmul(bk[:, 0:512],
                                        wb[:, kc * 1024 + oc * 128:kc * 1024 + (oc + 1) * 128],
                                        xln[:, kc * TOK + n0:kc * TOK + n0 + 512],
                                        start=(kc == 0), stop=(kc == 7))
                    tmm = b.op("tensor", fmm, inc="t" if kc == 7 else None)
                b.wait("scalar", "t", tmm)

                def fev(E, n0=n0, bk=bk, sg=sg, scale=scale):
                    return E.activation(sg[:, n0:n0 + 512], bk[:, 0:512], AF.Identity,
                                        scale=scale)
                state[bid] = b.op("scalar", fev, inc="s")
            b.wait("sync", "s", b.counts["s"])
            stg_war[oc % 2] = b.dma("sync", "st", outs[wi][oc * 128:(oc + 1) * 128, :], sg)
    b.wait("sync", "st", b.counts["st"])
    return b.finalize()


def emit_l1_attn_head(b, h, Qe, Ke, Vh, Vtok, bufs, is_first):
    Pb, sink = bufs["Pb"], bufs["sink"]
    mz, bpack, banks = bufs["mz"], bufs["bpack"], bufs["banks"]
    idb, m1, m2 = bufs["idb"], bufs["m1"], bufs["m2"]
    bts = bufs["bts"]
    O = bufs["O"]

    b.wait("tensor", "v", b.counts["v"])
    pT = bufs["pT"]
    big = bufs["big"]
    sink4k = bufs["sink4k"]
    vev = 0
    for g8 in range(4):
        if g8 > 0:
            b.wait("tensor", "v", vev)
        for j8 in range(8):
            kt = g8 * 8 + j8
            def ftr(E, kt=kt, j8=j8):
                return E.transpose(pT[0:128, j8 * 64:(j8 + 1) * 64],
                                   Vh[0:64, kt * 128:(kt + 1) * 128], idb[0:64, 0:64])
            ttr = b.op("tensor", ftr, inc="t" if j8 == 7 else None)
        b.wait("vector", "t", ttr)
        def fev(E, g8=g8):
            return E.tensor_copy(Vtok[:, g8 * 512:(g8 + 1) * 512], pT[0:128, 0:512])
        vev = b.op("vector", fev, inc="v")
    b.wait("tensor", "v", vev)
    sexp = 0
    for lq in range(16):
        km = L1_KMAX[lq]
        nk = km * 128
        nsl = (nk + 511) // 512
        for s in range(nsl):
            k0, k1 = s * 512, min(nk, s * 512 + 512)
            def fmm(E, lq=lq, s=s, k0=k0, k1=k1):
                return E.matmul(banks[s][:, 0:k1 - k0], Qe[0:67, lq * 128:(lq + 1) * 128],
                                Ke[0:67, k0:k1], start=True, stop=True)
            tp1 = b.op("tensor", fmm, inc="t" if s == nsl - 1 else None)
        b.wait("vector", "t", tp1)
        for j in range(3):
            kt = km - 3 + j
            def fmask(E, lq=lq, j=j, kt=kt):
                return E.tensor_add(big[:, kt * 128:(kt + 1) * 128],
                                    big[:, kt * 128:(kt + 1) * 128],
                                    m1[:, (lq * 3 + j) * 128:(lq * 3 + j + 1) * 128])
            b.op("vector", fmask)
        def fred(E, lq=lq, nk=nk):
            return E.reduce_max(mz[:, lq:lq + 1], big[:, 0:nk], axis=AX, negate=True)
        vred = b.op("vector", fred, inc="v")
        b.wait("scalar", "v", vred)
        def fexp(E, lq=lq, nk=nk):
            return E.activation(sink4k[:, 0:nk], big[:, 0:nk], AF.Exp,
                                bias=mz[:, lq:lq + 1], accum_out=mz[:, 32 + lq:33 + lq])
        sexp = b.op("scalar", fexp, inc="s")
        b.wait("tensor", "s", sexp)
        b.wait("vector", "s", sexp)
    b.wait("vector", "v", vred)
    b.wait("scalar", "s", sexp)
    sln = b.op("scalar", lambda E: E.activation(mz[:, 96:112], mz[:, 32:48], AF.Ln), inc="s")
    b.wait("vector", "s", sln)

    def vstep(fn):
        t = b.op("vector", fn, inc="v")
        b.wait("vector", "v", t)
        return t

    vstep(lambda E: E.tensor_sub(mz[:, 64:80], mz[:, 0:16], mz[:, 96:112]))
    bp = bpack
    vstep(lambda E: E.tensor_copy(bp[:, 0:48:3], mz[:, 64:80]))
    vstep(lambda E: E.tensor_copy(mz[:, 96:112], bp[:, 0:48:3]))
    vstep(lambda E: E.tensor_sub(mz[:, 112:128], mz[:, 64:80], mz[:, 96:112]))
    vstep(lambda E: E.tensor_copy(bp[:, 1:48:3], mz[:, 112:128]))
    vstep(lambda E: E.tensor_copy(mz[:, 96:112], bp[:, 1:48:3]))
    vstep(lambda E: E.tensor_sub(mz[:, 112:128], mz[:, 112:128], mz[:, 96:112]))
    vsp = vstep(lambda E: E.tensor_copy(bp[:, 2:48:3], mz[:, 112:128]))
    b.wait("tensor", "v", vsp)
    for lq in range(16):
        def ftr(E, lq=lq):
            return E.transpose(pT[0:3, 0:128], bp[:, lq * 3:lq * 3 + 3], idb[:])
        ttr = b.op("tensor", ftr, inc="t")
        b.wait("vector", "t", ttr)
        def fev(E, lq=lq):
            return E.tensor_copy(bts[0:3, lq * 128:(lq + 1) * 128], pT[0:3, 0:128])
        vev = b.op("vector", fev, inc="v")
        b.wait("tensor", "v", vev)
    b.wait("sync", "v", b.counts["v"])
    bdma = b.dma("sync", "ds", Qe[67:70, :], bts[0:3, :])
    b.wait("tensor", "ds", bdma)
    p2b = [banks[0], banks[1]]
    avb = [banks[2], banks[3]]
    last_av_evac = 0
    prev_tav = prev_sexp = 0
    for g in range(4):
        av = avb[g % 2]
        if last_av_evac:
            b.wait("tensor", "v", last_av_evac)
        if g > 0:
            b.wait("scalar", "t", prev_tav)
            b.wait("tensor", "s", prev_sexp)
        kms = [L1_KMAX[4 * g + i] for i in range(4)]
        ktmax = kms[3] - 1
        sexp_hist = {}
        tav_hist = {}
        for kt in range(ktmax + 1):
            nvalid = sum(1 for km in kms if kt < km)
            co = (4 - nvalid) * 128
            p2 = p2b[kt % 2]
            if kt >= 2:
                b.wait("tensor", "s", sexp_hist[kt - 2])
            def fmm(E, g=g, kt=kt, co=co, p2=p2):
                return E.matmul(p2[:, co:512], Ke[0:70, kt * 128:(kt + 1) * 128],
                                Qe[0:70, g * 512 + co:(g + 1) * 512], start=True, stop=True)
            tp2 = b.op("tensor", fmm, inc="t")
            nmask = 0
            for i in range(4):
                lq = 4 * g + i
                j = kt - (kms[i] - 3)
                if 0 <= j < 3 and kt < kms[i]:
                    def fmask(E, lq=lq, j=j, i=i, p2=p2):
                        return E.tensor_add(p2[:, i * 128:(i + 1) * 128],
                                            p2[:, i * 128:(i + 1) * 128],
                                            m2[:, (lq * 3 + j) * 128:(lq * 3 + j + 1) * 128])
                    if nmask == 0:
                        b.wait("vector", "t", tp2)
                    vm = b.op("vector", fmask, inc="v")
                    nmask += 1
            if nmask:
                b.wait("scalar", "v", vm)
            else:
                b.wait("scalar", "t", tp2)
            if kt >= 2:
                b.wait("scalar", "t", tav_hist[kt - 2])
            pb = Pb[kt % 2]
            def fexp(E, kt=kt, co=co, p2=p2, pb=pb):
                return E.activation(pb[:, co:512], p2[:, co:512], AF.Exp)
            sexp = b.op("scalar", fexp, inc="s")
            sexp_hist[kt] = sexp
            b.wait("tensor", "s", sexp)
            def fav(E, g=g, kt=kt, co=co, pb=pb, av=av):
                return E.matmul(av[0:64, co:512], Vtok[:, kt * 64:(kt + 1) * 64],
                                pb[:, co:512], start=(kt == 0), stop=(kt == ktmax))
            tav_hist[kt] = b.op("tensor", fav, inc="t")
        tav = b.counts["t"]
        prev_tav, prev_sexp = tav, sexp_hist[ktmax]
        b.wait("vector", "t", tav)
        cj, off = (64 * h) // 128, (64 * h) % 128
        def fevac(E, g=g, cj=cj, off=off, av=av):
            return E.tensor_copy(O[off:off + 64, cj * TOK + g * 512: cj * TOK + (g + 1) * 512],
                                 av[0:64, 0:512])
        last_av_evac = b.op("vector", fevac, inc="v")
    return last_av_evac


def build_l1_attn():
    nc = bass.Bass("TRN2", target_bir_lowering=False)
    xin = nc.dram_tensor("xin", [D, TOK], F32, kind="ExternalInput").ap()
    qin = nc.dram_tensor("qin", [D, TOK], BF16, kind="ExternalInput").ap()
    kgd = nc.dram_tensor("kg", [D, S_LEN], BF16, kind="ExternalInput").ap()
    vgd = nc.dram_tensor("vg", [D, S_LEN], BF16, kind="ExternalInput").ap()
    owd = nc.dram_tensor("ow", [D, D], F32, kind="ExternalInput").ap()
    al3d = nc.dram_tensor("al3", [16, 3, S_LEN], BF16, kind="ExternalInput").ap()
    ones3d = nc.dram_tensor("ones3", [3, S_LEN], BF16, kind="ExternalInput").ap()
    m1d = nc.dram_tensor("m1", [128, 6144], BF16, kind="ExternalInput").ap()
    m2d = nc.dram_tensor("m2", [128, 6144], BF16, kind="ExternalInput").ap()
    identd = nc.dram_tensor("ident", [128, 128], BF16, kind="ExternalInput").ap()
    xout = nc.dram_tensor("xout", [D, TOK], F32, kind="ExternalOutput").ap()

    b = Builder(nc)
    x = b.sbuf("x", [128, 8 * TOK], F32)
    O = b.sbuf("O", [128, 8 * TOK], BF16)
    Qes = [b.sbuf(f"Qe{i}", [128, TOK], BF16) for i in range(2)]
    Kes = [b.sbuf(f"Ke{i}", [128, S_LEN], BF16) for i in range(2)]
    Vhs = [b.sbuf(f"Vh{i}", [64, S_LEN], BF16) for i in range(2)]
    Vtoks = [b.sbuf(f"Vtok{i}", [128, 2048], BF16) for i in range(2)]
    ow_sb = b.sbuf("ow_sb", [128, 8 * 1024], BF16)
    bufs = {
        "Pb": [b.sbuf(f"Pb{i}", [128, 512], BF16) for i in range(2)],
        "sink": b.sbuf("sink", [128, 512], BF16),
        "mz": b.sbuf("mz", [128, 128], F32),
        "bpack": b.sbuf("bpack", [128, 48], BF16),
        "bts": b.sbuf("bts", [3, TOK], BF16),
        "m1": b.sbuf("m1_sb", [128, 6144], BF16),
        "m2": b.sbuf("m2_sb", [128, 6144], BF16),
        "idb": b.sbuf("idb_sb", [128, 128], BF16),
        "O": O.ap(),
    }
    banks = [b.psum(f"bk{i}") for i in range(8)]
    bufs["banks"] = banks

    ldx = b.dma("sync", "ds", x[:].rearrange("p (c n) -> p c n", c=8),
                xin.rearrange("(c p) n -> p c n", p=128))
    b.dma("sync", "ds", bufs["m1"][:], m1d)
    b.dma("sync", "ds", bufs["m2"][:], m2d)
    b.dma("sync", "ds", bufs["idb"][:], identd)
    for i in range(2):
        b.dma("sync", "ds", Qes[i][64:67, :], ones3d[:, 0:TOK])
        b.dma("sync", "ds", Kes[i][67:70, :], ones3d)
    ldow = b.dma("gpsimd", "dg", ow_sb[:].rearrange("p (c n) -> p c n", c=8),
                 owd.rearrange("(c p) n -> p c n", p=128))
    ldc = b.counts["ds"]
    for e in ("scalar", "tensor", "vector"):
        b.wait(e, "ds", ldc)

    hdma = [0, 0]
    for h in range(16):
        sl_ = h % 2
        Qe, Ke, Vh, Vtok = Qes[sl_], Kes[sl_], Vhs[sl_], Vtoks[sl_]
        b.wait("sync", "t", b.counts["t"])
        b.dma("sync", "ds", Qe[0:64, :], qin[64 * h:64 * h + 64, :])
        b.dma("sync", "ds", Ke[0:64, :], kgd[64 * h:64 * h + 64, :])
        b.dma("sync", "ds", Ke[64:67, :], al3d[h])
        hdma[sl_] = b.dma("sync", "ds", Vh[0:64, :], vgd[64 * h:64 * h + 64, :])
        for e in ("tensor", "vector", "scalar"):
            b.wait(e, "ds", hdma[sl_])
        emit_l1_attn_head(b, h, Qe, Ke, Vh, Vtok, bufs, h == 0)
    vO = b.counts["v"]
    b.wait("tensor", "v", vO)
    b.wait("tensor", "dg", ldow)
    ev_war = {}
    for oc in range(8):
        for sl in range(4):
            n0 = sl * 512
            bid = 4 + (oc * 4 + sl) % 2
            bk = banks[bid]
            if bid in ev_war:
                b.wait("tensor", "v", ev_war[bid])
            for kc in range(8):
                def fmm(E, oc=oc, kc=kc, n0=n0, bk=bk):
                    return E.matmul(bk[:, 0:512],
                                    ow_sb[:, kc * 1024 + oc * 128:kc * 1024 + (oc + 1) * 128],
                                    O[:, kc * TOK + n0:kc * TOK + n0 + 512],
                                    start=(kc == 0), stop=(kc == 7))
                tmm = b.op("tensor", fmm, inc="t" if kc == 7 else None)
            b.wait("vector", "t", tmm)

            def fad(E, oc=oc, n0=n0, bk=bk):
                xs = x[:, oc * TOK + n0:oc * TOK + n0 + 512]
                return E.tensor_add(xs, xs, bk[:, 0:512])
            ev_war[bid] = b.op("vector", fad, inc="v")
    vfin = b.counts["v"]
    b.wait("sync", "v", vfin)
    st = b.dma("sync", "st", xout.rearrange("(c p) n -> p c n", p=128),
               x[:].rearrange("p (c n) -> p c n", c=8))
    b.wait("sync", "st", st)
    return b.finalize()


def host_l1_consts():
    slopes = np.asarray([(2.0 ** (-8.0 / 16)) ** (i + 1) for i in range(16)], np.float64)
    k = np.arange(S_LEN, dtype=np.float64)
    al3 = np.zeros((16, 3, S_LEN), BF16NP)
    for h in range(16):
        v = slopes[h] * k
        h1 = v.astype(BF16NP)
        r1 = v - h1.astype(np.float64)
        h2 = r1.astype(BF16NP)
        r2 = r1 - h2.astype(np.float64)
        h3 = r2.astype(BF16NP)
        al3[h, 0], al3[h, 1], al3[h, 2] = h1, h2, h3
    ones3 = np.ones((3, S_LEN), BF16NP)
    ident = np.eye(128, dtype=BF16NP)
    return al3, ones3, ident


def build_l1_ffn():
    nc = bass.Bass("TRN2", target_bir_lowering=False)
    xin = nc.dram_tensor("xin", [D, TOK], F32, kind="ExternalInput").ap()
    seld = nc.dram_tensor("selt", [D, 32], F32, kind="ExternalInput").ap()
    keyd = nc.dram_tensor("keys", [32, D, 512], F32, kind="ExternalInput").ap()
    vald = nc.dram_tensor("values", [32, 512, D], F32, kind="ExternalInput").ap()
    idfd = nc.dram_tensor("idf", [128, 128], F32, kind="ExternalInput").ap()
    eyed = nc.dram_tensor("eyeb32", [32, 32 * 128], BF16, kind="ExternalInput").ap()
    xout = nc.dram_tensor("xout", [D, TOK], F32, kind="ExternalOutput").ap()

    b = Builder(nc)
    x = b.sbuf("x", [128, 8 * TOK], F32)
    xln = b.sbuf("xln", [128, 8 * TOK], BF16)
    arB = b.sbuf("arB", [128, 6144], F32)
    sq = arB.ap()[:, 0:TOK]
    rb = arB.ap()[:, TOK:2 * TOK]
    mb_ = arB.ap()[:, 2 * TOK:3 * TOK]
    gates = arB.ap()[0:32, 0:TOK]
    gtok = arB.ap()[:, 2048:2080]
    et = arB.ap()[:, 2560:2592]
    tsc = arB.ap()[:, 2592:2624]
    tmsk = arB.ap()[:, 2624:2656]
    gftok = arB.ap()[:, 2656:3168]
    gbc = arB.ap().bitcast(BF16)[:, 6336:8384]
    scsc = arB.ap().bitcast(BF16)[:, 8384:10432]
    sctmp = arB.ap().bitcast(BF16)[:, 10432:10944]
    rowbuf = b.sbuf("rowbuf", [1, 4 * TOK], F32)
    k_sb = b.sbuf("k_sb", [128, 8 * 512], BF16)
    v_sb = b.sbuf("v_sb", [128, 4 * 1024], BF16)
    selw = b.sbuf("selw", [128, 8 * 32], BF16)
    gT = b.sbuf("gT", [32, TOK], BF16)
    idf = b.sbuf("idf_sb", [128, 128], F32)
    eyeb = b.sbuf("eyeb_sb", [32, 32 * 128], BF16)
    sml = b.sbuf("sml", [128, 8], F32)
    banks = [b.psum(f"bk{i}") for i in range(8)]
    c = alloc_consts(b)

    ldx = b.dma("sync", "ds", x[:].rearrange("p (c n) -> p c n", c=8),
                xin.rearrange("(c p) n -> p c n", p=128))
    b.dma("sync", "ds", idf[:], idfd)
    b.dma("sync", "ds", eyeb[:], eyed)
    ldsel = b.dma("gpsimd", "dg", selw[:].rearrange("p (c n) -> p c n", c=8),
                  seld.rearrange("(c p) n -> p c n", p=128))
    for e in ("scalar", "tensor", "vector"):
        b.wait(e, "ds", ldx)
        b.wait(e, "v", 1)
    vln = emit_layernorm(b, x, xln, banks, sq, rowbuf, c["ones_col"], c["ones_row"],
                         rb, mb_, c["epsb"])
    b.wait("tensor", "v", vln)
    b.wait("tensor", "dg", ldsel)
    b.wait("tensor", "ds", b.counts["ds"])

    for sl in range(4):
        n0 = sl * 512
        for kc in range(8):
            def fmm(E, sl=sl, kc=kc, n0=n0):
                return E.matmul(banks[0][0:32, 0:512],
                                selw[:, kc * 32:(kc + 1) * 32],
                                xln[:, kc * TOK + n0: kc * TOK + n0 + 512],
                                start=(kc == 0), stop=(kc == 7))
            tmm = b.op("tensor", fmm, inc="t" if kc == 7 else None)
        b.wait("scalar", "t", tmm)

        def fev(E, n0=n0):
            return E.activation(gates[:, n0:n0 + 512], banks[0][0:32, 0:512], AF.Sigmoid)
        sev = b.op("scalar", fev, inc="s")
        b.wait("tensor", "s", sev)
    for tt in range(16):
        def ftr(E, tt=tt):
            return E.transpose(banks[1][0:128, 0:32], gates[:, tt * 128:(tt + 1) * 128],
                               idf[0:32, 0:32])
        ttr = b.op("tensor", ftr, inc="t")
        b.wait("vector", "t", ttr)
        vv = b.op("vector", lambda E, tt=tt: E.tensor_copy(gtok, banks[1][0:128, 0:32]),
                  inc="v")
        b.wait("tensor", "v", vv)

        def vstep(fn):
            t = b.op("vector", fn, inc="v")
            b.wait("vector", "v", t)
            return t

        vstep(lambda E: E.tensor_copy(tsc, gtok))
        for r_ in range(3):
            vstep(lambda E: E.reduce_max(sml[:, 1:2], tsc, axis=AX))
            vstep(lambda E: E.tensor_scalar(tmsk, tsc, sml[:, 1:2], NEG_BIG,
                                            ALU.is_ge, ALU.mult))
            vstep(lambda E: E.tensor_add(tsc, tsc, tmsk))
        vstep(lambda E: E.reduce_max(sml[:, 1:2], tsc, axis=AX))
        vstep(lambda E: E.scalar_tensor_tensor(gftok[:, 0:32], gtok, sml[:, 1:2], gtok,
                                               ALU.is_ge, ALU.mult))
        vgf = b.counts["v"]
        b.wait("tensor", "v", vgf)
        def ftr2(E, tt=tt):
            return E.transpose(banks[1][0:32, 0:128], gftok[:, 0:32], idf[:])
        ttr2 = b.op("tensor", ftr2, inc="t")
        b.wait("vector", "t", ttr2)
        vv2 = b.op("vector", lambda E, tt=tt: E.tensor_copy(gT[0:32, tt * 128:(tt + 1) * 128],
                                                            banks[1][0:32, 0:128]), inc="v")
        b.wait("tensor", "v", vv2)

    for e in range(32):
        b.wait("gpsimd", "t", b.counts["t"])
        lk = b.dma("gpsimd", "dg", k_sb[:].rearrange("p (c n) -> p c n", c=8),
                   keyd[e].rearrange("(c p) n -> p c n", p=128))
        lv = b.dma("gpsimd", "dg", v_sb[:].rearrange("p (c n) -> p c n", c=4),
                   vald[e].rearrange("(c p) n -> p c n", p=128))
        b.wait("tensor", "v", b.counts["v"])
        for sl in range(4):
            n0 = sl * 512
            def fbc(E, e=e, n0=n0):
                return E.matmul(banks[4][:, 0:512], eyeb[0:32, e * 128:(e + 1) * 128],
                                gT[0:32, n0:n0 + 512], start=True, stop=True)
            tb = b.op("tensor", fbc, inc="t")
            b.wait("vector", "t", tb)
            vv = b.op("vector", lambda E, n0=n0: E.tensor_copy(gbc[:, n0:n0 + 512],
                                                               banks[4][:, 0:512]), inc="v")
            b.wait("tensor", "v", vv)
        b.wait("tensor", "dg", lk)
        for sl in range(4):
            n0 = sl * 512
            b.wait("vector", "t", b.counts["t"])
            ev_war = {}
            for ic in range(4):
                bk = banks[ic % 2]
                if (ic % 2) in ev_war:
                    b.wait("tensor", "s", ev_war[ic % 2])
                for kc in range(8):
                    def fmm(E, e=e, sl=sl, ic=ic, kc=kc, n0=n0, bk=bk):
                        return E.matmul(bk[:, 0:512],
                                        k_sb[:, kc * 512 + ic * 128:kc * 512 + (ic + 1) * 128],
                                        xln[:, kc * TOK + n0:kc * TOK + n0 + 512],
                                        start=(kc == 0), stop=(kc == 7))
                    tmm = b.op("tensor", fmm, inc="t" if kc == 7 else None)
                b.wait("scalar", "t", tmm)

                def fre(E, ic=ic, bk=bk):
                    return E.activation(sctmp, bk[:, 0:512], AF.Relu)
                sre = b.op("scalar", fre, inc="s")
                ev_war[ic % 2] = sre
                b.wait("vector", "s", sre)

                def fmu(E, ic=ic, n0=n0):
                    return E.tensor_mul(scsc[:, ic * 512:(ic + 1) * 512], sctmp,
                                        gbc[:, n0:n0 + 512])
                vmu = b.op("vector", fmu, inc="v")
                b.wait("scalar", "v", vmu)
            vhm = b.counts["v"]
            b.wait("tensor", "v", vhm)
            if e == 0 and sl == 0:
                b.wait("tensor", "dg", lv)
            ev2_war = {}
            for oc in range(8):
                bk = banks[2 + oc % 2]
                if (oc % 2) in ev2_war:
                    b.wait("tensor", "v", ev2_war[oc % 2])
                for ic in range(4):
                    def fm2(E, oc=oc, ic=ic, bk=bk):
                        return E.matmul(bk[:, 0:512],
                                        v_sb[:, ic * 1024 + oc * 128:ic * 1024 + (oc + 1) * 128],
                                        scsc[:, ic * 512:(ic + 1) * 512],
                                        start=(ic == 0), stop=(ic == 3))
                    tm2 = b.op("tensor", fm2, inc="t" if ic == 3 else None)
                b.wait("vector", "t", tm2)

                def fad(E, oc=oc, n0=n0, bk=bk):
                    xs = x[:, oc * TOK + n0:oc * TOK + n0 + 512]
                    return E.tensor_add(xs, xs, bk[:, 0:512])
                ev2_war[oc % 2] = b.op("vector", fad, inc="v")
    vfin = b.counts["v"]
    b.wait("sync", "v", vfin)
    st = b.dma("sync", "st", xout.rearrange("(c p) n -> p c n", p=128),
               x[:].rearrange("p (c n) -> p c n", c=8))
    b.wait("sync", "st", st)
    return b.finalize()


def _host_consts():
    tri1, tri2 = host_tri_masks()
    ident = np.eye(128, dtype=BF16NP)
    idf = np.eye(128, dtype=np.float32)
    eyeb16 = np.zeros((16, 16 * 128), BF16NP)
    for e in range(16):
        eyeb16[e, e * 128:(e + 1) * 128] = 1.0
    eyeb32 = np.zeros((32, 32 * 128), BF16NP)
    for e in range(32):
        eyeb32[e, e * 128:(e + 1) * 128] = 1.0
    al3, ones3, _ = host_l1_consts()
    m1 = {}
    m2 = {}
    for is_a in (True, False):
        a1 = host_l1_masks(is_a, transposed=False).astype(BF16NP)
        a2 = host_l1_masks(is_a, transposed=True).astype(BF16NP)
        m1[is_a] = np.ascontiguousarray(a1.transpose(2, 0, 1, 3).reshape(128, 6144))
        m2[is_a] = np.ascontiguousarray(a2.transpose(2, 0, 1, 3).reshape(128, 6144))
    return dict(tri1=tri1, tri2=tri2, ident=ident, idf=idf, eyeb16=eyeb16,
                eyeb32=eyeb32, al3=al3, ones3=ones3, m1=m1, m2=m2)


def _assemble_global(parts, bi):
    out = np.empty((D, S_LEN), BF16NP)
    for half, ch in enumerate((CH_A, CH_B)):
        src = parts[2 * bi + half]
        for li, gc in enumerate(ch):
            out[:, 256 * gc:256 * gc + 256] = src[:, 256 * li:256 * li + 256]
    return out


def kernel_multilaunch(L0_attn_inputs, L0_attn_qkv_kernel, L0_attn_qkv_bias,
           L0_ffn_router_w, L0_ffn_router_b, L0_ffn_w1, L0_ffn_b1,
           L0_ffn_w2, L0_ffn_b2, L1_attn_q_w, L1_attn_k_w, L1_attn_v_w,
           L1_attn_o_w, L1_ffn_expert_sel, L1_ffn_keys, L1_ffn_values):
    cst = _host_consts()
    cores = list(range(8))
    shards = shard_tokens(np.asarray(L0_attn_inputs, dtype=np.float32))

    nc = build_l0_qkv()
    res = run_bass_kernel_spmd(
        nc, [{"xin": s_, "wq": np.asarray(L0_attn_qkv_kernel),
              "bq": np.asarray(L0_attn_qkv_bias)} for s_ in shards], core_ids=cores)
    qkvs = [r["qkvout"] for r in res.results]

    nc = build_l0_attn()
    res = run_bass_kernel_spmd(
        nc, [{"xin": s_, "qkv": q_, "tri1": cst["tri1"], "tri2": cst["tri2"],
              "ident": cst["ident"]} for s_, q_ in zip(shards, qkvs)], core_ids=cores)
    shards = [r["xout"] for r in res.results]

    nc = build_l0_ffn()
    b1t = np.ascontiguousarray(np.asarray(L0_ffn_b1).T)
    res = run_bass_kernel_spmd(
        nc, [{"xin": s_, "rw": np.asarray(L0_ffn_router_w),
              "rb_in": np.asarray(L0_ffn_router_b).reshape(16, 1),
              "w1": np.asarray(L0_ffn_w1), "b1t": b1t,
              "w2": np.asarray(L0_ffn_w2), "b2": np.asarray(L0_ffn_b2),
              "idf": cst["idf"], "eyeb": cst["eyeb16"]} for s_ in shards],
        core_ids=cores)
    shards = [r["xout"] for r in res.results]

    nc = build_l1_qkv()
    res = run_bass_kernel_spmd(
        nc, [{"xin": s_, "qw": np.asarray(L1_attn_q_w), "kw": np.asarray(L1_attn_k_w),
              "vw": np.asarray(L1_attn_v_w)} for s_ in shards], core_ids=cores)
    qs = [r["qout"] for r in res.results]
    ks = [r["kout"] for r in res.results]
    vs = [r["vout"] for r in res.results]
    kg = [_assemble_global(ks, bi) for bi in range(4)]
    vg = [_assemble_global(vs, bi) for bi in range(4)]

    nc = build_l1_attn()
    in_maps = []
    for ci in range(8):
        bi, half = ci // 2, ci % 2
        is_a = (half == 0)
        in_maps.append({"xin": shards[ci], "qin": qs[ci], "kg": kg[bi], "vg": vg[bi],
                        "ow": np.asarray(L1_attn_o_w), "al3": cst["al3"],
                        "ones3": cst["ones3"], "m1": cst["m1"][is_a],
                        "m2": cst["m2"][is_a], "ident": cst["ident"]})
    res = run_bass_kernel_spmd(nc, in_maps, core_ids=cores)
    shards = [r["xout"] for r in res.results]

    nc = build_l1_ffn()
    selt = np.ascontiguousarray(np.asarray(L1_ffn_expert_sel).T)
    res = run_bass_kernel_spmd(
        nc, [{"xin": s_, "selt": selt, "keys": np.asarray(L1_ffn_keys),
              "values": np.asarray(L1_ffn_values), "idf": cst["idf"],
              "eyeb32": cst["eyeb32"]} for s_ in shards], core_ids=cores)
    return unshard_tokens([r["xout"] for r in res.results], np.float32)


def build_fused():
    nc = bass.Bass("TRN2", target_bir_lowering=False)
    xin = nc.dram_tensor("xin", [D, TOK], F32, kind="ExternalInput").ap()
    wqkv = nc.dram_tensor("wqkv", [D, 3072], F32, kind="ExternalInput").ap()
    bqkv = nc.dram_tensor("bqkv", [1, 3072], F32, kind="ExternalInput").ap()
    rwd = nc.dram_tensor("rw", [D, 16], F32, kind="ExternalInput").ap()
    rbd = nc.dram_tensor("rb_in", [16, 1], F32, kind="ExternalInput").ap()
    w1d = nc.dram_tensor("w1", [16, D, 2048], F32, kind="ExternalInput").ap()
    b1d = nc.dram_tensor("b1t", [2048, 16], F32, kind="ExternalInput").ap()
    w2d = nc.dram_tensor("w2", [16, 2048, D], F32, kind="ExternalInput").ap()
    b2d = nc.dram_tensor("b2", [16, D], F32, kind="ExternalInput").ap()
    qwd = nc.dram_tensor("qw", [D, D], F32, kind="ExternalInput").ap()
    kwd = nc.dram_tensor("kw", [D, D], F32, kind="ExternalInput").ap()
    vwd = nc.dram_tensor("vw", [D, D], F32, kind="ExternalInput").ap()
    owd = nc.dram_tensor("ow", [D, D], F32, kind="ExternalInput").ap()
    seld = nc.dram_tensor("selt", [D, 32], F32, kind="ExternalInput").ap()
    keyd = nc.dram_tensor("keys", [32, D, 512], F32, kind="ExternalInput").ap()
    vald = nc.dram_tensor("values", [32, 512, D], F32, kind="ExternalInput").ap()
    tri1d = nc.dram_tensor("tri1", [128, 128], F32, kind="ExternalInput").ap()
    tri2d = nc.dram_tensor("tri2", [128, 128], F32, kind="ExternalInput").ap()
    identd = nc.dram_tensor("ident", [128, 128], BF16, kind="ExternalInput").ap()
    idfd = nc.dram_tensor("idf", [128, 128], F32, kind="ExternalInput").ap()
    eye16d = nc.dram_tensor("eyeb16", [16, 16 * 128], BF16, kind="ExternalInput").ap()
    eye32d = nc.dram_tensor("eyeb32", [32, 32 * 128], BF16, kind="ExternalInput").ap()
    al3d = nc.dram_tensor("al3", [16, 3, S_LEN], BF16, kind="ExternalInput").ap()
    ones3d = nc.dram_tensor("ones3", [3, S_LEN], BF16, kind="ExternalInput").ap()
    m1d = nc.dram_tensor("m1", [128, 6144], BF16, kind="ExternalInput").ap()
    m2d = nc.dram_tensor("m2", [128, 6144], BF16, kind="ExternalInput").ap()
    xout = nc.dram_tensor("xout", [D, TOK], F32, kind="ExternalOutput").ap()
    qkv_s = nc.dram_tensor("qkv_s", [3072, TOK], BF16, kind="Internal").ap()
    q_s = nc.dram_tensor("q_s", [D, TOK], BF16, kind="Internal").ap()
    kv_loc = nc.dram_tensor("kv_loc", [2 * D, TOK], BF16, kind="Internal").ap()
    kv_gath = nc.dram_tensor("kv_gath", [4 * D, TOK], BF16, kind="Internal").ap()
    kg_s = nc.dram_tensor("kg_s", [D, S_LEN], BF16, kind="Internal").ap()
    vg_s = nc.dram_tensor("vg_s", [D, S_LEN], BF16, kind="Internal").ap()

    b = Builder(nc)
    x = b.sbuf("x", [128, 8 * TOK], F32)
    xln = b.sbuf("xln", [128, 8 * TOK], BF16)
    AR1 = b.sbuf("AR1", [128, 15680], F32)
    AR2 = b.sbuf("AR2", [128, 12288], F32)
    idb = b.sbuf("idb_sb", [128, 128], BF16)
    idf = b.sbuf("idf_sb", [128, 128], F32)
    mz = b.sbuf("mz", [128, 160], F32)
    bpack = b.sbuf("bpack", [128, 96], BF16)
    sml = b.sbuf("sml", [128, 8], F32)
    qkvbias = b.sbuf("qkvbias", [128, 24], F32)
    rbsb = b.sbuf("rbsb", [16, 1], F32)
    c = alloc_consts(b)
    bigps = b.psum("bigps", (128, 4096))
    banks = [bigps.ap()[:, i * 512:(i + 1) * 512] for i in range(8)]
    bigap = bigps.ap()

    a1f = AR1.ap()
    a1b = AR1.ap().bitcast(BF16)
    a2f = AR2.ap()
    a2b = AR2.ap().bitcast(BF16)
    tri1 = a2f[:, 7680:7808]
    tri2 = a2f[:, 7808:7936]
    Pb2 = [a2b[:, 14336:14848], a2b[:, 14848:15360]]
    Pb6 = [a2b[:, 18432:18944], a2b[:, 18944:19456]]
    rowbuf = a1f[0:1, 0:4 * TOK]
    ln_sq = a1f[:, 8192:10240]
    ln_rb = a1f[:, 10240:12288]
    ln_mb = a1f[:, 12288:14336]

    def do_ln():
        vln = emit_layernorm(b, x, xln, banks, ln_sq, rowbuf, c["ones_col"],
                             c["ones_row"], ln_rb, ln_mb, c["epsb"])
        return vln

    ldx = b.dma("sync", "ds", x[:].rearrange("p (c n) -> p c n", c=8),
                xin.rearrange("(c p) n -> p c n", p=128))
    b.dma("sync", "ds", idb[:], identd)
    b.dma("sync", "ds", idf[:], idfd)
    b.dma("sync", "ds", rbsb[:], rbd)
    ld0 = b.dma("gpsimd", "dg", qkvbias[:], bqkv.rearrange("o (c p) -> p (o c)", p=128),
                allow_slow_non_contiguous=True)
    w_qkv = a2b[:, 0:24576]
    ldw = b.dma("gpsimd", "dg", w_qkv.rearrange("p (c n) -> p c n", c=8),
                wqkv.rearrange("(c p) n -> p c n", p=128))
    for e in ("scalar", "tensor", "vector"):
        b.wait(e, "ds", ldx)
        b.wait(e, "v", 1)

    vln = do_ln()
    b.wait("tensor", "v", vln)
    b.wait("tensor", "dg", ldw)
    b.wait("scalar", "dg", ld0)
    stg = [a1b[:, 0:2048], a1b[:, 2048:4096]]
    state = {}
    stg_war = [0, 0]
    for oc in range(24):
        sg = stg[oc % 2]
        b.wait("scalar", "st", stg_war[oc % 2])
        for sl in range(4):
            n0 = sl * 512
            bid = sl % 2
            bk = banks[bid]
            if bid in state:
                b.wait("tensor", "s", state[bid])
            for kc in range(8):
                def fmm(E, oc=oc, kc=kc, n0=n0, bk=bk):
                    return E.matmul(bk[:, 0:512],
                                    w_qkv[:, kc * 3072 + oc * 128: kc * 3072 + (oc + 1) * 128],
                                    xln[:, kc * TOK + n0: kc * TOK + n0 + 512],
                                    start=(kc == 0), stop=(kc == 7))
                tmm = b.op("tensor", fmm, inc="t" if kc == 7 else None)
            b.wait("scalar", "t", tmm)

            def fev(E, oc=oc, n0=n0, bk=bk, sg=sg):
                return E.activation(sg[:, n0:n0 + 512], bk[:, 0:512], AF.Identity,
                                    bias=qkvbias[:, oc:oc + 1])
            state[bid] = b.op("scalar", fev, inc="s")
        b.wait("sync", "s", b.counts["s"])
        stg_war[oc % 2] = b.dma("sync", "st", qkv_s[oc * 128:(oc + 1) * 128, :], sg)
    b.barrier()
    qkv_done = b.counts["st"]
    b.wait("tensor", "st", qkv_done)
    b.wait("sync", "st", qkv_done)

    ldtr = b.dma("sync", "ds", tri1, tri1d)
    ldtr = b.dma("sync", "ds", tri2, tri2d)
    for e_ in ("tensor", "vector", "scalar"):
        b.wait(e_, "ds", ldtr)
    abufs = {
        "Qt": a2b[:, 0:4096], "Kt": a2b[:, 4096:8192],
        "Vt": a2b[0:64, 8192:12288], "Vtok": a2b[:, 12288:14336],
        "Pb": Pb2, "sink": Pb2[0], "mz": mz, "bpack": bpack,
        "tri1": tri1, "tri2": tri2, "idb": idb.ap(), "banks": banks,
        "pT": bigap.bitcast(BF16)[:, 7168:8192], "big": bigap,
        "sink4k": a2b[:, 16384:20480],
    }
    b.op("vector", lambda E: E.memset(abufs["Kt"][64:67, :], 1.0), inc="v")
    b.wait("tensor", "v", b.counts["v"])
    qkv_r = qkv_s.rearrange("(c p) n -> p c n", p=128)
    qkv_g = [a1b[:, 4096 + i * 12288: 4096 + (i + 1) * 12288] for i in range(2)]
    for g in range(4):
        gb = qkv_g[g % 2]
        b.wait("sync", "v", b.counts["v"])
        ldg = b.dma("sync", "ds", gb.rearrange("p (c n) -> p c n", c=24),
                    qkv_r[:, :, g * 512:(g + 1) * 512])
        b.wait("vector", "ds", ldg)
        for i in range(2):
            li = 2 * g + i
            emit_l0_attn_instance(b, gb, i * 256, x, li * 256, abufs, None)
    b.barrier()

    w2sb = a1b[:, 0:16384]
    gtmp = a1b[:, 16384:16896]
    cbc = a1b[:, 16896:18944]
    cTb_f = a1b[0:16, 18944:20992]
    eyeb16 = a1b[0:16, 20992:23040]
    b1sb = a1f[:, 11776:12032]
    logits = a1f[0:16, 12032:14080]
    ltok = a1f[:, 14080:14336]
    l_et = a1f[:, 14336:14352]
    l_tsc = a1f[:, 14352:14368]
    l_tmsk = a1f[:, 14368:14384]
    ctok = a1f[:, 14384:14640]
    w1sb = a2b[:, 0:16384]
    hm = a2b[:, 16384:24576]
    rwsb = b.sbuf("rwsb", [128, 8 * 16], BF16)
    b2sb = a1b[0:16, 29280:30304]

    vln = do_ln()
    b.wait("gpsimd", "v", vln)
    ldc = b.dma("gpsimd", "dg", rwsb[:].rearrange("p (c n) -> p c n", c=8),
                rwd.rearrange("(c p) n -> p c n", p=128))
    b.dma("gpsimd", "dg", b1sb.rearrange("p (c e) -> p c e", c=16),
          b1d.rearrange("(c p) e -> p c e", p=128))
    b.dma("gpsimd", "dg", b2sb, b2d)
    ldg2 = b.dma("gpsimd", "dg", eyeb16, eye16d)
    b.wait("tensor", "v", vln)
    b.wait("tensor", "dg", ldc)

    for sl in range(4):
        n0 = sl * 512
        for kc in range(8):
            def fmm(E, sl=sl, kc=kc, n0=n0):
                return E.matmul(banks[0][0:16, 0:512],
                                rwsb[:, kc * 16:(kc + 1) * 16],
                                xln[:, kc * TOK + n0: kc * TOK + n0 + 512],
                                start=(kc == 0), stop=(kc == 7))
            tmm = b.op("tensor", fmm, inc="t" if kc == 7 else None)
        b.wait("scalar", "t", tmm)

        def fev(E, n0=n0):
            return E.activation(logits[0:16, n0:n0 + 512], banks[0][0:16, 0:512],
                                AF.Identity, bias=rbsb[:])
        sev = b.op("scalar", fev, inc="s")
        b.wait("tensor", "s", sev)
    for tt in range(16):
        def ftr(E, tt=tt):
            return E.transpose(banks[1][0:128, 0:16], logits[0:16, tt * 128:(tt + 1) * 128],
                               idf[0:16, 0:16])
        ttr = b.op("tensor", ftr, inc="t")
        b.wait("vector", "t", ttr)
        vv = b.op("vector", lambda E, tt=tt: E.tensor_copy(ltok[:, tt * 16:(tt + 1) * 16],
                                                           banks[1][0:128, 0:16]), inc="v")
        b.wait("tensor", "v", vv)
    for tt in range(16):
        lt = ltok[:, tt * 16:(tt + 1) * 16]

        def vstep(fn):
            t = b.op("vector", fn, inc="v")
            b.wait("vector", "v", t)
            return t

        vstep(lambda E, lt=lt: E.reduce_max(sml[:, 0:1], lt, axis=AX, negate=True))
        vm = b.counts["v"]
        b.wait("scalar", "v", vm)
        sx = b.op("scalar", lambda E, lt=lt: E.activation(l_et, lt, AF.Exp,
                                                          bias=sml[:, 0:1]), inc="s")
        b.wait("vector", "s", sx)
        vstep(lambda E: E.tensor_copy(l_tsc, l_et))
        for r_ in range(3):
            vstep(lambda E: E.reduce_max(sml[:, 1:2], l_tsc, axis=AX))
            vstep(lambda E: E.tensor_scalar(l_tmsk, l_tsc, sml[:, 1:2], NEG_BIG,
                                            ALU.is_ge, ALU.mult))
            vstep(lambda E: E.tensor_add(l_tsc, l_tsc, l_tmsk))
        vstep(lambda E: E.reduce_max(sml[:, 1:2], l_tsc, axis=AX))
        vstep(lambda E: E.scalar_tensor_tensor(l_tmsk, l_et, sml[:, 1:2], l_et,
                                               ALU.is_ge, ALU.mult, accum_out=sml[:, 2:3]))
        vstep(lambda E: E.reciprocal(sml[:, 3:4], sml[:, 2:3]))
        vstep(lambda E, tt=tt: E.tensor_scalar_mul(ctok[:, tt * 16:(tt + 1) * 16],
                                                   l_tmsk, sml[:, 3:4]))
    vct = b.counts["v"]
    b.wait("tensor", "v", vct)
    for tt in range(16):
        def ftr(E, tt=tt):
            return E.transpose(banks[1][0:16, 0:128], ctok[:, tt * 16:(tt + 1) * 16], idf[:])
        ttr = b.op("tensor", ftr, inc="t")
        b.wait("vector", "t", ttr)
        vv = b.op("vector", lambda E, tt=tt: E.tensor_copy(cTb_f[0:16, tt * 128:(tt + 1) * 128],
                                                           banks[1][0:16, 0:128]), inc="v")
        b.wait("tensor", "v", vv)
    b.wait("scalar", "dg", ldg2)
    lw1d = {}
    lw2d = {}

    def issue_w1(e):
        lw1d[e] = b.dma("gpsimd", "dg", w1sb.rearrange("p (c n) -> p c n", c=8),
                        w1d[e].rearrange("(c p) n -> p c n", p=128))

    def issue_w2(e):
        lw2d[e] = b.dma("gpsimd", "dg", w2sb.rearrange("p (c n) -> p c n", c=16),
                        w2d[e].rearrange("(c p) n -> p c n", p=128))

    b.wait("gpsimd", "t", b.counts["t"])
    b.wait("gpsimd", "v", b.counts["v"])
    issue_w1(0)
    issue_w2(0)
    for e in range(16):
        b.wait("tensor", "v", b.counts["v"])
        for sl in range(4):
            n0 = sl * 512
            def fbc(E, e=e, sl=sl, n0=n0):
                return E.matmul(banks[4 + sl][:, 0:512], eyeb16[0:16, e * 128:(e + 1) * 128],
                                cTb_f[0:16, n0:n0 + 512], start=True, stop=True)
            tb = b.op("tensor", fbc, inc="t" if sl == 3 else None)
        b.wait("vector", "t", tb)
        vv = b.op("vector", lambda E: E.tensor_copy(cbc[:, 0:2048], bigap[:, 2048:4096]),
                  inc="v")
        b.wait("tensor", "v", vv)
        b.wait("tensor", "dg", lw2d[e])
        for sl in range(4):
            n0 = sl * 512
            b.wait("vector", "t", b.counts["t"])
            ev_war = {}
            for ic in range(16):
                bk = banks[ic % 2]
                if (ic % 2) in ev_war:
                    b.wait("tensor", "s", ev_war[ic % 2])
                for kc in range(8):
                    def fmm(E, e=e, sl=sl, ic=ic, kc=kc, n0=n0, bk=bk):
                        return E.matmul(bk[:, 0:512],
                                        w1sb[:, kc * 2048 + ic * 128:kc * 2048 + (ic + 1) * 128],
                                        xln[:, kc * TOK + n0:kc * TOK + n0 + 512],
                                        start=(kc == 0), stop=(kc == 7))
                    tmm = b.op("tensor", fmm, inc="t" if kc == 7 else None)
                b.wait("scalar", "t", tmm)

                def fge(E, e=e, ic=ic, bk=bk):
                    return E.activation(gtmp, bk[:, 0:512], AF.Gelu,
                                        bias=b1sb[:, ic * 16 + e:ic * 16 + e + 1])
                sge = b.op("scalar", fge, inc="s")
                ev_war[ic % 2] = sge
                b.wait("vector", "s", sge)

                def fmu(E, ic=ic, n0=n0):
                    return E.tensor_mul(hm[:, ic * 512:(ic + 1) * 512], gtmp,
                                        cbc[:, n0:n0 + 512])
                vmu = b.op("vector", fmu, inc="v")
                b.wait("scalar", "v", vmu)
            if sl == 3 and e < 15:
                b.wait("gpsimd", "t", b.counts["t"])
                issue_w1(e + 1)
            vhm = b.counts["v"]
            b.wait("tensor", "v", vhm)
            ev2_war = {}
            for oc in range(8):
                bk = banks[2 + oc % 2]
                if (oc % 2) in ev2_war:
                    b.wait("tensor", "v", ev2_war[oc % 2])
                for ic in range(16):
                    def fm2(E, e=e, sl=sl, oc=oc, ic=ic, n0=n0, bk=bk):
                        return E.matmul(bk[:, 0:512],
                                        w2sb[:, ic * 1024 + oc * 128:ic * 1024 + (oc + 1) * 128],
                                        hm[:, ic * 512:(ic + 1) * 512],
                                        start=(ic == 0), stop=(e > 0 and ic == 15))
                    tm2 = b.op("tensor", fm2,
                               inc="t" if (e > 0 and ic == 15) else None)
                if e == 0:
                    def fb2(E, oc=oc, n0=n0, bk=bk):
                        return E.matmul(bk[:, 0:512], b2sb[0:16, oc * 128:(oc + 1) * 128],
                                        cTb_f[0:16, n0:n0 + 512], start=False, stop=True)
                    tm2 = b.op("tensor", fb2, inc="t")
                b.wait("vector", "t", tm2)

                def fad(E, oc=oc, n0=n0, bk=bk):
                    xs = x[:, oc * TOK + n0:oc * TOK + n0 + 512]
                    return E.tensor_add(xs, xs, bk[:, 0:512])
                ev2_war[oc % 2] = b.op("vector", fad, inc="v")
            if sl == 3 and e < 15:
                b.wait("gpsimd", "t", b.counts["t"])
                issue_w2(e + 1)
    b.barrier()

    vln = do_ln()
    b.wait("tensor", "v", vln)
    w_l1 = [a2b[:, 0:8192], a2b[:, 8192:16384]]
    dsts = [q_s, kv_loc[0:D, :], kv_loc[D:2 * D, :]]
    wsrcs = [qwd, kwd, vwd]
    wdma = [0, 0]
    for wi in range(3):
        wb = w_l1[wi % 2]
        b.wait("gpsimd", "t", b.counts["t"])
        wdma[wi % 2] = b.dma("gpsimd", "dg", wb.rearrange("p (c n) -> p c n", c=8),
                             wsrcs[wi].rearrange("(c p) n -> p c n", p=128))
        b.wait("tensor", "dg", wdma[wi % 2])
        scale = 0.125 if wi == 0 else 1.0
        state = {}
        stg_war = [0, 0]
        for oc in range(8):
            sg = stg[oc % 2]
            b.wait("scalar", "st", stg_war[oc % 2])
            for sl in range(4):
                n0 = sl * 512
                bid = sl % 2
                bk = banks[bid]
                if bid in state:
                    b.wait("tensor", "s", state[bid])
                for kc in range(8):
                    def fmm(E, oc=oc, kc=kc, n0=n0, bk=bk, wb=wb):
                        return E.matmul(bk[:, 0:512],
                                        wb[:, kc * 1024 + oc * 128:kc * 1024 + (oc + 1) * 128],
                                        xln[:, kc * TOK + n0:kc * TOK + n0 + 512],
                                        start=(kc == 0), stop=(kc == 7))
                    tmm = b.op("tensor", fmm, inc="t" if kc == 7 else None)
                b.wait("scalar", "t", tmm)

                def fev(E, n0=n0, bk=bk, sg=sg, scale=scale):
                    return E.activation(sg[:, n0:n0 + 512], bk[:, 0:512], AF.Identity,
                                        scale=scale)
                state[bid] = b.op("scalar", fev, inc="s")
            b.wait("sync", "s", b.counts["s"])
            stg_war[oc % 2] = b.dma("sync", "st", dsts[wi][oc * 128:(oc + 1) * 128, :], sg)
    b.wait("gpsimd", "st", b.counts["st"])
    s_cc, cc_tot = b.inc("cc", 1)
    b.emit("gpsimd", lambda E: E.collective_compute(
        "AllGather", ALU.bypass, replica_groups=[[0, 1], [2, 3], [4, 5], [6, 7]],
        ins=[kv_loc.opt()], outs=[kv_gath.opt()]).then_inc(s_cc))
    b.wait("sync", "cc", cc_tot)
    for half, ch in enumerate((CH_A, CH_B)):
        for li, gc in enumerate(ch):
            roff = half * 2048
            b.dma("sync", "st", kg_s[:, 256 * gc:256 * gc + 256],
                  kv_gath[roff:roff + D, 256 * li:256 * li + 256])
            b.dma("sync", "st", vg_s[:, 256 * gc:256 * gc + 256],
                  kv_gath[roff + D:roff + 2 * D, 256 * li:256 * li + 256])
    relay = b.counts["st"]
    b.barrier()
    b.wait("sync", "st", relay)

    O = a1b[:, 14976:31360]
    lbufs = {
        "Pb": Pb6, "sink": Pb6[0], "mz": mz, "bpack": bpack,
        "m1": a2b[:, 4096:10240],
        "m2": a2b[:, 10240:16384],
        "idb": idb.ap(), "bts": a2b[0:3, 16384:18432], "O": O, "banks": banks,
        "pT": bigap.bitcast(BF16)[:, 7168:8192], "big": bigap,
        "sink4k": a2b[:, 20480:24576],
    }
    Qe = a2b[:, 0:2048]
    Ke = a1b[:, 0:4096]
    Vh = a1b[0:64, 4096:8192]
    Vtok = a2b[:, 2048:4096]
    ldm1 = b.dma("sync", "ds", lbufs["m1"], m1d)
    ldm2 = b.dma("sync", "ds", lbufs["m2"], m2d)
    b.dma("sync", "ds", Qe[64:67, :], ones3d[:, 0:TOK])
    ldst = b.dma("sync", "ds", Ke[67:70, :], ones3d)
    for e in ("tensor", "vector", "scalar"):
        b.wait(e, "ds", ldst)
    for h in range(16):
        b.wait("sync", "t", b.counts["t"])
        b.wait("sync", "s", b.counts["s"])
        b.dma("sync", "ds", Qe[0:64, :], q_s[64 * h:64 * h + 64, :])
        b.dma("sync", "ds", Ke[0:64, :], kg_s[64 * h:64 * h + 64, :])
        b.dma("sync", "ds", Ke[64:67, :], al3d[h])
        hd = b.dma("sync", "ds", Vh[0:64, :], vg_s[64 * h:64 * h + 64, :])
        for e in ("tensor", "vector", "scalar"):
            b.wait(e, "ds", hd)
        emit_l1_attn_head(b, h, Qe, Ke, Vh, Vtok, lbufs, h == 0)
    vO = b.counts["v"]
    b.wait("gpsimd", "v", vO)
    b.wait("gpsimd", "t", b.counts["t"])
    ow_sb = a2b[:, 4096:12288]
    ldow = b.dma("gpsimd", "dg", ow_sb.rearrange("p (c n) -> p c n", c=8),
                 owd.rearrange("(c p) n -> p c n", p=128))
    b.wait("tensor", "v", vO)
    b.wait("tensor", "dg", ldow)
    ev_war = {}
    for oc in range(8):
        for sl in range(4):
            n0 = sl * 512
            bid = 4 + (oc * 4 + sl) % 2
            bk = banks[bid]
            if bid in ev_war:
                b.wait("tensor", "v", ev_war[bid])
            for kc in range(8):
                def fmm(E, oc=oc, kc=kc, n0=n0, bk=bk):
                    return E.matmul(bk[:, 0:512],
                                    ow_sb[:, kc * 1024 + oc * 128:kc * 1024 + (oc + 1) * 128],
                                    O[:, kc * TOK + n0:kc * TOK + n0 + 512],
                                    start=(kc == 0), stop=(kc == 7))
                tmm = b.op("tensor", fmm, inc="t" if kc == 7 else None)
            b.wait("vector", "t", tmm)

            def fad(E, oc=oc, n0=n0, bk=bk):
                xs = x[:, oc * TOK + n0:oc * TOK + n0 + 512]
                return E.tensor_add(xs, xs, bk[:, 0:512])
            ev_war[bid] = b.op("vector", fad, inc="v")
    b.barrier()

    gates = a1f[0:32, 0:TOK]
    gtok = a1f[:, 2048:2080]
    f_et = a1f[:, 2080:2112]
    f_tsc = a1f[:, 2112:2144]
    f_tmsk = a1f[:, 2144:2176]
    gftok = a1f[:, 2176:2688]
    gbc = a1b[:, 5632:7680]
    scsc = a1b[:, 7680:9728]
    sctmp = a1b[:, 9728:10240]
    k_sb = a2b[:, 0:4096]
    v_sb = a2b[:, 4096:8192]
    selw = a2b[:, 8192:8448]
    eyeb32 = a2b[0:32, 8448:12544]
    gT = a2b[0:32, 12544:14592]

    vln = do_ln()
    b.wait("gpsimd", "v", vln)
    ldsel = b.dma("gpsimd", "dg", selw.rearrange("p (c n) -> p c n", c=8),
                  seld.rearrange("(c p) n -> p c n", p=128))
    lde32 = b.dma("gpsimd", "dg", eyeb32, eye32d)
    b.wait("tensor", "v", vln)
    b.wait("tensor", "dg", ldsel)
    for sl in range(4):
        n0 = sl * 512
        for kc in range(8):
            def fmm(E, sl=sl, kc=kc, n0=n0):
                return E.matmul(banks[0][0:32, 0:512],
                                selw[:, kc * 32:(kc + 1) * 32],
                                xln[:, kc * TOK + n0: kc * TOK + n0 + 512],
                                start=(kc == 0), stop=(kc == 7))
            tmm = b.op("tensor", fmm, inc="t" if kc == 7 else None)
        b.wait("scalar", "t", tmm)

        def fev(E, n0=n0):
            return E.activation(gates[:, n0:n0 + 512], banks[0][0:32, 0:512], AF.Sigmoid)
        sev = b.op("scalar", fev, inc="s")
        b.wait("tensor", "s", sev)
    for tt in range(16):
        def ftr(E, tt=tt):
            return E.transpose(banks[1][0:128, 0:32], gates[:, tt * 128:(tt + 1) * 128],
                               idf[0:32, 0:32])
        ttr = b.op("tensor", ftr, inc="t")
        b.wait("vector", "t", ttr)
        vv = b.op("vector", lambda E, tt=tt: E.tensor_copy(gtok, banks[1][0:128, 0:32]),
                  inc="v")
        b.wait("tensor", "v", vv)

        def vstep(fn):
            t = b.op("vector", fn, inc="v")
            b.wait("vector", "v", t)
            return t

        vstep(lambda E: E.tensor_copy(f_tsc, gtok))
        for r_ in range(3):
            vstep(lambda E: E.reduce_max(sml[:, 1:2], f_tsc, axis=AX))
            vstep(lambda E: E.tensor_scalar(f_tmsk, f_tsc, sml[:, 1:2], NEG_BIG,
                                            ALU.is_ge, ALU.mult))
            vstep(lambda E: E.tensor_add(f_tsc, f_tsc, f_tmsk))
        vstep(lambda E: E.reduce_max(sml[:, 1:2], f_tsc, axis=AX))
        vstep(lambda E: E.scalar_tensor_tensor(gftok[:, 0:32], gtok, sml[:, 1:2], gtok,
                                               ALU.is_ge, ALU.mult))
        vgf = b.counts["v"]
        b.wait("tensor", "v", vgf)
        def ftr2(E, tt=tt):
            return E.transpose(banks[1][0:32, 0:128], gftok[:, 0:32], idf[:])
        ttr2 = b.op("tensor", ftr2, inc="t")
        b.wait("vector", "t", ttr2)
        vv2 = b.op("vector", lambda E, tt=tt: E.tensor_copy(gT[0:32, tt * 128:(tt + 1) * 128],
                                                            banks[1][0:32, 0:128]), inc="v")
        b.wait("tensor", "v", vv2)
    b.wait("tensor", "dg", lde32)
    lkd = {}
    lvd = {}

    def issue_k(e):
        lkd[e] = b.dma("gpsimd", "dg", k_sb.rearrange("p (c n) -> p c n", c=8),
                       keyd[e].rearrange("(c p) n -> p c n", p=128))

    def issue_v(e):
        lvd[e] = b.dma("gpsimd", "dg", v_sb.rearrange("p (c n) -> p c n", c=4),
                       vald[e].rearrange("(c p) n -> p c n", p=128))

    b.wait("gpsimd", "t", b.counts["t"])
    b.wait("gpsimd", "v", b.counts["v"])
    issue_k(0)
    issue_v(0)
    for e in range(32):
        b.wait("tensor", "v", b.counts["v"])
        for sl in range(4):
            n0 = sl * 512
            def fbc(E, e=e, sl=sl, n0=n0):
                return E.matmul(banks[4 + sl][:, 0:512], eyeb32[0:32, e * 128:(e + 1) * 128],
                                gT[0:32, n0:n0 + 512], start=True, stop=True)
            tb = b.op("tensor", fbc, inc="t" if sl == 3 else None)
        b.wait("vector", "t", tb)
        vv = b.op("vector", lambda E: E.tensor_copy(gbc[:, 0:2048], bigap[:, 2048:4096]),
                  inc="v")
        b.wait("tensor", "v", vv)
        b.wait("tensor", "dg", lvd[e])
        for sl in range(4):
            n0 = sl * 512
            b.wait("vector", "t", b.counts["t"])
            ev_war = {}
            for ic in range(4):
                bk = banks[ic % 2]
                if (ic % 2) in ev_war:
                    b.wait("tensor", "s", ev_war[ic % 2])
                for kc in range(8):
                    def fmm(E, e=e, sl=sl, ic=ic, kc=kc, n0=n0, bk=bk):
                        return E.matmul(bk[:, 0:512],
                                        k_sb[:, kc * 512 + ic * 128:kc * 512 + (ic + 1) * 128],
                                        xln[:, kc * TOK + n0:kc * TOK + n0 + 512],
                                        start=(kc == 0), stop=(kc == 7))
                    tmm = b.op("tensor", fmm, inc="t" if kc == 7 else None)
                b.wait("scalar", "t", tmm)

                def fre(E, ic=ic, bk=bk):
                    return E.activation(sctmp, bk[:, 0:512], AF.Relu)
                sre = b.op("scalar", fre, inc="s")
                ev_war[ic % 2] = sre
                b.wait("vector", "s", sre)

                def fmu(E, ic=ic, n0=n0):
                    return E.tensor_mul(scsc[:, ic * 512:(ic + 1) * 512], sctmp,
                                        gbc[:, n0:n0 + 512])
                vmu = b.op("vector", fmu, inc="v")
                b.wait("scalar", "v", vmu)
            if sl == 3 and e < 31:
                b.wait("gpsimd", "t", b.counts["t"])
                issue_k(e + 1)
            vhm = b.counts["v"]
            b.wait("tensor", "v", vhm)
            ev2_war = {}
            for oc in range(8):
                bk = banks[2 + oc % 2]
                if (oc % 2) in ev2_war:
                    b.wait("tensor", "v", ev2_war[oc % 2])
                for ic in range(4):
                    def fm2(E, oc=oc, ic=ic, bk=bk):
                        return E.matmul(bk[:, 0:512],
                                        v_sb[:, ic * 1024 + oc * 128:ic * 1024 + (oc + 1) * 128],
                                        scsc[:, ic * 512:(ic + 1) * 512],
                                        start=(ic == 0), stop=(ic == 3))
                    tm2 = b.op("tensor", fm2, inc="t" if ic == 3 else None)
                b.wait("vector", "t", tm2)

                def fad(E, oc=oc, n0=n0, bk=bk):
                    xs = x[:, oc * TOK + n0:oc * TOK + n0 + 512]
                    return E.tensor_add(xs, xs, bk[:, 0:512])
                ev2_war[oc % 2] = b.op("vector", fad, inc="v")
            if sl == 3 and e < 31:
                b.wait("gpsimd", "t", b.counts["t"])
                issue_v(e + 1)
    vfin = b.counts["v"]
    b.wait("sync", "v", vfin)
    st = b.dma("sync", "st", xout.rearrange("(c p) n -> p c n", p=128),
               x[:].rearrange("p (c n) -> p c n", c=8))
    b.wait("sync", "st", st)
    return b.finalize()


def kernel_fused(**inputs):
    cst = _host_consts()
    cores = list(range(8))
    shards = shard_tokens(np.asarray(inputs["L0_attn_inputs"], dtype=np.float32))
    b1t = np.ascontiguousarray(np.asarray(inputs["L0_ffn_b1"]).T)
    selt = np.ascontiguousarray(np.asarray(inputs["L1_ffn_expert_sel"]).T)
    nc = build_fused()
    in_maps = []
    for ci in range(8):
        is_a = (ci % 2 == 0)
        in_maps.append({
            "xin": shards[ci],
            "wqkv": np.asarray(inputs["L0_attn_qkv_kernel"]),
            "bqkv": np.asarray(inputs["L0_attn_qkv_bias"]),
            "rw": np.asarray(inputs["L0_ffn_router_w"]),
            "rb_in": np.asarray(inputs["L0_ffn_router_b"]).reshape(16, 1),
            "w1": np.asarray(inputs["L0_ffn_w1"]), "b1t": b1t,
            "w2": np.asarray(inputs["L0_ffn_w2"]), "b2": np.asarray(inputs["L0_ffn_b2"]),
            "qw": np.asarray(inputs["L1_attn_q_w"]), "kw": np.asarray(inputs["L1_attn_k_w"]),
            "vw": np.asarray(inputs["L1_attn_v_w"]), "ow": np.asarray(inputs["L1_attn_o_w"]),
            "selt": selt, "keys": np.asarray(inputs["L1_ffn_keys"]),
            "values": np.asarray(inputs["L1_ffn_values"]),
            "tri1": cst["tri1"], "tri2": cst["tri2"], "ident": cst["ident"],
            "idf": cst["idf"], "eyeb16": cst["eyeb16"], "eyeb32": cst["eyeb32"],
            "al3": cst["al3"], "ones3": cst["ones3"],
            "m1": cst["m1"][is_a], "m2": cst["m2"][is_a],
        })
    res = run_bass_kernel_spmd(nc, in_maps, core_ids=cores)
    return unshard_tokens([r["xout"] for r in res.results], np.float32)


def kernel(**inputs):
    return kernel_fused(**inputs)


# version 05 (final)
# speedup vs baseline: 1.1445x; 1.0718x over previous
import collections
import contextlib

import numpy as np
import ml_dtypes

import concourse.bass as bass
import concourse.mybir as mybir
from concourse.bass_utils import run_bass_kernel_spmd

F32 = mybir.dt.float32
BF16 = mybir.dt.bfloat16
AX = mybir.AxisListType.X
AF = mybir.ActivationFunctionType
ALU = mybir.AluOpType
BF16NP = ml_dtypes.bfloat16

B_SZ = 4
S_LEN = 4096
D = 1024
CHUNK = 256
TOK = 2048
CH_A = [0, 3, 4, 7, 8, 11, 12, 15]
CH_B = [1, 2, 5, 6, 9, 10, 13, 14]
NEG_BIG = -1.0e30


def _gq_list(is_a):
    ch = CH_A if is_a else CH_B
    return [2 * ch[l // 2] + (l % 2) for l in range(16)]


L1_KMAX = [max(a, b) + 1 for a, b in zip(_gq_list(True), _gq_list(False))]


def shard_tokens(x):
    shards = []
    for b in range(B_SZ):
        for ch in (CH_A, CH_B):
            rows = np.concatenate([x[b, 256 * c:256 * c + 256] for c in ch], axis=0)
            shards.append(np.ascontiguousarray(rows.T))
    return shards


def unshard_tokens(shards, dtype=np.float32):
    out = np.empty((B_SZ, S_LEN, D), dtype)
    for b in range(B_SZ):
        for k, ch in enumerate((CH_A, CH_B)):
            rows = np.asarray(shards[2 * b + k]).T
            for li, c in enumerate(ch):
                out[b, 256 * c:256 * c + 256] = rows[256 * li:256 * li + 256]
    return out


def host_l1_masks(is_a, transposed):
    gq = _gq_list(is_a)
    m = np.zeros((16, 3, 128, 128), np.float32)
    q = np.arange(128)
    for lq in range(16):
        for j in range(3):
            kt = L1_KMAX[lq] - 3 + j
            delta = gq[lq] - kt
            if delta >= 1:
                continue
            if delta < 0:
                m[lq, j, :, :] = NEG_BIG
            elif transposed:
                m[lq, j][q[:, None] > q[None, :]] = NEG_BIG
            else:
                m[lq, j][q[None, :] > q[:, None]] = NEG_BIG
    return m


def host_tri_masks():
    q = np.arange(128)
    tri_p1 = np.where(q[None, :] > q[:, None], NEG_BIG, 0.0).astype(np.float32)
    tri_p2 = np.where(q[:, None] > q[None, :], NEG_BIG, 0.0).astype(np.float32)
    return tri_p1, tri_p2


ENGINES = ("tensor", "vector", "scalar", "gpsimd", "sync")


class Builder:
    def __init__(self, nc):
        self.nc = nc
        self.streams = {e: [] for e in ENGINES}
        self.sems = {}
        self.counts = collections.defaultdict(int)
        self.es = contextlib.ExitStack()
        self._uid = 0

    def uid(self):
        self._uid += 1
        return self._uid

    def sem(self, name):
        if name not in self.sems:
            self.sems[name] = self.es.enter_context(self.nc.semaphore(name))
            self.counts[name] = 0
        return self.sems[name]

    def emit(self, eng, fn):
        self.streams[eng].append(fn)

    def wait(self, eng, sem_name, val):
        if val is None or val <= 0:
            return
        s = self.sem(sem_name)
        self.emit(eng, lambda E: E.wait_ge(s, val))

    def inc(self, sem_name, n=1):
        s = self.sem(sem_name)
        self.counts[sem_name] += n
        return s, self.counts[sem_name]

    def op(self, eng, fn, inc=None):
        if inc is None:
            self.emit(eng, fn)
            return None
        s, tot = self.inc(inc, 1)
        if eng in ("vector", "scalar", "gpsimd"):
            def wrapped(E):
                fn(E)
                E.drain(fusable=True).then_inc(s, 1)
            self.emit(eng, wrapped)
        else:
            self.emit(eng, lambda E: fn(E).then_inc(s, 1))
        return tot

    def dma(self, eng, sem_name, out, in_, **kw):
        s, tot = self.inc(sem_name, 16)
        self.emit(eng, lambda E: E.dma_start(out=out, in_=in_, **kw).then_inc(s, 16))
        return tot

    def sbuf(self, name, shape, dtype):
        return self.es.enter_context(self.nc.sbuf_tensor(name, shape, dtype))

    def psum(self, name, shape=(128, 512), dtype=F32):
        return self.es.enter_context(self.nc.psum_tensor(name, list(shape), dtype))

    def barrier(self, name=None):
        nm = name or f"bar{self.uid()}"
        s = self.sem(nm)
        for e in ("tensor", "vector", "scalar", "gpsimd"):
            _, tot = self.inc(nm, 1)
            self.emit(e, lambda E: E.nop().then_inc(s, 1))
        tot = self.counts[nm]
        for e in ("tensor", "vector", "scalar", "gpsimd", "sync"):
            self.emit(e, lambda E: E.wait_ge(s, tot))

    def finalize(self):
        block = self.es.enter_context(self.nc.Block())
        streams = self.streams

        @block.gpsimd
        def _(E):
            for fn in streams["gpsimd"]:
                fn(E)

        @block.tensor
        def _(E):
            for fn in streams["tensor"]:
                fn(E)

        @block.vector
        def _(E):
            for fn in streams["vector"]:
                fn(E)

        @block.scalar
        def _(E):
            for fn in streams["scalar"]:
                fn(E)

        @block.sync
        def _(E):
            for fn in streams["sync"]:
                fn(E)

        self.es.close()
        return self.nc


def emit_layernorm(b, x, xln, banks, sq, rowbuf, ones_col, ones_row, rb, mb_,
                   epsb):
    R_MU, R_R, R_T, R_T2 = 0, 1, 2, 3

    def row(i, n0=0, n1=TOK):
        return rowbuf[0:1, i * TOK + n0: i * TOK + n1]

    chunk_mark = None
    for c in range(8):
        if chunk_mark is not None:
            b.wait("scalar", "t", chunk_mark)

        def fsq(E, c=c):
            return E.activation(sq[:, 0:TOK], x[:, c * TOK:(c + 1) * TOK], AF.Square)
        tot_s = b.op("scalar", fsq, inc="s")
        b.wait("tensor", "s", tot_s)
        for sl in range(4):
            n0 = sl * 512

            def fmm1(E, c=c, sl=sl, n0=n0):
                return E.matmul(banks[sl][0:1, :], ones_col[:],
                                x[:, c * TOK + n0:c * TOK + n0 + 512],
                                start=(c == 0), stop=False)

            b.op("tensor", fmm1)

            def fmm2(E, c=c, sl=sl, n0=n0):
                return E.matmul(banks[sl][32:33, :], ones_col[:], sq[:, n0:n0 + 512],
                                start=(c == 0), stop=(c == 7))

            chunk_mark = b.op("tensor", fmm2, inc="t" if sl == 3 else None)
    t_stats = b.counts["t"]
    b.wait("vector", "t", t_stats)
    for sl in range(4):
        n0, n1 = sl * 512, sl * 512 + 512

        def f1(E, sl=sl, n0=n0, n1=n1):
            return E.tensor_scalar_mul(row(R_MU, n0, n1), banks[sl][0:1, :], 1.0 / D)

        b.op("vector", f1)

        def f2(E, sl=sl, n0=n0, n1=n1):
            return E.tensor_scalar_mul(row(R_T, n0, n1), banks[sl][32:33, :], 1.0 / D)

        b.op("vector", f2)
    b.op("vector", lambda E: E.tensor_mul(row(R_T2), row(R_MU), row(R_MU)))
    tot_v = b.op("vector", lambda E: E.tensor_sub(row(R_T), row(R_T), row(R_T2)),
                 inc="v")
    b.wait("scalar", "v", tot_v)
    tot_s = b.op("scalar", lambda E: E.activation(row(R_T2), row(R_T), AF.Ln,
                                                  bias=epsb[0:1, 0:1]), inc="s")
    b.wait("scalar", "s", tot_s)
    tot_s = b.op("scalar", lambda E: E.activation(row(R_R), row(R_T2), AF.Exp, scale=-0.5),
                 inc="s")
    b.wait("vector", "s", tot_s)
    tot_v = b.op("vector", lambda E: E.tensor_mul(row(R_T), row(R_MU), row(R_R)),
                 inc="v")
    b.wait("tensor", "v", tot_v)
    for sl in range(4):
        n0 = sl * 512
        bk_r, bk_m = banks[4], banks[5]

        def f1(E, n0=n0, bk=bk_r):
            return E.matmul(bk[:, 0:512], ones_row[:], row(R_R, n0, n0 + 512),
                            start=True, stop=True)

        t1 = b.op("tensor", f1, inc="t")
        b.wait("vector", "t", t1)

        def c1(E, n0=n0, bk=bk_r):
            return E.tensor_copy(rb[:, n0:n0 + 512], bk[:, 0:512])

        b.op("vector", c1)

        def f2(E, n0=n0, bk=bk_m):
            return E.matmul(bk[:, 0:512], ones_row[:], row(R_T, n0, n0 + 512),
                            start=True, stop=True)

        t2 = b.op("tensor", f2, inc="t")
        b.wait("vector", "t", t2)

        def c2(E, n0=n0, bk=bk_m):
            return E.tensor_copy(mb_[:, n0:n0 + 512], bk[:, 0:512])

        v2 = b.op("vector", c2, inc="v")
        b.wait("tensor", "v", v2)
    for c in range(8):
        def fa(E, c=c):
            return E.tensor_mul(sq[:, 0:TOK], x[:, c * TOK:(c + 1) * TOK], rb[:, 0:TOK])

        b.op("vector", fa)

        def fb(E, c=c):
            return E.tensor_sub(xln[:, c * TOK:(c + 1) * TOK], sq[:, 0:TOK], mb_[:, 0:TOK])

        b.op("vector", fb, inc="v" if c == 7 else None)
    return b.counts["v"]


if __name__ == "__main__":
    print("import ok; L1_KMAX =", L1_KMAX)



DBG_STAGES = False
PAIRW = 2
AVB0 = 4
DBG_PB = None
DBG_AV = None
DBG_SC = None


def alloc_consts(b):
    c = {}
    c["ones_col"] = b.sbuf("c_ones_col", [128, 1], F32)
    c["ones_row"] = b.sbuf("c_ones_row", [1, 128], F32)
    c["epsb"] = b.sbuf("c_epsb", [1, 1], F32)
    b.op("vector", lambda E: E.memset(c["ones_col"][:], 1.0))
    b.op("vector", lambda E: E.memset(c["ones_row"][:], 1.0))
    b.op("vector", lambda E: E.memset(c["epsb"][:], 1e-5), inc="v")
    return c


def tok_ap(t, c, n0, n1):
    return t[:, c * TOK + n0: c * TOK + n1]


def emit_proj(b, w_sb, n_oc, n_kc, xln, n0, banks, bank_ids, evac, state):
    outw = n_oc * 128
    for oc in range(n_oc):
        bid = bank_ids[oc % len(bank_ids)]
        bk = banks[bid]
        if bid in state:
            sem_name, val = state[bid]
            b.wait("tensor", sem_name, val)
        for kc in range(n_kc):
            def fmm(E, oc=oc, kc=kc, n0=n0, bk=bk):
                return E.matmul(bk[:, 0:512],
                                w_sb[:, kc * outw + oc * 128: kc * outw + (oc + 1) * 128],
                                tok_ap(xln, kc, n0, n0 + 512),
                                start=(kc == 0), stop=(kc == n_kc - 1))
            tmm = b.op("tensor", fmm, inc="t" if kc == n_kc - 1 else None)
        state[bid] = evac(oc, n0, bk, tmm)


def emit_l0_attn_instance(b, qkv_g, icol0, x, xcol0, bufs, consts):
    Qt, Kt, Vt = bufs["Qt"], bufs["Kt"], bufs["Vt"]
    Vtok, Pb, sink = bufs["Vtok"], bufs["Pb"], bufs["sink"]
    mz, bpack, banks = bufs["mz"], bufs["bpack"], bufs["banks"]
    idb, tri1, tri2 = bufs["idb"], bufs["tri1"], bufs["tri2"]
    SCALE = 1.0 / (2.0 * D) ** 0.5
    GW = 512

    mneg = lambda q0, q1: mz[:, q0:q1]
    zall = lambda q0, q1: mz[:, 32 + q0:32 + q1]

    b.wait("tensor", "v", b.counts["v"])

    for j in range(16):
        fq, fk, fv = 192 * j, 192 * j + 64, 192 * j + 128
        for (src_f, dst, scale) in ((fq, Qt, SCALE), (fk, Kt, None), (fv, Vt, None)):
            cj, off = src_f // 128, src_f % 128
            src = qkv_g[off:off + 64, cj * GW + icol0: cj * GW + icol0 + 256]
            d = dst[0:64, j:4096:16]
            if scale is None:
                def fcp(E, src=src, d=d):
                    return E.tensor_copy(d, src)
            else:
                def fcp(E, src=src, d=d, scale=scale):
                    return E.tensor_scalar_mul(d, src, scale)
            b.op("vector", fcp)
    vbuild = b.op("vector", lambda E: E.engine_nop(), inc="v")
    b.wait("tensor", "v", vbuild)
    pT = bufs["pT"]
    big = bufs["big"]
    sink4k = bufs["sink4k"]
    vev = 0
    for g8 in range(4):
        if g8 > 0:
            b.wait("tensor", "v", vev)
        for j8 in range(8):
            kt = g8 * 8 + j8
            def ftr(E, kt=kt, j8=j8):
                return E.transpose(pT[0:128, j8 * 64:(j8 + 1) * 64],
                                   Vt[0:64, kt * 128:(kt + 1) * 128], idb[0:64, 0:64])
            ttr = b.op("tensor", ftr, inc="t" if j8 == 7 else None)
        b.wait("vector", "t", ttr)
        def fev(E, g8=g8):
            return E.tensor_copy(Vtok[:, g8 * 512:(g8 + 1) * 512], pT[0:128, 0:512])
        vev = b.op("vector", fev, inc="v")
    b.wait("tensor", "v", vev)
    sexp = 0
    for qt in range(32):
        nk = (qt + 1) * 128
        nsl = (nk + 511) // 512
        for s in range(nsl):
            k0, k1 = s * 512, min(nk, s * 512 + 512)
            def fmm(E, qt=qt, s=s, k0=k0, k1=k1):
                return E.matmul(banks[s][:, 0:k1 - k0], Qt[0:64, qt * 128:(qt + 1) * 128],
                                Kt[0:64, k0:k1], start=True, stop=True)
            tp1 = b.op("tensor", fmm, inc="t" if s == nsl - 1 else None)
        b.wait("vector", "t", tp1)
        def fmask(E, qt=qt):
            return E.tensor_add(big[:, qt * 128:(qt + 1) * 128],
                                big[:, qt * 128:(qt + 1) * 128], tri1[:])
        b.op("vector", fmask)
        def fred(E, qt=qt, nk=nk):
            return E.reduce_max(mneg(qt, qt + 1), big[:, 0:nk], axis=AX, negate=True)
        vred = b.op("vector", fred, inc="v")
        b.wait("scalar", "v", vred)
        def fexp(E, qt=qt, nk=nk):
            return E.activation(sink4k[:, 0:nk], big[:, 0:nk], AF.Exp,
                                bias=mneg(qt, qt + 1), accum_out=zall(qt, qt + 1))
        sexp = b.op("scalar", fexp, inc="s")
        b.wait("tensor", "s", sexp)
        b.wait("vector", "s", sexp)
    b.wait("vector", "v", vred)
    b.wait("scalar", "s", sexp)
    sln = b.op("scalar", lambda E: E.activation(mz[:, 96:128], mz[:, 32:64], AF.Ln), inc="s")
    b.wait("vector", "s", sln)

    def vstep(fn):
        t = b.op("vector", fn, inc="v")
        b.wait("vector", "v", t)
        return t

    vstep(lambda E: E.tensor_sub(mz[:, 64:96], mz[:, 0:32], mz[:, 96:128]))
    bp = bpack
    vstep(lambda E: E.tensor_copy(bp[:, 0:96:3], mz[:, 64:96]))
    vstep(lambda E: E.tensor_copy(mz[:, 96:128], bp[:, 0:96:3]))
    vstep(lambda E: E.tensor_sub(mz[:, 128:160], mz[:, 64:96], mz[:, 96:128]))
    vstep(lambda E: E.tensor_copy(bp[:, 1:96:3], mz[:, 128:160]))
    vstep(lambda E: E.tensor_copy(mz[:, 96:128], bp[:, 1:96:3]))
    vstep(lambda E: E.tensor_sub(mz[:, 128:160], mz[:, 128:160], mz[:, 96:128]))
    vsp = vstep(lambda E: E.tensor_copy(bp[:, 2:96:3], mz[:, 128:160]))
    b.wait("tensor", "v", vsp)
    for qt in range(32):
        def ftr(E, qt=qt):
            return E.transpose(pT[0:3, 0:128], bp[:, qt * 3:qt * 3 + 3], idb[:])
        ttr = b.op("tensor", ftr, inc="t")
        b.wait("vector", "t", ttr)
        def fev(E, qt=qt):
            return E.tensor_copy(Qt[64:67, qt * 128:(qt + 1) * 128], pT[0:3, 0:128])
        vev = b.op("vector", fev, inc="v")
        b.wait("tensor", "v", vev)
    p2b = [banks[0], banks[1], banks[2], banks[3]]
    avb = [banks[4], banks[5]]
    Pb4 = [Pb[0][:, 0:512], Pb[0][:, 512:1024], Pb[1][:, 0:512], Pb[1][:, 512:1024]]
    last_av_evac = 0
    prev_tav = prev_sexp = 0
    for g in range(8):
        av = avb[g % 2]
        if last_av_evac:
            b.wait("tensor", "v", last_av_evac)
        if g > 0:
            b.wait("scalar", "t", prev_tav)
            b.wait("tensor", "s", prev_sexp)
        ktmax = 4 * g + 3
        sexp_hist = {}
        tav_hist = {}
        for k0_ in range(0, ktmax + 1, 2):
            kts = list(range(k0_, min(k0_ + 2, ktmax + 1)))
            pi = k0_ // 2
            if pi >= 2:
                b.wait("tensor", "s", sexp_hist[pi - 2])
            tp2 = None
            for j, kt in enumerate(kts):
                co = max(0, (kt - 4 * g)) * 128
                p2 = p2b[kt % 4]
                def fmm(E, g=g, kt=kt, co=co, p2=p2):
                    return E.matmul(p2[:, co:512], Kt[0:67, kt * 128:(kt + 1) * 128],
                                    Qt[0:67, g * 512 + co:(g + 1) * 512], start=True, stop=True)
                tp2 = b.op("tensor", fmm, inc="t" if j == len(kts) - 1 else None)
            nmask = 0
            vm = None
            for j, kt in enumerate(kts):
                if kt >= 4 * g:
                    co = max(0, (kt - 4 * g)) * 128
                    p2 = p2b[kt % 4]
                    if nmask == 0:
                        b.wait("vector", "t", tp2)
                    def fmask(E, co=co, p2=p2):
                        return E.tensor_add(p2[:, co:co + 128], p2[:, co:co + 128], tri2[:])
                    vm = b.op("vector", fmask, inc="v")
                    nmask += 1
            if nmask:
                b.wait("scalar", "v", vm)
            else:
                b.wait("scalar", "t", tp2)
            if pi >= 2:
                b.wait("scalar", "t", tav_hist[pi - 2])
            for j, kt in enumerate(kts):
                co = max(0, (kt - 4 * g)) * 128
                p2 = p2b[kt % 4]
                pb = Pb4[kt % 4]
                def fexp(E, kt=kt, co=co, p2=p2, pb=pb):
                    return E.activation(pb[:, co:512], p2[:, co:512], AF.Exp)
                sexp = b.op("scalar", fexp, inc="s" if j == len(kts) - 1 else None)
            sexp_hist[pi] = sexp
            b.wait("tensor", "s", sexp)
            for j, kt in enumerate(kts):
                co = max(0, (kt - 4 * g)) * 128
                pb = Pb4[kt % 4]
                def fav(E, g=g, kt=kt, co=co, pb=pb, av=av):
                    return E.matmul(av[0:64, co:512], Vtok[:, kt * 64:(kt + 1) * 64],
                                    pb[:, co:512], start=(kt == 0), stop=(kt == ktmax))
                tav_hist[pi] = b.op("tensor", fav, inc="t")
        tav = b.counts["t"]
        prev_tav, prev_sexp = tav, sexp_hist[(ktmax) // 2]
        b.wait("vector", "t", tav)
        for j in range(16):
            cj, off = (64 * j) // 128, (64 * j) % 128
            xs = x[off:off + 64, cj * TOK + xcol0 + 32 * g: cj * TOK + xcol0 + 32 * g + 32]
            def fadd(E, j=j, xs=xs, av=av):
                return E.tensor_add(xs, xs, av[0:64, j:512:16])
            last_av_evac = b.op("vector", fadd, inc="v" if j == 15 else None)
    return last_av_evac


def build_l0_qkv():
    nc = bass.Bass("TRN2", target_bir_lowering=False)
    xin = nc.dram_tensor("xin", [D, TOK], F32, kind="ExternalInput").ap()
    wq = nc.dram_tensor("wq", [D, 3072], F32, kind="ExternalInput").ap()
    bq = nc.dram_tensor("bq", [1, 3072], F32, kind="ExternalInput").ap()
    qkvout = nc.dram_tensor("qkvout", [3072, TOK], BF16, kind="ExternalOutput").ap()

    b = Builder(nc)
    x = b.sbuf("x", [128, 8 * TOK], F32)
    xln = b.sbuf("xln", [128, 8 * TOK], BF16)
    w_sb = b.sbuf("w_sb", [128, 8 * 3072], BF16)
    bias_sb = b.sbuf("bias_sb", [128, 24], F32)
    sq = b.sbuf("sq", [128, TOK], F32)
    rowbuf = b.sbuf("rowbuf", [1, 4 * TOK], F32)
    rb = b.sbuf("rb", [128, TOK], F32)
    mb_ = b.sbuf("mb", [128, TOK], F32)
    sq_b = sq.ap().bitcast(BF16)
    stg = [sq_b[:, 0:TOK], sq_b[:, TOK:2 * TOK]]
    banks = [b.psum(f"bk{i}") for i in range(8)]
    c = alloc_consts(b)

    ldx = b.dma("sync", "ds", x[:].rearrange("p (c n) -> p c n", c=8),
                xin.rearrange("(c p) n -> p c n", p=128))
    ldw = b.dma("gpsimd", "dg", w_sb[:].rearrange("p (c n) -> p c n", c=8),
                wq.rearrange("(c p) n -> p c n", p=128))
    ldb = b.dma("gpsimd", "dg", bias_sb[:], bq.rearrange("o (c p) -> p (o c)", p=128),
                allow_slow_non_contiguous=True)
    for e in ("scalar", "tensor", "vector"):
        b.wait(e, "ds", ldx)
        b.wait(e, "v", 1)
    vln = emit_layernorm(b, x, xln, banks, sq, rowbuf, c["ones_col"], c["ones_row"],
                         rb, mb_, c["epsb"])
    b.wait("tensor", "v", vln)
    b.wait("tensor", "dg", ldw)
    b.wait("scalar", "dg", ldb)
    state = {}
    stg_war = [0, 0]
    for oc in range(24):
        sg = stg[oc % 2]
        b.wait("scalar", "st", stg_war[oc % 2])
        for sl in range(4):
            n0 = sl * 512
            bid = sl % 2
            bk = banks[bid]
            if bid in state:
                b.wait("tensor", "s", state[bid])
            for kc in range(8):
                def fmm(E, oc=oc, kc=kc, n0=n0, bk=bk):
                    return E.matmul(bk[:, 0:512],
                                    w_sb[:, kc * 3072 + oc * 128: kc * 3072 + (oc + 1) * 128],
                                    xln[:, kc * TOK + n0: kc * TOK + n0 + 512],
                                    start=(kc == 0), stop=(kc == 7))
                tmm = b.op("tensor", fmm, inc="t" if kc == 7 else None)
            b.wait("scalar", "t", tmm)

            def fev(E, oc=oc, n0=n0, bk=bk, sg=sg):
                return E.activation(sg[:, n0:n0 + 512], bk[:, 0:512], AF.Identity,
                                    bias=bias_sb[:, oc:oc + 1])
            state[bid] = b.op("scalar", fev, inc="s")
        b.wait("sync", "s", b.counts["s"])
        stg_war[oc % 2] = b.dma("sync", "st", qkvout[oc * 128:(oc + 1) * 128, :], sg)
    b.wait("sync", "st", b.counts["st"])
    return b.finalize()


def build_l0_attn():
    nc = bass.Bass("TRN2", target_bir_lowering=False)
    xin = nc.dram_tensor("xin", [D, TOK], F32, kind="ExternalInput").ap()
    qkv = nc.dram_tensor("qkv", [3072, TOK], BF16, kind="ExternalInput").ap()
    tri1d = nc.dram_tensor("tri1", [128, 128], F32, kind="ExternalInput").ap()
    tri2d = nc.dram_tensor("tri2", [128, 128], F32, kind="ExternalInput").ap()
    identd = nc.dram_tensor("ident", [128, 128], BF16, kind="ExternalInput").ap()
    xout = nc.dram_tensor("xout", [D, TOK], F32, kind="ExternalOutput").ap()

    b = Builder(nc)
    x = b.sbuf("x", [128, 8 * TOK], F32)
    qkv_g = [b.sbuf(f"qkv_g{i}", [128, 24 * 512], BF16) for i in range(2)]
    bufs = {
        "Qt": b.sbuf("Qt", [128, 4096], BF16),
        "Kt": b.sbuf("Kt", [128, 4096], BF16),
        "Vt": b.sbuf("Vt", [64, 4096], BF16),
        "Vtok": b.sbuf("Vtok", [128, 2048], BF16),
        "Pb": [b.sbuf(f"Pb{i}", [128, 512], BF16) for i in range(2)],
        "sink": b.sbuf("sink", [128, 512], BF16),
        "mz": b.sbuf("mz", [128, 160], F32),
        "bpack": b.sbuf("bpack", [128, 96], BF16),
        "tri1": b.sbuf("tri1_sb", [128, 128], F32),
        "tri2": b.sbuf("tri2_sb", [128, 128], F32),
        "idb": b.sbuf("idb_sb", [128, 128], BF16),
    }
    banks = [b.psum(f"bk{i}") for i in range(8)]
    bufs["banks"] = banks

    ldx = b.dma("sync", "ds", x[:].rearrange("p (c n) -> p c n", c=8),
                xin.rearrange("(c p) n -> p c n", p=128))
    b.dma("sync", "ds", bufs["tri1"][:], tri1d)
    b.dma("sync", "ds", bufs["tri2"][:], tri2d)
    ldc = b.dma("sync", "ds", bufs["idb"][:], identd)
    b.op("vector", lambda E: E.memset(bufs["Kt"][64:67, :], 1.0), inc="v")
    for e in ("scalar", "tensor", "vector"):
        b.wait(e, "ds", ldc)
    b.wait("tensor", "v", 1)

    qkv_r = qkv.rearrange("(c p) n -> p c n", p=128)
    prev_g_dma = [0, 0]
    for g in range(4):
        gb = qkv_g[g % 2]
        b.wait("sync", "v", b.counts["v"])
        ldg = b.dma("sync", "ds", gb[:].rearrange("p (c n) -> p c n", c=24),
                    qkv_r[:, :, g * 512:(g + 1) * 512])
        b.wait("vector", "ds", ldg)
        for i in range(2):
            li = 2 * g + i
            emit_l0_attn_instance(b, gb, i * 256, x, li * 256, bufs, None)
    vfin = b.counts["v"]
    b.wait("sync", "v", vfin)
    st = b.dma("sync", "st", xout.rearrange("(c p) n -> p c n", p=128),
               x[:].rearrange("p (c n) -> p c n", c=8))
    b.wait("sync", "st", st)
    return b.finalize()


def build_l0_ffn():
    nc = bass.Bass("TRN2", target_bir_lowering=False)
    xin = nc.dram_tensor("xin", [D, TOK], F32, kind="ExternalInput").ap()
    rwd = nc.dram_tensor("rw", [D, 16], F32, kind="ExternalInput").ap()
    rbd = nc.dram_tensor("rb_in", [16, 1], F32, kind="ExternalInput").ap()
    w1d = nc.dram_tensor("w1", [16, D, 2048], F32, kind="ExternalInput").ap()
    b1d = nc.dram_tensor("b1t", [2048, 16], F32, kind="ExternalInput").ap()
    w2d = nc.dram_tensor("w2", [16, 2048, D], F32, kind="ExternalInput").ap()
    b2d = nc.dram_tensor("b2", [16, D], F32, kind="ExternalInput").ap()
    idfd = nc.dram_tensor("idf", [128, 128], F32, kind="ExternalInput").ap()
    eyed = nc.dram_tensor("eyeb", [16, 16 * 128], BF16, kind="ExternalInput").ap()
    xout = nc.dram_tensor("xout", [D, TOK], F32, kind="ExternalOutput").ap()

    b = Builder(nc)
    x = b.sbuf("x", [128, 8 * TOK], F32)
    xln = b.sbuf("xln", [128, 8 * TOK], BF16)
    arA = b.sbuf("arA", [128, 2 * 4096], F32)
    rowbuf = arA.ap()[0:1, 0:4 * TOK]
    w2sb = arA.ap().bitcast(BF16)
    arB = b.sbuf("arB", [128, 6144], F32)
    sq = arB.ap()[:, 0:TOK]
    rb = arB.ap()[:, TOK:2 * TOK]
    mb_ = arB.ap()[:, 2 * TOK:3 * TOK]
    hm = arB.ap().bitcast(BF16)[:, 0:16 * 512]
    gtmp = arB.ap().bitcast(BF16)[:, 8192:8704]
    cbc = arB.ap().bitcast(BF16)[:, 8704:10752]
    w1sb = b.sbuf("w1sb", [128, 8 * 2048], BF16)
    rwsb = b.sbuf("rwsb", [128, 8 * 16], BF16)
    rbsb = b.sbuf("rbsb", [16, 1], F32)
    b1sb = b.sbuf("b1sb", [128, 256], F32)
    b2sb = b.sbuf("b2sb", [16, 1024], BF16)
    logits = arB.ap()[0:16, 0:TOK]
    ltok = arB.ap()[:, 2048:2304]
    et = arB.ap()[:, 2304:2320]
    tsc = arB.ap()[:, 2320:2336]
    tmsk = arB.ap()[:, 2336:2352]
    ctok = arB.ap()[:, 2352:2608]
    cTb = b.sbuf("cTb", [16, TOK], BF16)
    idf = b.sbuf("idf_sb", [128, 128], F32)
    eyeb = b.sbuf("eyeb_sb", [16, 16 * 128], BF16)
    sml = b.sbuf("sml", [128, 8], F32)
    banks = [b.psum(f"bk{i}") for i in range(8)]
    c = alloc_consts(b)

    ldx = b.dma("sync", "ds", x[:].rearrange("p (c n) -> p c n", c=8),
                xin.rearrange("(c p) n -> p c n", p=128))
    b.dma("sync", "ds", idf[:], idfd)
    b.dma("sync", "ds", eyeb[:], eyed)
    b.dma("sync", "ds", rbsb[:], rbd)
    ldc = b.dma("gpsimd", "dg", rwsb[:].rearrange("p (c n) -> p c n", c=8),
                rwd.rearrange("(c p) n -> p c n", p=128))
    b.dma("gpsimd", "dg", b1sb[:].rearrange("p (c e) -> p c e", c=16),
          b1d.rearrange("(c p) e -> p c e", p=128))
    ldg = b.dma("gpsimd", "dg", b2sb[:], b2d)
    for e in ("scalar", "tensor", "vector"):
        b.wait(e, "ds", ldx)
        b.wait(e, "v", 1)
    vln = emit_layernorm(b, x, xln, banks, sq, rowbuf, c["ones_col"], c["ones_row"],
                         rb, mb_, c["epsb"])
    b.wait("tensor", "v", vln)
    b.wait("tensor", "dg", ldc)
    b.wait("tensor", "ds", b.counts["ds"])

    for sl in range(4):
        n0 = sl * 512
        for kc in range(8):
            def fmm(E, sl=sl, kc=kc, n0=n0):
                return E.matmul(banks[0][0:16, 0:512],
                                rwsb[:, kc * 16:(kc + 1) * 16],
                                xln[:, kc * TOK + n0: kc * TOK + n0 + 512],
                                start=(kc == 0), stop=(kc == 7))
            tmm = b.op("tensor", fmm, inc="t" if kc == 7 else None)
        b.wait("scalar", "t", tmm)
        b.wait("scalar", "dg", b.counts["dg"])

        def fev(E, n0=n0):
            return E.activation(logits[0:16, n0:n0 + 512], banks[0][0:16, 0:512],
                                AF.Identity, bias=rbsb[:])
        sev = b.op("scalar", fev, inc="s")
        b.wait("tensor", "s", sev)
    for tt in range(16):
        def ftr(E, tt=tt):
            return E.transpose(banks[1][0:128, 0:16], logits[0:16, tt * 128:(tt + 1) * 128],
                               idf[0:16, 0:16])
        ttr = b.op("tensor", ftr, inc="t")
        b.wait("vector", "t", ttr)
        vv = b.op("vector", lambda E, tt=tt: E.tensor_copy(ltok[:, tt * 16:(tt + 1) * 16],
                                                           banks[1][0:128, 0:16]), inc="v")
        b.wait("tensor", "v", vv)
    for tt in range(16):
        lt = ltok[:, tt * 16:(tt + 1) * 16]

        def vstep(fn):
            t = b.op("vector", fn, inc="v")
            b.wait("vector", "v", t)
            return t

        vstep(lambda E, lt=lt: E.reduce_max(sml[:, 0:1], lt, axis=AX, negate=True))
        vm = b.counts["v"]
        b.wait("scalar", "v", vm)
        sx = b.op("scalar", lambda E, lt=lt: E.activation(et, lt, AF.Exp, bias=sml[:, 0:1]),
                  inc="s")
        b.wait("vector", "s", sx)
        vstep(lambda E: E.tensor_copy(tsc, et))
        for r_ in range(3):
            vstep(lambda E: E.reduce_max(sml[:, 1:2], tsc, axis=AX))
            vstep(lambda E: E.tensor_scalar(tmsk, tsc, sml[:, 1:2], NEG_BIG,
                                            ALU.is_ge, ALU.mult))
            vstep(lambda E: E.tensor_add(tsc, tsc, tmsk))
        vstep(lambda E: E.reduce_max(sml[:, 1:2], tsc, axis=AX))
        vstep(lambda E: E.scalar_tensor_tensor(tmsk, et, sml[:, 1:2], et,
                                               ALU.is_ge, ALU.mult, accum_out=sml[:, 2:3]))
        vstep(lambda E: E.reciprocal(sml[:, 3:4], sml[:, 2:3]))
        vstep(lambda E, tt=tt: E.tensor_scalar_mul(ctok[:, tt * 16:(tt + 1) * 16],
                                                   tmsk, sml[:, 3:4]))
    vct = b.counts["v"]
    b.wait("tensor", "v", vct)
    for tt in range(16):
        def ftr(E, tt=tt):
            return E.transpose(banks[1][0:16, 0:128], ctok[:, tt * 16:(tt + 1) * 16], idf[:])
        ttr = b.op("tensor", ftr, inc="t")
        b.wait("vector", "t", ttr)
        vv = b.op("vector", lambda E, tt=tt: E.tensor_copy(cTb[0:16, tt * 128:(tt + 1) * 128],
                                                           banks[1][0:16, 0:128]), inc="v")
        b.wait("tensor", "v", vv)

    for e in range(16):
        b.wait("gpsimd", "t", b.counts["t"])
        lw1 = b.dma("gpsimd", "dg",
                    w1sb[:].rearrange("p (c n) -> p c n", c=8),
                    w1d[e].rearrange("(c p) n -> p c n", p=128))
        lw2 = b.dma("gpsimd", "dg",
                    w2sb[:].rearrange("p (c n) -> p c n", c=16),
                    w2d[e].rearrange("(c p) n -> p c n", p=128))
        b.wait("tensor", "v", b.counts["v"])
        for sl in range(4):
            n0 = sl * 512
            def fbc(E, e=e, n0=n0):
                return E.matmul(banks[4][:, 0:512], eyeb[0:16, e * 128:(e + 1) * 128],
                                cTb[0:16, n0:n0 + 512], start=True, stop=True)
            tb = b.op("tensor", fbc, inc="t")
            b.wait("vector", "t", tb)
            vv = b.op("vector", lambda E, n0=n0: E.tensor_copy(cbc[:, n0:n0 + 512],
                                                               banks[4][:, 0:512]), inc="v")
            b.wait("tensor", "v", vv)
        b.wait("tensor", "dg", lw1)
        for sl in range(4):
            n0 = sl * 512
            b.wait("vector", "t", b.counts["t"])
            ev_war = {}
            for ic in range(16):
                bk = banks[ic % 2]
                if (ic % 2) in ev_war:
                    b.wait("tensor", "s", ev_war[ic % 2])
                for kc in range(8):
                    def fmm(E, e=e, sl=sl, ic=ic, kc=kc, n0=n0, bk=bk):
                        return E.matmul(bk[:, 0:512],
                                        w1sb[:, kc * 2048 + ic * 128:kc * 2048 + (ic + 1) * 128],
                                        xln[:, kc * TOK + n0:kc * TOK + n0 + 512],
                                        start=(kc == 0), stop=(kc == 7))
                    tmm = b.op("tensor", fmm, inc="t" if kc == 7 else None)
                b.wait("scalar", "t", tmm)

                def fge(E, e=e, ic=ic, bk=bk):
                    return E.activation(gtmp, bk[:, 0:512], AF.Gelu,
                                        bias=b1sb[:, ic * 16 + e:ic * 16 + e + 1])
                sge = b.op("scalar", fge, inc="s")
                ev_war[ic % 2] = sge
                b.wait("vector", "s", sge)

                def fmu(E, ic=ic, n0=n0):
                    return E.tensor_mul(hm[:, ic * 512:(ic + 1) * 512], gtmp,
                                        cbc[:, n0:n0 + 512])
                vmu = b.op("vector", fmu, inc="v")
                b.wait("scalar", "v", vmu)
            vhm = b.counts["v"]
            b.wait("tensor", "v", vhm)
            if e == 0 and sl == 0:
                b.wait("tensor", "dg", lw2)
            ev2_war = {}
            for oc in range(8):
                bk = banks[2 + oc % 2]
                if (oc % 2) in ev2_war:
                    b.wait("tensor", "v", ev2_war[oc % 2])
                for ic in range(16):
                    def fm2(E, e=e, sl=sl, oc=oc, ic=ic, n0=n0, bk=bk):
                        return E.matmul(bk[:, 0:512],
                                        w2sb[:, ic * 1024 + oc * 128:ic * 1024 + (oc + 1) * 128],
                                        hm[:, ic * 512:(ic + 1) * 512],
                                        start=(ic == 0), stop=(e > 0 and ic == 15))
                    tm2 = b.op("tensor", fm2,
                               inc="t" if (e > 0 and ic == 15) else None)
                if e == 0:
                    def fb2(E, oc=oc, n0=n0, bk=bk):
                        return E.matmul(bk[:, 0:512], b2sb[0:16, oc * 128:(oc + 1) * 128],
                                        cTb[0:16, n0:n0 + 512], start=False, stop=True)
                    tm2 = b.op("tensor", fb2, inc="t")
                b.wait("vector", "t", tm2)

                def fad(E, oc=oc, n0=n0, bk=bk):
                    xs = x[:, oc * TOK + n0:oc * TOK + n0 + 512]
                    return E.tensor_add(xs, xs, bk[:, 0:512])
                ev2_war[oc % 2] = b.op("vector", fad, inc="v")
    vfin = b.counts["v"]
    b.wait("sync", "v", vfin)
    st = b.dma("sync", "st", xout.rearrange("(c p) n -> p c n", p=128),
               x[:].rearrange("p (c n) -> p c n", c=8))
    b.wait("sync", "st", st)
    return b.finalize()


def build_l1_qkv():
    nc = bass.Bass("TRN2", target_bir_lowering=False)
    xin = nc.dram_tensor("xin", [D, TOK], F32, kind="ExternalInput").ap()
    wds = [nc.dram_tensor(n, [D, D], F32, kind="ExternalInput").ap()
           for n in ("qw", "kw", "vw")]
    outs = [nc.dram_tensor(n, [D, TOK], BF16, kind="ExternalOutput").ap()
            for n in ("qout", "kout", "vout")]

    b = Builder(nc)
    x = b.sbuf("x", [128, 8 * TOK], F32)
    xln = b.sbuf("xln", [128, 8 * TOK], BF16)
    arB = b.sbuf("arB", [128, 6144], F32)
    sq = arB.ap()[:, 0:TOK]
    rb = arB.ap()[:, TOK:2 * TOK]
    mb_ = arB.ap()[:, 2 * TOK:3 * TOK]
    stg = [arB.ap().bitcast(BF16)[:, 0:TOK], arB.ap().bitcast(BF16)[:, TOK:2 * TOK]]
    rowbuf = b.sbuf("rowbuf", [1, 4 * TOK], F32)
    w_sb = [b.sbuf(f"w_sb{i}", [128, 8 * 1024], BF16) for i in range(2)]
    banks = [b.psum(f"bk{i}") for i in range(8)]
    c = alloc_consts(b)

    ldx = b.dma("sync", "ds", x[:].rearrange("p (c n) -> p c n", c=8),
                xin.rearrange("(c p) n -> p c n", p=128))
    for e in ("scalar", "tensor", "vector"):
        b.wait(e, "ds", ldx)
        b.wait(e, "v", 1)
    vln = emit_layernorm(b, x, xln, banks, sq, rowbuf, c["ones_col"], c["ones_row"],
                         rb, mb_, c["epsb"])
    b.wait("tensor", "v", vln)
    wdma = [0, 0]
    for wi in range(3):
        wb = w_sb[wi % 2]
        b.wait("gpsimd", "t", b.counts["t"])
        wdma[wi % 2] = b.dma("gpsimd", "dg", wb[:].rearrange("p (c n) -> p c n", c=8),
                             wds[wi].rearrange("(c p) n -> p c n", p=128))
        b.wait("tensor", "dg", wdma[wi % 2])
        scale = 0.125 if wi == 0 else 1.0
        state = {}
        stg_war = [0, 0]
        for oc in range(8):
            sg = stg[oc % 2]
            b.wait("scalar", "st", stg_war[oc % 2])
            for sl in range(4):
                n0 = sl * 512
                bid = sl % 2
                bk = banks[bid]
                if bid in state:
                    b.wait("tensor", "s", state[bid])
                for kc in range(8):
                    def fmm(E, oc=oc, kc=kc, n0=n0, bk=bk, wb=wb):
                        return E.matmul(bk[:, 0:512],
                                        wb[:, kc * 1024 + oc * 128:kc * 1024 + (oc + 1) * 128],
                                        xln[:, kc * TOK + n0:kc * TOK + n0 + 512],
                                        start=(kc == 0), stop=(kc == 7))
                    tmm = b.op("tensor", fmm, inc="t" if kc == 7 else None)
                b.wait("scalar", "t", tmm)

                def fev(E, n0=n0, bk=bk, sg=sg, scale=scale):
                    return E.activation(sg[:, n0:n0 + 512], bk[:, 0:512], AF.Identity,
                                        scale=scale)
                state[bid] = b.op("scalar", fev, inc="s")
            b.wait("sync", "s", b.counts["s"])
            stg_war[oc % 2] = b.dma("sync", "st", outs[wi][oc * 128:(oc + 1) * 128, :], sg)
    b.wait("sync", "st", b.counts["st"])
    return b.finalize()


def emit_l1_attn_head(b, h, Qe, Ke, Vh, Vtok, bufs, is_first):
    Pb, sink = bufs["Pb"], bufs["sink"]
    mz, bpack, banks = bufs["mz"], bufs["bpack"], bufs["banks"]
    idb, m1, m2 = bufs["idb"], bufs["m1"], bufs["m2"]
    bts = bufs["bts"]
    O = bufs["O"]

    b.wait("tensor", "v", b.counts["v"])
    pT = bufs["pT"]
    big = bufs["big"]
    sink4k = bufs["sink4k"]
    vev = 0
    for g8 in range(4):
        if g8 > 0:
            b.wait("tensor", "v", vev)
        for j8 in range(8):
            kt = g8 * 8 + j8
            def ftr(E, kt=kt, j8=j8):
                return E.transpose(pT[0:128, j8 * 64:(j8 + 1) * 64],
                                   Vh[0:64, kt * 128:(kt + 1) * 128], idb[0:64, 0:64])
            ttr = b.op("tensor", ftr, inc="t" if j8 == 7 else None)
        b.wait("vector", "t", ttr)
        def fev(E, g8=g8):
            return E.tensor_copy(Vtok[:, g8 * 512:(g8 + 1) * 512], pT[0:128, 0:512])
        vev = b.op("vector", fev, inc="v")
    b.wait("tensor", "v", vev)
    sexp = 0
    for lq in range(16):
        km = L1_KMAX[lq]
        nk = km * 128
        nsl = (nk + 511) // 512
        for s in range(nsl):
            k0, k1 = s * 512, min(nk, s * 512 + 512)
            def fmm(E, lq=lq, s=s, k0=k0, k1=k1):
                return E.matmul(banks[s][:, 0:k1 - k0], Qe[0:67, lq * 128:(lq + 1) * 128],
                                Ke[0:67, k0:k1], start=True, stop=True)
            tp1 = b.op("tensor", fmm, inc="t" if s == nsl - 1 else None)
        b.wait("vector", "t", tp1)
        for j in range(3):
            kt = km - 3 + j
            def fmask(E, lq=lq, j=j, kt=kt):
                return E.tensor_add(big[:, kt * 128:(kt + 1) * 128],
                                    big[:, kt * 128:(kt + 1) * 128],
                                    m1[:, (lq * 3 + j) * 128:(lq * 3 + j + 1) * 128])
            b.op("vector", fmask)
        def fred(E, lq=lq, nk=nk):
            return E.reduce_max(mz[:, lq:lq + 1], big[:, 0:nk], axis=AX, negate=True)
        vred = b.op("vector", fred, inc="v")
        b.wait("scalar", "v", vred)
        def fexp(E, lq=lq, nk=nk):
            return E.activation(sink4k[:, 0:nk], big[:, 0:nk], AF.Exp,
                                bias=mz[:, lq:lq + 1], accum_out=mz[:, 32 + lq:33 + lq])
        sexp = b.op("scalar", fexp, inc="s")
        b.wait("tensor", "s", sexp)
        b.wait("vector", "s", sexp)
    b.wait("vector", "v", vred)
    b.wait("scalar", "s", sexp)
    sln = b.op("scalar", lambda E: E.activation(mz[:, 96:112], mz[:, 32:48], AF.Ln), inc="s")
    b.wait("vector", "s", sln)

    def vstep(fn):
        t = b.op("vector", fn, inc="v")
        b.wait("vector", "v", t)
        return t

    vstep(lambda E: E.tensor_sub(mz[:, 64:80], mz[:, 0:16], mz[:, 96:112]))
    bp = bpack
    vstep(lambda E: E.tensor_copy(bp[:, 0:48:3], mz[:, 64:80]))
    vstep(lambda E: E.tensor_copy(mz[:, 96:112], bp[:, 0:48:3]))
    vstep(lambda E: E.tensor_sub(mz[:, 112:128], mz[:, 64:80], mz[:, 96:112]))
    vstep(lambda E: E.tensor_copy(bp[:, 1:48:3], mz[:, 112:128]))
    vstep(lambda E: E.tensor_copy(mz[:, 96:112], bp[:, 1:48:3]))
    vstep(lambda E: E.tensor_sub(mz[:, 112:128], mz[:, 112:128], mz[:, 96:112]))
    vsp = vstep(lambda E: E.tensor_copy(bp[:, 2:48:3], mz[:, 112:128]))
    b.wait("tensor", "v", vsp)
    for lq in range(16):
        def ftr(E, lq=lq):
            return E.transpose(pT[0:3, 0:128], bp[:, lq * 3:lq * 3 + 3], idb[:])
        ttr = b.op("tensor", ftr, inc="t")
        b.wait("vector", "t", ttr)
        def fev(E, lq=lq):
            return E.tensor_copy(bts[0:3, lq * 128:(lq + 1) * 128], pT[0:3, 0:128])
        vev = b.op("vector", fev, inc="v")
        b.wait("tensor", "v", vev)
    b.wait("sync", "v", b.counts["v"])
    bdma = b.dma("sync", "ds", Qe[67:70, :], bts[0:3, :])
    b.wait("tensor", "ds", bdma)
    p2b = [banks[0], banks[1], banks[2], banks[3]]
    avb = [banks[4], banks[5]]
    Pb4 = [Pb[0][:, 0:512], Pb[0][:, 512:1024], Pb[1][:, 0:512], Pb[1][:, 512:1024]]
    last_av_evac = 0
    prev_tav = prev_sexp = 0
    for g in range(4):
        av = avb[g % 2]
        if last_av_evac:
            b.wait("tensor", "v", last_av_evac)
        if g > 0:
            b.wait("scalar", "t", prev_tav)
            b.wait("tensor", "s", prev_sexp)
        kms = [L1_KMAX[4 * g + i] for i in range(4)]
        ktmax = kms[3] - 1
        sexp_hist = {}
        tav_hist = {}

        def co_of(kt):
            nvalid = sum(1 for km in kms if kt < km)
            return (4 - nvalid) * 128

        for k0_ in range(0, ktmax + 1, 2):
            kts = list(range(k0_, min(k0_ + 2, ktmax + 1)))
            pi = k0_ // 2
            if pi >= 2:
                b.wait("tensor", "s", sexp_hist[pi - 2])
            tp2 = None
            for j, kt in enumerate(kts):
                co = co_of(kt)
                p2 = p2b[kt % 4]
                def fmm(E, g=g, kt=kt, co=co, p2=p2):
                    return E.matmul(p2[:, co:512], Ke[0:70, kt * 128:(kt + 1) * 128],
                                    Qe[0:70, g * 512 + co:(g + 1) * 512], start=True, stop=True)
                tp2 = b.op("tensor", fmm, inc="t" if j == len(kts) - 1 else None)
            nmask = 0
            vm = None
            for j, kt in enumerate(kts):
                p2 = p2b[kt % 4]
                for i in range(4):
                    lq = 4 * g + i
                    jj = kt - (kms[i] - 3)
                    if 0 <= jj < 3 and kt < kms[i]:
                        if nmask == 0:
                            b.wait("vector", "t", tp2)
                        def fmask(E, lq=lq, jj=jj, i=i, p2=p2):
                            return E.tensor_add(p2[:, i * 128:(i + 1) * 128],
                                                p2[:, i * 128:(i + 1) * 128],
                                                m2[:, (lq * 3 + jj) * 128:(lq * 3 + jj + 1) * 128])
                        vm = b.op("vector", fmask, inc="v")
                        nmask += 1
            if nmask:
                b.wait("scalar", "v", vm)
            else:
                b.wait("scalar", "t", tp2)
            if pi >= 2:
                b.wait("scalar", "t", tav_hist[pi - 2])
            for j, kt in enumerate(kts):
                co = co_of(kt)
                p2 = p2b[kt % 4]
                pb = Pb4[kt % 4]
                def fexp(E, kt=kt, co=co, p2=p2, pb=pb):
                    return E.activation(pb[:, co:512], p2[:, co:512], AF.Exp)
                sexp = b.op("scalar", fexp, inc="s" if j == len(kts) - 1 else None)
            sexp_hist[pi] = sexp
            b.wait("tensor", "s", sexp)
            for j, kt in enumerate(kts):
                co = co_of(kt)
                pb = Pb4[kt % 4]
                def fav(E, g=g, kt=kt, co=co, pb=pb, av=av):
                    return E.matmul(av[0:64, co:512], Vtok[:, kt * 64:(kt + 1) * 64],
                                    pb[:, co:512], start=(kt == 0), stop=(kt == ktmax))
                tav_hist[pi] = b.op("tensor", fav, inc="t")
        tav = b.counts["t"]
        prev_tav, prev_sexp = tav, sexp_hist[ktmax // 2]
        b.wait("vector", "t", tav)
        cj, off = (64 * h) // 128, (64 * h) % 128
        def fevac(E, g=g, cj=cj, off=off, av=av):
            return E.tensor_copy(O[off:off + 64, cj * TOK + g * 512: cj * TOK + (g + 1) * 512],
                                 av[0:64, 0:512])
        last_av_evac = b.op("vector", fevac, inc="v")
    return last_av_evac


def build_l1_attn():
    nc = bass.Bass("TRN2", target_bir_lowering=False)
    xin = nc.dram_tensor("xin", [D, TOK], F32, kind="ExternalInput").ap()
    qin = nc.dram_tensor("qin", [D, TOK], BF16, kind="ExternalInput").ap()
    kgd = nc.dram_tensor("kg", [D, S_LEN], BF16, kind="ExternalInput").ap()
    vgd = nc.dram_tensor("vg", [D, S_LEN], BF16, kind="ExternalInput").ap()
    owd = nc.dram_tensor("ow", [D, D], F32, kind="ExternalInput").ap()
    al3d = nc.dram_tensor("al3", [16, 3, S_LEN], BF16, kind="ExternalInput").ap()
    ones3d = nc.dram_tensor("ones3", [3, S_LEN], BF16, kind="ExternalInput").ap()
    m1d = nc.dram_tensor("m1", [128, 6144], BF16, kind="ExternalInput").ap()
    m2d = nc.dram_tensor("m2", [128, 6144], BF16, kind="ExternalInput").ap()
    identd = nc.dram_tensor("ident", [128, 128], BF16, kind="ExternalInput").ap()
    xout = nc.dram_tensor("xout", [D, TOK], F32, kind="ExternalOutput").ap()

    b = Builder(nc)
    x = b.sbuf("x", [128, 8 * TOK], F32)
    O = b.sbuf("O", [128, 8 * TOK], BF16)
    Qes = [b.sbuf(f"Qe{i}", [128, TOK], BF16) for i in range(2)]
    Kes = [b.sbuf(f"Ke{i}", [128, S_LEN], BF16) for i in range(2)]
    Vhs = [b.sbuf(f"Vh{i}", [64, S_LEN], BF16) for i in range(2)]
    Vtoks = [b.sbuf(f"Vtok{i}", [128, 2048], BF16) for i in range(2)]
    ow_sb = b.sbuf("ow_sb", [128, 8 * 1024], BF16)
    bufs = {
        "Pb": [b.sbuf(f"Pb{i}", [128, 512], BF16) for i in range(2)],
        "sink": b.sbuf("sink", [128, 512], BF16),
        "mz": b.sbuf("mz", [128, 128], F32),
        "bpack": b.sbuf("bpack", [128, 48], BF16),
        "bts": b.sbuf("bts", [3, TOK], BF16),
        "m1": b.sbuf("m1_sb", [128, 6144], BF16),
        "m2": b.sbuf("m2_sb", [128, 6144], BF16),
        "idb": b.sbuf("idb_sb", [128, 128], BF16),
        "O": O.ap(),
    }
    banks = [b.psum(f"bk{i}") for i in range(8)]
    bufs["banks"] = banks

    ldx = b.dma("sync", "ds", x[:].rearrange("p (c n) -> p c n", c=8),
                xin.rearrange("(c p) n -> p c n", p=128))
    b.dma("sync", "ds", bufs["m1"][:], m1d)
    b.dma("sync", "ds", bufs["m2"][:], m2d)
    b.dma("sync", "ds", bufs["idb"][:], identd)
    for i in range(2):
        b.dma("sync", "ds", Qes[i][64:67, :], ones3d[:, 0:TOK])
        b.dma("sync", "ds", Kes[i][67:70, :], ones3d)
    ldow = b.dma("gpsimd", "dg", ow_sb[:].rearrange("p (c n) -> p c n", c=8),
                 owd.rearrange("(c p) n -> p c n", p=128))
    ldc = b.counts["ds"]
    for e in ("scalar", "tensor", "vector"):
        b.wait(e, "ds", ldc)

    hdma = [0, 0]
    for h in range(16):
        sl_ = h % 2
        Qe, Ke, Vh, Vtok = Qes[sl_], Kes[sl_], Vhs[sl_], Vtoks[sl_]
        b.wait("sync", "t", b.counts["t"])
        b.dma("sync", "ds", Qe[0:64, :], qin[64 * h:64 * h + 64, :])
        b.dma("sync", "ds", Ke[0:64, :], kgd[64 * h:64 * h + 64, :])
        b.dma("sync", "ds", Ke[64:67, :], al3d[h])
        hdma[sl_] = b.dma("sync", "ds", Vh[0:64, :], vgd[64 * h:64 * h + 64, :])
        for e in ("tensor", "vector", "scalar"):
            b.wait(e, "ds", hdma[sl_])
        emit_l1_attn_head(b, h, Qe, Ke, Vh, Vtok, bufs, h == 0)
    vO = b.counts["v"]
    b.wait("tensor", "v", vO)
    b.wait("tensor", "dg", ldow)
    ev_war = {}
    for oc in range(8):
        for sl in range(4):
            n0 = sl * 512
            bid = 4 + (oc * 4 + sl) % 2
            bk = banks[bid]
            if bid in ev_war:
                b.wait("tensor", "v", ev_war[bid])
            for kc in range(8):
                def fmm(E, oc=oc, kc=kc, n0=n0, bk=bk):
                    return E.matmul(bk[:, 0:512],
                                    ow_sb[:, kc * 1024 + oc * 128:kc * 1024 + (oc + 1) * 128],
                                    O[:, kc * TOK + n0:kc * TOK + n0 + 512],
                                    start=(kc == 0), stop=(kc == 7))
                tmm = b.op("tensor", fmm, inc="t" if kc == 7 else None)
            b.wait("vector", "t", tmm)

            def fad(E, oc=oc, n0=n0, bk=bk):
                xs = x[:, oc * TOK + n0:oc * TOK + n0 + 512]
                return E.tensor_add(xs, xs, bk[:, 0:512])
            ev_war[bid] = b.op("vector", fad, inc="v")
    vfin = b.counts["v"]
    b.wait("sync", "v", vfin)
    st = b.dma("sync", "st", xout.rearrange("(c p) n -> p c n", p=128),
               x[:].rearrange("p (c n) -> p c n", c=8))
    b.wait("sync", "st", st)
    return b.finalize()


def host_l1_consts():
    slopes = np.asarray([(2.0 ** (-8.0 / 16)) ** (i + 1) for i in range(16)], np.float64)
    k = np.arange(S_LEN, dtype=np.float64)
    al3 = np.zeros((16, 3, S_LEN), BF16NP)
    for h in range(16):
        v = slopes[h] * k
        h1 = v.astype(BF16NP)
        r1 = v - h1.astype(np.float64)
        h2 = r1.astype(BF16NP)
        r2 = r1 - h2.astype(np.float64)
        h3 = r2.astype(BF16NP)
        al3[h, 0], al3[h, 1], al3[h, 2] = h1, h2, h3
    ones3 = np.ones((3, S_LEN), BF16NP)
    ident = np.eye(128, dtype=BF16NP)
    return al3, ones3, ident


def build_l1_ffn():
    nc = bass.Bass("TRN2", target_bir_lowering=False)
    xin = nc.dram_tensor("xin", [D, TOK], F32, kind="ExternalInput").ap()
    seld = nc.dram_tensor("selt", [D, 32], F32, kind="ExternalInput").ap()
    keyd = nc.dram_tensor("keys", [32, D, 512], F32, kind="ExternalInput").ap()
    vald = nc.dram_tensor("values", [32, 512, D], F32, kind="ExternalInput").ap()
    idfd = nc.dram_tensor("idf", [128, 128], F32, kind="ExternalInput").ap()
    eyed = nc.dram_tensor("eyeb32", [32, 32 * 128], BF16, kind="ExternalInput").ap()
    xout = nc.dram_tensor("xout", [D, TOK], F32, kind="ExternalOutput").ap()

    b = Builder(nc)
    x = b.sbuf("x", [128, 8 * TOK], F32)
    xln = b.sbuf("xln", [128, 8 * TOK], BF16)
    arB = b.sbuf("arB", [128, 6144], F32)
    sq = arB.ap()[:, 0:TOK]
    rb = arB.ap()[:, TOK:2 * TOK]
    mb_ = arB.ap()[:, 2 * TOK:3 * TOK]
    gates = arB.ap()[0:32, 0:TOK]
    gtok = arB.ap()[:, 2048:2080]
    et = arB.ap()[:, 2560:2592]
    tsc = arB.ap()[:, 2592:2624]
    tmsk = arB.ap()[:, 2624:2656]
    gftok = arB.ap()[:, 2656:3168]
    gbc = arB.ap().bitcast(BF16)[:, 6336:8384]
    scsc = arB.ap().bitcast(BF16)[:, 8384:10432]
    sctmp = arB.ap().bitcast(BF16)[:, 10432:10944]
    rowbuf = b.sbuf("rowbuf", [1, 4 * TOK], F32)
    k_sb = b.sbuf("k_sb", [128, 8 * 512], BF16)
    v_sb = b.sbuf("v_sb", [128, 4 * 1024], BF16)
    selw = b.sbuf("selw", [128, 8 * 32], BF16)
    gT = b.sbuf("gT", [32, TOK], BF16)
    idf = b.sbuf("idf_sb", [128, 128], F32)
    eyeb = b.sbuf("eyeb_sb", [32, 32 * 128], BF16)
    sml = b.sbuf("sml", [128, 8], F32)
    banks = [b.psum(f"bk{i}") for i in range(8)]
    c = alloc_consts(b)

    ldx = b.dma("sync", "ds", x[:].rearrange("p (c n) -> p c n", c=8),
                xin.rearrange("(c p) n -> p c n", p=128))
    b.dma("sync", "ds", idf[:], idfd)
    b.dma("sync", "ds", eyeb[:], eyed)
    ldsel = b.dma("gpsimd", "dg", selw[:].rearrange("p (c n) -> p c n", c=8),
                  seld.rearrange("(c p) n -> p c n", p=128))
    for e in ("scalar", "tensor", "vector"):
        b.wait(e, "ds", ldx)
        b.wait(e, "v", 1)
    vln = emit_layernorm(b, x, xln, banks, sq, rowbuf, c["ones_col"], c["ones_row"],
                         rb, mb_, c["epsb"])
    b.wait("tensor", "v", vln)
    b.wait("tensor", "dg", ldsel)
    b.wait("tensor", "ds", b.counts["ds"])

    for sl in range(4):
        n0 = sl * 512
        for kc in range(8):
            def fmm(E, sl=sl, kc=kc, n0=n0):
                return E.matmul(banks[0][0:32, 0:512],
                                selw[:, kc * 32:(kc + 1) * 32],
                                xln[:, kc * TOK + n0: kc * TOK + n0 + 512],
                                start=(kc == 0), stop=(kc == 7))
            tmm = b.op("tensor", fmm, inc="t" if kc == 7 else None)
        b.wait("scalar", "t", tmm)

        def fev(E, n0=n0):
            return E.activation(gates[:, n0:n0 + 512], banks[0][0:32, 0:512], AF.Sigmoid)
        sev = b.op("scalar", fev, inc="s")
        b.wait("tensor", "s", sev)
    for tt in range(16):
        def ftr(E, tt=tt):
            return E.transpose(banks[1][0:128, 0:32], gates[:, tt * 128:(tt + 1) * 128],
                               idf[0:32, 0:32])
        ttr = b.op("tensor", ftr, inc="t")
        b.wait("vector", "t", ttr)
        vv = b.op("vector", lambda E, tt=tt: E.tensor_copy(gtok, banks[1][0:128, 0:32]),
                  inc="v")
        b.wait("tensor", "v", vv)

        def vstep(fn):
            t = b.op("vector", fn, inc="v")
            b.wait("vector", "v", t)
            return t

        vstep(lambda E: E.tensor_copy(tsc, gtok))
        for r_ in range(3):
            vstep(lambda E: E.reduce_max(sml[:, 1:2], tsc, axis=AX))
            vstep(lambda E: E.tensor_scalar(tmsk, tsc, sml[:, 1:2], NEG_BIG,
                                            ALU.is_ge, ALU.mult))
            vstep(lambda E: E.tensor_add(tsc, tsc, tmsk))
        vstep(lambda E: E.reduce_max(sml[:, 1:2], tsc, axis=AX))
        vstep(lambda E: E.scalar_tensor_tensor(gftok[:, 0:32], gtok, sml[:, 1:2], gtok,
                                               ALU.is_ge, ALU.mult))
        vgf = b.counts["v"]
        b.wait("tensor", "v", vgf)
        def ftr2(E, tt=tt):
            return E.transpose(banks[1][0:32, 0:128], gftok[:, 0:32], idf[:])
        ttr2 = b.op("tensor", ftr2, inc="t")
        b.wait("vector", "t", ttr2)
        vv2 = b.op("vector", lambda E, tt=tt: E.tensor_copy(gT[0:32, tt * 128:(tt + 1) * 128],
                                                            banks[1][0:32, 0:128]), inc="v")
        b.wait("tensor", "v", vv2)

    for e in range(32):
        b.wait("gpsimd", "t", b.counts["t"])
        lk = b.dma("gpsimd", "dg", k_sb[:].rearrange("p (c n) -> p c n", c=8),
                   keyd[e].rearrange("(c p) n -> p c n", p=128))
        lv = b.dma("gpsimd", "dg", v_sb[:].rearrange("p (c n) -> p c n", c=4),
                   vald[e].rearrange("(c p) n -> p c n", p=128))
        b.wait("tensor", "v", b.counts["v"])
        for sl in range(4):
            n0 = sl * 512
            def fbc(E, e=e, n0=n0):
                return E.matmul(banks[4][:, 0:512], eyeb[0:32, e * 128:(e + 1) * 128],
                                gT[0:32, n0:n0 + 512], start=True, stop=True)
            tb = b.op("tensor", fbc, inc="t")
            b.wait("vector", "t", tb)
            vv = b.op("vector", lambda E, n0=n0: E.tensor_copy(gbc[:, n0:n0 + 512],
                                                               banks[4][:, 0:512]), inc="v")
            b.wait("tensor", "v", vv)
        b.wait("tensor", "dg", lk)
        for sl in range(4):
            n0 = sl * 512
            b.wait("vector", "t", b.counts["t"])
            ev_war = {}
            for ic in range(4):
                bk = banks[ic % 2]
                if (ic % 2) in ev_war:
                    b.wait("tensor", "s", ev_war[ic % 2])
                for kc in range(8):
                    def fmm(E, e=e, sl=sl, ic=ic, kc=kc, n0=n0, bk=bk):
                        return E.matmul(bk[:, 0:512],
                                        k_sb[:, kc * 512 + ic * 128:kc * 512 + (ic + 1) * 128],
                                        xln[:, kc * TOK + n0:kc * TOK + n0 + 512],
                                        start=(kc == 0), stop=(kc == 7))
                    tmm = b.op("tensor", fmm, inc="t" if kc == 7 else None)
                b.wait("scalar", "t", tmm)

                def fre(E, ic=ic, bk=bk):
                    return E.activation(sctmp, bk[:, 0:512], AF.Relu)
                sre = b.op("scalar", fre, inc="s")
                ev_war[ic % 2] = sre
                b.wait("vector", "s", sre)

                def fmu(E, ic=ic, n0=n0):
                    return E.tensor_mul(scsc[:, ic * 512:(ic + 1) * 512], sctmp,
                                        gbc[:, n0:n0 + 512])
                vmu = b.op("vector", fmu, inc="v")
                b.wait("scalar", "v", vmu)
            vhm = b.counts["v"]
            b.wait("tensor", "v", vhm)
            if e == 0 and sl == 0:
                b.wait("tensor", "dg", lv)
            ev2_war = {}
            for oc in range(8):
                bk = banks[2 + oc % 2]
                if (oc % 2) in ev2_war:
                    b.wait("tensor", "v", ev2_war[oc % 2])
                for ic in range(4):
                    def fm2(E, oc=oc, ic=ic, bk=bk):
                        return E.matmul(bk[:, 0:512],
                                        v_sb[:, ic * 1024 + oc * 128:ic * 1024 + (oc + 1) * 128],
                                        scsc[:, ic * 512:(ic + 1) * 512],
                                        start=(ic == 0), stop=(ic == 3))
                    tm2 = b.op("tensor", fm2, inc="t" if ic == 3 else None)
                b.wait("vector", "t", tm2)

                def fad(E, oc=oc, n0=n0, bk=bk):
                    xs = x[:, oc * TOK + n0:oc * TOK + n0 + 512]
                    return E.tensor_add(xs, xs, bk[:, 0:512])
                ev2_war[oc % 2] = b.op("vector", fad, inc="v")
    vfin = b.counts["v"]
    b.wait("sync", "v", vfin)
    st = b.dma("sync", "st", xout.rearrange("(c p) n -> p c n", p=128),
               x[:].rearrange("p (c n) -> p c n", c=8))
    b.wait("sync", "st", st)
    return b.finalize()


def _host_consts():
    tri1, tri2 = host_tri_masks()
    ident = np.eye(128, dtype=BF16NP)
    idf = np.eye(128, dtype=np.float32)
    eyeb16 = np.zeros((16, 16 * 128), BF16NP)
    for e in range(16):
        eyeb16[e, e * 128:(e + 1) * 128] = 1.0
    eyeb32 = np.zeros((32, 32 * 128), BF16NP)
    for e in range(32):
        eyeb32[e, e * 128:(e + 1) * 128] = 1.0
    al3, ones3, _ = host_l1_consts()
    m1 = {}
    m2 = {}
    for is_a in (True, False):
        a1 = host_l1_masks(is_a, transposed=False).astype(BF16NP)
        a2 = host_l1_masks(is_a, transposed=True).astype(BF16NP)
        m1[is_a] = np.ascontiguousarray(a1.transpose(2, 0, 1, 3).reshape(128, 6144))
        m2[is_a] = np.ascontiguousarray(a2.transpose(2, 0, 1, 3).reshape(128, 6144))
    return dict(tri1=tri1, tri2=tri2, ident=ident, idf=idf, eyeb16=eyeb16,
                eyeb32=eyeb32, al3=al3, ones3=ones3, m1=m1, m2=m2)


def _assemble_global(parts, bi):
    out = np.empty((D, S_LEN), BF16NP)
    for half, ch in enumerate((CH_A, CH_B)):
        src = parts[2 * bi + half]
        for li, gc in enumerate(ch):
            out[:, 256 * gc:256 * gc + 256] = src[:, 256 * li:256 * li + 256]
    return out


def kernel_multilaunch(L0_attn_inputs, L0_attn_qkv_kernel, L0_attn_qkv_bias,
           L0_ffn_router_w, L0_ffn_router_b, L0_ffn_w1, L0_ffn_b1,
           L0_ffn_w2, L0_ffn_b2, L1_attn_q_w, L1_attn_k_w, L1_attn_v_w,
           L1_attn_o_w, L1_ffn_expert_sel, L1_ffn_keys, L1_ffn_values):
    cst = _host_consts()
    cores = list(range(8))
    shards = shard_tokens(np.asarray(L0_attn_inputs, dtype=np.float32))

    nc = build_l0_qkv()
    res = run_bass_kernel_spmd(
        nc, [{"xin": s_, "wq": np.asarray(L0_attn_qkv_kernel),
              "bq": np.asarray(L0_attn_qkv_bias)} for s_ in shards], core_ids=cores)
    qkvs = [r["qkvout"] for r in res.results]

    nc = build_l0_attn()
    res = run_bass_kernel_spmd(
        nc, [{"xin": s_, "qkv": q_, "tri1": cst["tri1"], "tri2": cst["tri2"],
              "ident": cst["ident"]} for s_, q_ in zip(shards, qkvs)], core_ids=cores)
    shards = [r["xout"] for r in res.results]

    nc = build_l0_ffn()
    b1t = np.ascontiguousarray(np.asarray(L0_ffn_b1).T)
    res = run_bass_kernel_spmd(
        nc, [{"xin": s_, "rw": np.asarray(L0_ffn_router_w),
              "rb_in": np.asarray(L0_ffn_router_b).reshape(16, 1),
              "w1": np.asarray(L0_ffn_w1), "b1t": b1t,
              "w2": np.asarray(L0_ffn_w2), "b2": np.asarray(L0_ffn_b2),
              "idf": cst["idf"], "eyeb": cst["eyeb16"]} for s_ in shards],
        core_ids=cores)
    shards = [r["xout"] for r in res.results]

    nc = build_l1_qkv()
    res = run_bass_kernel_spmd(
        nc, [{"xin": s_, "qw": np.asarray(L1_attn_q_w), "kw": np.asarray(L1_attn_k_w),
              "vw": np.asarray(L1_attn_v_w)} for s_ in shards], core_ids=cores)
    qs = [r["qout"] for r in res.results]
    ks = [r["kout"] for r in res.results]
    vs = [r["vout"] for r in res.results]
    kg = [_assemble_global(ks, bi) for bi in range(4)]
    vg = [_assemble_global(vs, bi) for bi in range(4)]

    nc = build_l1_attn()
    in_maps = []
    for ci in range(8):
        bi, half = ci // 2, ci % 2
        is_a = (half == 0)
        in_maps.append({"xin": shards[ci], "qin": qs[ci], "kg": kg[bi], "vg": vg[bi],
                        "ow": np.asarray(L1_attn_o_w), "al3": cst["al3"],
                        "ones3": cst["ones3"], "m1": cst["m1"][is_a],
                        "m2": cst["m2"][is_a], "ident": cst["ident"]})
    res = run_bass_kernel_spmd(nc, in_maps, core_ids=cores)
    shards = [r["xout"] for r in res.results]

    nc = build_l1_ffn()
    selt = np.ascontiguousarray(np.asarray(L1_ffn_expert_sel).T)
    res = run_bass_kernel_spmd(
        nc, [{"xin": s_, "selt": selt, "keys": np.asarray(L1_ffn_keys),
              "values": np.asarray(L1_ffn_values), "idf": cst["idf"],
              "eyeb32": cst["eyeb32"]} for s_ in shards], core_ids=cores)
    return unshard_tokens([r["xout"] for r in res.results], np.float32)


def build_fused():
    nc = bass.Bass("TRN2", target_bir_lowering=False)
    xin = nc.dram_tensor("xin", [D, TOK], F32, kind="ExternalInput").ap()
    wqkv = nc.dram_tensor("wqkv", [D, 3072], F32, kind="ExternalInput").ap()
    bqkv = nc.dram_tensor("bqkv", [1, 3072], F32, kind="ExternalInput").ap()
    rwd = nc.dram_tensor("rw", [D, 16], F32, kind="ExternalInput").ap()
    rbd = nc.dram_tensor("rb_in", [16, 1], F32, kind="ExternalInput").ap()
    w1d = nc.dram_tensor("w1", [16, D, 2048], F32, kind="ExternalInput").ap()
    b1d = nc.dram_tensor("b1t", [2048, 16], F32, kind="ExternalInput").ap()
    w2d = nc.dram_tensor("w2", [16, 2048, D], F32, kind="ExternalInput").ap()
    b2d = nc.dram_tensor("b2", [16, D], F32, kind="ExternalInput").ap()
    qwd = nc.dram_tensor("qw", [D, D], F32, kind="ExternalInput").ap()
    kwd = nc.dram_tensor("kw", [D, D], F32, kind="ExternalInput").ap()
    vwd = nc.dram_tensor("vw", [D, D], F32, kind="ExternalInput").ap()
    owd = nc.dram_tensor("ow", [D, D], F32, kind="ExternalInput").ap()
    seld = nc.dram_tensor("selt", [D, 32], F32, kind="ExternalInput").ap()
    keyd = nc.dram_tensor("keys", [32, D, 512], F32, kind="ExternalInput").ap()
    vald = nc.dram_tensor("values", [32, 512, D], F32, kind="ExternalInput").ap()
    tri1d = nc.dram_tensor("tri1", [128, 128], F32, kind="ExternalInput").ap()
    tri2d = nc.dram_tensor("tri2", [128, 128], F32, kind="ExternalInput").ap()
    identd = nc.dram_tensor("ident", [128, 128], BF16, kind="ExternalInput").ap()
    idfd = nc.dram_tensor("idf", [128, 128], F32, kind="ExternalInput").ap()
    eye16d = nc.dram_tensor("eyeb16", [16, 16 * 128], BF16, kind="ExternalInput").ap()
    eye32d = nc.dram_tensor("eyeb32", [32, 32 * 128], BF16, kind="ExternalInput").ap()
    al3d = nc.dram_tensor("al3", [16, 3, S_LEN], BF16, kind="ExternalInput").ap()
    ones3d = nc.dram_tensor("ones3", [3, S_LEN], BF16, kind="ExternalInput").ap()
    m1d = nc.dram_tensor("m1", [128, 6144], BF16, kind="ExternalInput").ap()
    m2d = nc.dram_tensor("m2", [128, 6144], BF16, kind="ExternalInput").ap()
    xout = nc.dram_tensor("xout", [D, TOK], F32, kind="ExternalOutput").ap()
    dbgs = {}
    if DBG_STAGES:
        for nm in ("dbg_l0a", "dbg_l0f", "dbg_l1a"):
            dbgs[nm] = nc.dram_tensor(nm, [D, TOK], F32, kind="ExternalOutput").ap()
    qkv_s = nc.dram_tensor("qkv_s", [3072, TOK], BF16, kind="Internal").ap()
    q_s = nc.dram_tensor("q_s", [D, TOK], BF16, kind="Internal").ap()
    kv_loc = nc.dram_tensor("kv_loc", [2 * D, TOK], BF16, kind="Internal").ap()
    kv_gath = nc.dram_tensor("kv_gath", [4 * D, TOK], BF16, kind="Internal").ap()
    kg_s = nc.dram_tensor("kg_s", [D, S_LEN], BF16, kind="Internal").ap()
    vg_s = nc.dram_tensor("vg_s", [D, S_LEN], BF16, kind="Internal").ap()

    b = Builder(nc)
    x = b.sbuf("x", [128, 8 * TOK], F32)
    xln = b.sbuf("xln", [128, 8 * TOK], BF16)
    AR1 = b.sbuf("AR1", [128, 15680], F32)
    AR2 = b.sbuf("AR2", [128, 12288], F32)
    idb = b.sbuf("idb_sb", [128, 128], BF16)
    idf = b.sbuf("idf_sb", [128, 128], F32)
    mz = b.sbuf("mz", [128, 160], F32)
    bpack = b.sbuf("bpack", [128, 96], BF16)
    sml = b.sbuf("sml", [128, 8], F32)
    qkvbias = b.sbuf("qkvbias", [128, 24], F32)
    rbsb = b.sbuf("rbsb", [16, 1], F32)
    c = alloc_consts(b)
    bigps = b.psum("bigps", (128, 4096))
    banks = [bigps.ap()[:, i * 512:(i + 1) * 512] for i in range(8)]
    bigap = bigps.ap()

    a1f = AR1.ap()
    a1b = AR1.ap().bitcast(BF16)
    a2f = AR2.ap()
    a2b = AR2.ap().bitcast(BF16)
    tri1 = a2f[:, 10240:10368]
    tri2 = a2f[:, 10368:10496]
    Pb2 = [a2b[:, 14336:15360], a2b[:, 15360:16384]]
    Pb6 = [a2b[:, 18432:19456], a2b[:, 19456:20480]]
    rowbuf = a1f[0:1, 0:4 * TOK]
    ln_sq = a1f[:, 8192:10240]
    ln_rb = a1f[:, 10240:12288]
    ln_mb = a1f[:, 12288:14336]

    def do_ln():
        vln = emit_layernorm(b, x, xln, banks, ln_sq, rowbuf, c["ones_col"],
                             c["ones_row"], ln_rb, ln_mb, c["epsb"])
        return vln

    ldx = b.dma("sync", "ds", x[:].rearrange("p (c n) -> p c n", c=8),
                xin.rearrange("(c p) n -> p c n", p=128))
    b.dma("sync", "ds", idb[:], identd)
    b.dma("sync", "ds", idf[:], idfd)
    b.dma("sync", "ds", rbsb[:], rbd)
    ld0 = b.dma("gpsimd", "dg", qkvbias[:], bqkv.rearrange("o (c p) -> p (o c)", p=128),
                allow_slow_non_contiguous=True)
    w_qkv = a2b[:, 0:24576]
    ldw = b.dma("gpsimd", "dg", w_qkv.rearrange("p (c n) -> p c n", c=8),
                wqkv.rearrange("(c p) n -> p c n", p=128))
    for e in ("scalar", "tensor", "vector"):
        b.wait(e, "ds", ldx)
        b.wait(e, "v", 1)

    vln = do_ln()
    b.wait("tensor", "v", vln)
    b.wait("tensor", "dg", ldw)
    b.wait("scalar", "dg", ld0)
    stg = [a1b[:, 0:2048], a1b[:, 2048:4096]]
    state = {}
    stg_war = [0, 0]
    for oc in range(24):
        sg = stg[oc % 2]
        b.wait("scalar", "st", stg_war[oc % 2])
        for sl in range(4):
            n0 = sl * 512
            bid = sl % 2
            bk = banks[bid]
            if bid in state:
                b.wait("tensor", "s", state[bid])
            for kc in range(8):
                def fmm(E, oc=oc, kc=kc, n0=n0, bk=bk):
                    return E.matmul(bk[:, 0:512],
                                    w_qkv[:, kc * 3072 + oc * 128: kc * 3072 + (oc + 1) * 128],
                                    xln[:, kc * TOK + n0: kc * TOK + n0 + 512],
                                    start=(kc == 0), stop=(kc == 7))
                tmm = b.op("tensor", fmm, inc="t" if kc == 7 else None)
            b.wait("scalar", "t", tmm)

            def fev(E, oc=oc, n0=n0, bk=bk, sg=sg):
                return E.activation(sg[:, n0:n0 + 512], bk[:, 0:512], AF.Identity,
                                    bias=qkvbias[:, oc:oc + 1])
            state[bid] = b.op("scalar", fev, inc="s")
        b.wait("sync", "s", b.counts["s"])
        stg_war[oc % 2] = b.dma("sync", "st", qkv_s[oc * 128:(oc + 1) * 128, :], sg)
    b.barrier()
    qkv_done = b.counts["st"]
    b.wait("tensor", "st", qkv_done)
    b.wait("sync", "st", qkv_done)

    ldtr = b.dma("sync", "ds", tri1, tri1d)
    ldtr = b.dma("sync", "ds", tri2, tri2d)
    for e_ in ("tensor", "vector", "scalar"):
        b.wait(e_, "ds", ldtr)
    abufs = {
        "Qt": a2b[:, 0:4096], "Kt": a2b[:, 4096:8192],
        "Vt": a2b[0:64, 8192:12288], "Vtok": a2b[:, 12288:14336],
        "Pb": Pb2, "sink": Pb2[0], "mz": mz, "bpack": bpack,
        "tri1": tri1, "tri2": tri2, "idb": idb.ap(), "banks": banks,
        "pT": bigap.bitcast(BF16)[:, 7168:8192], "big": bigap,
        "sink4k": a2b[:, 16384:20480],
    }
    b.op("vector", lambda E: E.memset(abufs["Kt"][64:67, :], 1.0), inc="v")
    b.wait("tensor", "v", b.counts["v"])
    qkv_r = qkv_s.rearrange("(c p) n -> p c n", p=128)
    qkv_g = [a1b[:, 4096 + i * 12288: 4096 + (i + 1) * 12288] for i in range(2)]
    for g in range(4):
        gb = qkv_g[g % 2]
        b.wait("sync", "v", b.counts["v"])
        ldg = b.dma("sync", "ds", gb.rearrange("p (c n) -> p c n", c=24),
                    qkv_r[:, :, g * 512:(g + 1) * 512])
        b.wait("vector", "ds", ldg)
        for i in range(2):
            li = 2 * g + i
            emit_l0_attn_instance(b, gb, i * 256, x, li * 256, abufs, None)
    if DBG_STAGES:
        b.wait("sync", "v", b.counts["v"])
        b.dma("sync", "st", dbgs["dbg_l0a"].rearrange("(c p) n -> p c n", p=128),
              x[:].rearrange("p (c n) -> p c n", c=8))
    b.barrier()

    w2sb = a1b[:, 0:16384]
    gtmp = a1b[:, 16384:17408]
    cbc = a1b[:, 17408:19456]
    cTb_f = a1b[0:16, 19456:21504]
    eyeb16 = a1b[0:16, 21504:23552]
    b1sb = a1f[:, 11776:12032]
    logits = a1f[0:16, 12032:14080]
    ltok = a1f[:, 14080:14336]
    l_et = a1f[:, 14336:14352]
    l_tsc = a1f[:, 14352:14368]
    l_tmsk = a1f[:, 14368:14384]
    ctok = a1f[:, 14384:14640]
    w1sb = a2b[:, 0:16384]
    hm = a2b[:, 16384:24576]
    rwsb = b.sbuf("rwsb", [128, 8 * 16], BF16)
    b2sb = a1b[0:16, 29280:30304]

    vln = do_ln()
    b.wait("gpsimd", "v", vln)
    ldc = b.dma("gpsimd", "dg", rwsb[:].rearrange("p (c n) -> p c n", c=8),
                rwd.rearrange("(c p) n -> p c n", p=128))
    b.dma("gpsimd", "dg", b1sb.rearrange("p (c e) -> p c e", c=16),
          b1d.rearrange("(c p) e -> p c e", p=128))
    b.dma("gpsimd", "dg", b2sb, b2d)
    ldg2 = b.dma("gpsimd", "dg", eyeb16, eye16d)
    b.wait("tensor", "v", vln)
    b.wait("tensor", "dg", ldc)

    for sl in range(4):
        n0 = sl * 512
        for kc in range(8):
            def fmm(E, sl=sl, kc=kc, n0=n0):
                return E.matmul(banks[0][0:16, 0:512],
                                rwsb[:, kc * 16:(kc + 1) * 16],
                                xln[:, kc * TOK + n0: kc * TOK + n0 + 512],
                                start=(kc == 0), stop=(kc == 7))
            tmm = b.op("tensor", fmm, inc="t" if kc == 7 else None)
        b.wait("scalar", "t", tmm)

        def fev(E, n0=n0):
            return E.activation(logits[0:16, n0:n0 + 512], banks[0][0:16, 0:512],
                                AF.Identity, bias=rbsb[:])
        sev = b.op("scalar", fev, inc="s")
        b.wait("tensor", "s", sev)
    for tt in range(16):
        def ftr(E, tt=tt):
            return E.transpose(banks[1][0:128, 0:16], logits[0:16, tt * 128:(tt + 1) * 128],
                               idf[0:16, 0:16])
        ttr = b.op("tensor", ftr, inc="t")
        b.wait("vector", "t", ttr)
        vv = b.op("vector", lambda E, tt=tt: E.tensor_copy(ltok[:, tt * 16:(tt + 1) * 16],
                                                           banks[1][0:128, 0:16]), inc="v")
        b.wait("tensor", "v", vv)
    for tt in range(16):
        lt = ltok[:, tt * 16:(tt + 1) * 16]

        def vstep(fn):
            t = b.op("vector", fn, inc="v")
            b.wait("vector", "v", t)
            return t

        vstep(lambda E, lt=lt: E.reduce_max(sml[:, 0:1], lt, axis=AX, negate=True))
        vm = b.counts["v"]
        b.wait("scalar", "v", vm)
        sx = b.op("scalar", lambda E, lt=lt: E.activation(l_et, lt, AF.Exp,
                                                          bias=sml[:, 0:1]), inc="s")
        b.wait("vector", "s", sx)
        vstep(lambda E: E.tensor_copy(l_tsc, l_et))
        for r_ in range(3):
            vstep(lambda E: E.reduce_max(sml[:, 1:2], l_tsc, axis=AX))
            vstep(lambda E: E.tensor_scalar(l_tmsk, l_tsc, sml[:, 1:2], NEG_BIG,
                                            ALU.is_ge, ALU.mult))
            vstep(lambda E: E.tensor_add(l_tsc, l_tsc, l_tmsk))
        vstep(lambda E: E.reduce_max(sml[:, 1:2], l_tsc, axis=AX))
        vstep(lambda E: E.scalar_tensor_tensor(l_tmsk, l_et, sml[:, 1:2], l_et,
                                               ALU.is_ge, ALU.mult, accum_out=sml[:, 2:3]))
        vstep(lambda E: E.reciprocal(sml[:, 3:4], sml[:, 2:3]))
        vstep(lambda E, tt=tt: E.tensor_scalar_mul(ctok[:, tt * 16:(tt + 1) * 16],
                                                   l_tmsk, sml[:, 3:4]))
    vct = b.counts["v"]
    b.wait("tensor", "v", vct)
    for tt in range(16):
        def ftr(E, tt=tt):
            return E.transpose(banks[1][0:16, 0:128], ctok[:, tt * 16:(tt + 1) * 16], idf[:])
        ttr = b.op("tensor", ftr, inc="t")
        b.wait("vector", "t", ttr)
        vv = b.op("vector", lambda E, tt=tt: E.tensor_copy(cTb_f[0:16, tt * 128:(tt + 1) * 128],
                                                           banks[1][0:16, 0:128]), inc="v")
        b.wait("tensor", "v", vv)
    b.wait("scalar", "dg", ldg2)
    lw1d = {}
    lw2d = {}

    def issue_w1(e):
        lw1d[e] = b.dma("gpsimd", "dg", w1sb.rearrange("p (c n) -> p c n", c=8),
                        w1d[e].rearrange("(c p) n -> p c n", p=128))

    def issue_w2(e):
        lw2d[e] = b.dma("gpsimd", "dg", w2sb.rearrange("p (c n) -> p c n", c=16),
                        w2d[e].rearrange("(c p) n -> p c n", p=128))

    b.wait("gpsimd", "t", b.counts["t"])
    b.wait("gpsimd", "v", b.counts["v"])
    issue_w1(0)
    issue_w2(0)
    for e in range(16):
        b.wait("tensor", "v", b.counts["v"])
        for sl in range(4):
            n0 = sl * 512
            def fbc(E, e=e, sl=sl, n0=n0):
                return E.matmul(banks[4 + sl][:, 0:512], eyeb16[0:16, e * 128:(e + 1) * 128],
                                cTb_f[0:16, n0:n0 + 512], start=True, stop=True)
            tb = b.op("tensor", fbc, inc="t" if sl == 3 else None)
        b.wait("vector", "t", tb)
        vv = b.op("vector", lambda E: E.tensor_copy(cbc[:, 0:2048], bigap[:, 2048:4096]),
                  inc="v")
        b.wait("tensor", "v", vv)
        b.wait("tensor", "dg", lw2d[e])
        ev_war = {}
        gt_war = {}
        ev2_war = {}
        for sl in range(4):
            n0 = sl * 512
            b.wait("vector", "t", b.counts["t"])
            for ic in range(16):
                bk = banks[ic % 4]
                if (ic % 4) in ev_war:
                    b.wait("tensor", "s", ev_war[ic % 4])
                for kc in range(8):
                    def fmm(E, e=e, sl=sl, ic=ic, kc=kc, n0=n0, bk=bk):
                        return E.matmul(bk[:, 0:512],
                                        w1sb[:, kc * 2048 + ic * 128:kc * 2048 + (ic + 1) * 128],
                                        xln[:, kc * TOK + n0:kc * TOK + n0 + 512],
                                        start=(kc == 0), stop=(kc == 7))
                    tmm = b.op("tensor", fmm, inc="t" if kc == 7 else None)
                b.wait("scalar", "t", tmm)
                if (ic % 2) in gt_war:
                    b.wait("scalar", "v", gt_war[ic % 2])

                def fge(E, e=e, ic=ic, bk=bk):
                    return E.activation(gtmp[:, (ic % 2) * 512:(ic % 2) * 512 + 512],
                                        bk[:, 0:512], AF.Gelu,
                                        bias=b1sb[:, ic * 16 + e:ic * 16 + e + 1])
                sge = b.op("scalar", fge, inc="s")
                ev_war[ic % 4] = sge
                b.wait("vector", "s", sge)

                def fmu(E, ic=ic, n0=n0):
                    return E.tensor_mul(hm[:, ic * 512:(ic + 1) * 512],
                                        gtmp[:, (ic % 2) * 512:(ic % 2) * 512 + 512],
                                        cbc[:, n0:n0 + 512])
                gt_war[ic % 2] = b.op("vector", fmu, inc="v")
            if sl == 3 and e < 15:
                b.wait("gpsimd", "t", b.counts["t"])
                issue_w1(e + 1)
            vhm = b.counts["v"]
            b.wait("tensor", "v", vhm)
            for oc in range(8):
                bk = banks[4 + oc % 4]
                if (oc % 4) in ev2_war:
                    b.wait("tensor", "v", ev2_war[oc % 4])
                for ic in range(16):
                    def fm2(E, e=e, sl=sl, oc=oc, ic=ic, n0=n0, bk=bk):
                        return E.matmul(bk[:, 0:512],
                                        w2sb[:, ic * 1024 + oc * 128:ic * 1024 + (oc + 1) * 128],
                                        hm[:, ic * 512:(ic + 1) * 512],
                                        start=(ic == 0), stop=(e > 0 and ic == 15))
                    tm2 = b.op("tensor", fm2,
                               inc="t" if (e > 0 and ic == 15) else None)
                if e == 0:
                    def fb2(E, oc=oc, n0=n0, bk=bk):
                        return E.matmul(bk[:, 0:512], b2sb[0:16, oc * 128:(oc + 1) * 128],
                                        cTb_f[0:16, n0:n0 + 512], start=False, stop=True)
                    tm2 = b.op("tensor", fb2, inc="t")
                b.wait("vector", "t", tm2)

                def fad(E, oc=oc, n0=n0, bk=bk):
                    xs = x[:, oc * TOK + n0:oc * TOK + n0 + 512]
                    return E.tensor_add(xs, xs, bk[:, 0:512])
                ev2_war[oc % 4] = b.op("vector", fad, inc="v")
            if sl == 3 and e < 15:
                b.wait("gpsimd", "t", b.counts["t"])
                issue_w2(e + 1)
    if DBG_STAGES:
        b.wait("sync", "v", b.counts["v"])
        b.dma("sync", "st", dbgs["dbg_l0f"].rearrange("(c p) n -> p c n", p=128),
              x[:].rearrange("p (c n) -> p c n", c=8))
    b.barrier()

    vln = do_ln()
    b.wait("tensor", "v", vln)
    w_l1 = [a2b[:, 0:8192], a2b[:, 8192:16384]]
    dsts = [q_s, kv_loc[0:D, :], kv_loc[D:2 * D, :]]
    wsrcs = [qwd, kwd, vwd]
    wdma = [0, 0]
    for wi in range(3):
        wb = w_l1[wi % 2]
        b.wait("gpsimd", "t", b.counts["t"])
        wdma[wi % 2] = b.dma("gpsimd", "dg", wb.rearrange("p (c n) -> p c n", c=8),
                             wsrcs[wi].rearrange("(c p) n -> p c n", p=128))
        b.wait("tensor", "dg", wdma[wi % 2])
        scale = 0.125 if wi == 0 else 1.0
        state = {}
        stg_war = [0, 0]
        for oc in range(8):
            sg = stg[oc % 2]
            b.wait("scalar", "st", stg_war[oc % 2])
            for sl in range(4):
                n0 = sl * 512
                bid = sl % 2
                bk = banks[bid]
                if bid in state:
                    b.wait("tensor", "s", state[bid])
                for kc in range(8):
                    def fmm(E, oc=oc, kc=kc, n0=n0, bk=bk, wb=wb):
                        return E.matmul(bk[:, 0:512],
                                        wb[:, kc * 1024 + oc * 128:kc * 1024 + (oc + 1) * 128],
                                        xln[:, kc * TOK + n0:kc * TOK + n0 + 512],
                                        start=(kc == 0), stop=(kc == 7))
                    tmm = b.op("tensor", fmm, inc="t" if kc == 7 else None)
                b.wait("scalar", "t", tmm)

                def fev(E, n0=n0, bk=bk, sg=sg, scale=scale):
                    return E.activation(sg[:, n0:n0 + 512], bk[:, 0:512], AF.Identity,
                                        scale=scale)
                state[bid] = b.op("scalar", fev, inc="s")
            b.wait("sync", "s", b.counts["s"])
            stg_war[oc % 2] = b.dma("sync", "st", dsts[wi][oc * 128:(oc + 1) * 128, :], sg)
    b.wait("gpsimd", "st", b.counts["st"])
    s_cc, cc_tot = b.inc("cc", 1)
    b.emit("gpsimd", lambda E: E.collective_compute(
        "AllGather", ALU.bypass, replica_groups=[[0, 1], [2, 3], [4, 5], [6, 7]],
        ins=[kv_loc.opt()], outs=[kv_gath.opt()]).then_inc(s_cc))
    b.wait("sync", "cc", cc_tot)
    for half, ch in enumerate((CH_A, CH_B)):
        for li, gc in enumerate(ch):
            roff = half * 2048
            b.dma("sync", "st", kg_s[:, 256 * gc:256 * gc + 256],
                  kv_gath[roff:roff + D, 256 * li:256 * li + 256])
            b.dma("sync", "st", vg_s[:, 256 * gc:256 * gc + 256],
                  kv_gath[roff + D:roff + 2 * D, 256 * li:256 * li + 256])
    relay = b.counts["st"]
    b.barrier()
    b.wait("sync", "st", relay)

    O = a1b[:, 14976:31360]
    lbufs = {
        "Pb": Pb6, "sink": Pb6[0], "mz": mz, "bpack": bpack,
        "m1": a2b[:, 4096:10240],
        "m2": a2b[:, 10240:16384],
        "idb": idb.ap(), "bts": a2b[0:3, 16384:18432], "O": O, "banks": banks,
        "pT": bigap.bitcast(BF16)[:, 7168:8192], "big": bigap,
        "sink4k": a2b[:, 20480:24576],
    }
    Qe = a2b[:, 0:2048]
    Ke = a1b[:, 0:4096]
    Vh = a1b[0:64, 4096:8192]
    Vtok = a2b[:, 2048:4096]
    ldm1 = b.dma("sync", "ds", lbufs["m1"], m1d)
    ldm2 = b.dma("sync", "ds", lbufs["m2"], m2d)
    b.dma("sync", "ds", Qe[64:67, :], ones3d[:, 0:TOK])
    ldst = b.dma("sync", "ds", Ke[67:70, :], ones3d)
    for e in ("tensor", "vector", "scalar"):
        b.wait(e, "ds", ldst)
    for h in range(16):
        b.wait("sync", "t", b.counts["t"])
        b.wait("sync", "s", b.counts["s"])
        b.dma("sync", "ds", Qe[0:64, :], q_s[64 * h:64 * h + 64, :])
        b.dma("sync", "ds", Ke[0:64, :], kg_s[64 * h:64 * h + 64, :])
        b.dma("sync", "ds", Ke[64:67, :], al3d[h])
        hd = b.dma("sync", "ds", Vh[0:64, :], vg_s[64 * h:64 * h + 64, :])
        for e in ("tensor", "vector", "scalar"):
            b.wait(e, "ds", hd)
        emit_l1_attn_head(b, h, Qe, Ke, Vh, Vtok, lbufs, h == 0)
    vO = b.counts["v"]
    b.wait("gpsimd", "v", vO)
    b.wait("gpsimd", "t", b.counts["t"])
    ow_sb = a2b[:, 4096:12288]
    ldow = b.dma("gpsimd", "dg", ow_sb.rearrange("p (c n) -> p c n", c=8),
                 owd.rearrange("(c p) n -> p c n", p=128))
    b.wait("tensor", "v", vO)
    b.wait("tensor", "dg", ldow)
    ev_war = {}
    for oc in range(8):
        for sl in range(4):
            n0 = sl * 512
            bid = 4 + (oc * 4 + sl) % 2
            bk = banks[bid]
            if bid in ev_war:
                b.wait("tensor", "v", ev_war[bid])
            for kc in range(8):
                def fmm(E, oc=oc, kc=kc, n0=n0, bk=bk):
                    return E.matmul(bk[:, 0:512],
                                    ow_sb[:, kc * 1024 + oc * 128:kc * 1024 + (oc + 1) * 128],
                                    O[:, kc * TOK + n0:kc * TOK + n0 + 512],
                                    start=(kc == 0), stop=(kc == 7))
                tmm = b.op("tensor", fmm, inc="t" if kc == 7 else None)
            b.wait("vector", "t", tmm)

            def fad(E, oc=oc, n0=n0, bk=bk):
                xs = x[:, oc * TOK + n0:oc * TOK + n0 + 512]
                return E.tensor_add(xs, xs, bk[:, 0:512])
            ev_war[bid] = b.op("vector", fad, inc="v")
    if DBG_STAGES:
        b.wait("sync", "v", b.counts["v"])
        b.dma("sync", "st", dbgs["dbg_l1a"].rearrange("(c p) n -> p c n", p=128),
              x[:].rearrange("p (c n) -> p c n", c=8))
    b.barrier()

    gates = a1f[0:32, 0:TOK]
    gtok = a1f[:, 2048:2080]
    f_et = a1f[:, 2080:2112]
    f_tsc = a1f[:, 2112:2144]
    f_tmsk = a1f[:, 2144:2176]
    gftok = a1f[:, 2176:2688]
    gbc = a1b[:, 5632:7680]
    scsc = a1b[:, 7680:9728]
    sctmp = a1b[:, 9728:10752]
    k_sb = a2b[:, 0:4096]
    v_sb = a2b[:, 4096:8192]
    selw = a2b[:, 8192:8448]
    eyeb32 = a2b[0:32, 8448:12544]
    gT = a2b[0:32, 12544:14592]

    vln = do_ln()
    b.wait("gpsimd", "v", vln)
    ldsel = b.dma("gpsimd", "dg", selw.rearrange("p (c n) -> p c n", c=8),
                  seld.rearrange("(c p) n -> p c n", p=128))
    lde32 = b.dma("gpsimd", "dg", eyeb32, eye32d)
    b.wait("tensor", "v", vln)
    b.wait("tensor", "dg", ldsel)
    for sl in range(4):
        n0 = sl * 512
        for kc in range(8):
            def fmm(E, sl=sl, kc=kc, n0=n0):
                return E.matmul(banks[0][0:32, 0:512],
                                selw[:, kc * 32:(kc + 1) * 32],
                                xln[:, kc * TOK + n0: kc * TOK + n0 + 512],
                                start=(kc == 0), stop=(kc == 7))
            tmm = b.op("tensor", fmm, inc="t" if kc == 7 else None)
        b.wait("scalar", "t", tmm)

        def fev(E, n0=n0):
            return E.activation(gates[:, n0:n0 + 512], banks[0][0:32, 0:512], AF.Sigmoid)
        sev = b.op("scalar", fev, inc="s")
        b.wait("tensor", "s", sev)
    for tt in range(16):
        def ftr(E, tt=tt):
            return E.transpose(banks[1][0:128, 0:32], gates[:, tt * 128:(tt + 1) * 128],
                               idf[0:32, 0:32])
        ttr = b.op("tensor", ftr, inc="t")
        b.wait("vector", "t", ttr)
        vv = b.op("vector", lambda E, tt=tt: E.tensor_copy(gtok, banks[1][0:128, 0:32]),
                  inc="v")
        b.wait("tensor", "v", vv)

        def vstep(fn):
            t = b.op("vector", fn, inc="v")
            b.wait("vector", "v", t)
            return t

        vstep(lambda E: E.tensor_copy(f_tsc, gtok))
        for r_ in range(3):
            vstep(lambda E: E.reduce_max(sml[:, 1:2], f_tsc, axis=AX))
            vstep(lambda E: E.tensor_scalar(f_tmsk, f_tsc, sml[:, 1:2], NEG_BIG,
                                            ALU.is_ge, ALU.mult))
            vstep(lambda E: E.tensor_add(f_tsc, f_tsc, f_tmsk))
        vstep(lambda E: E.reduce_max(sml[:, 1:2], f_tsc, axis=AX))
        vstep(lambda E: E.scalar_tensor_tensor(gftok[:, 0:32], gtok, sml[:, 1:2], gtok,
                                               ALU.is_ge, ALU.mult))
        vgf = b.counts["v"]
        b.wait("tensor", "v", vgf)
        def ftr2(E, tt=tt):
            return E.transpose(banks[1][0:32, 0:128], gftok[:, 0:32], idf[:])
        ttr2 = b.op("tensor", ftr2, inc="t")
        b.wait("vector", "t", ttr2)
        vv2 = b.op("vector", lambda E, tt=tt: E.tensor_copy(gT[0:32, tt * 128:(tt + 1) * 128],
                                                            banks[1][0:32, 0:128]), inc="v")
        b.wait("tensor", "v", vv2)
    b.wait("tensor", "dg", lde32)
    lkd = {}
    lvd = {}

    def issue_k(e):
        lkd[e] = b.dma("gpsimd", "dg", k_sb.rearrange("p (c n) -> p c n", c=8),
                       keyd[e].rearrange("(c p) n -> p c n", p=128))

    def issue_v(e):
        lvd[e] = b.dma("gpsimd", "dg", v_sb.rearrange("p (c n) -> p c n", c=4),
                       vald[e].rearrange("(c p) n -> p c n", p=128))

    b.wait("gpsimd", "t", b.counts["t"])
    b.wait("gpsimd", "v", b.counts["v"])
    issue_k(0)
    issue_v(0)
    for e in range(32):
        b.wait("tensor", "v", b.counts["v"])
        for sl in range(4):
            n0 = sl * 512
            def fbc(E, e=e, sl=sl, n0=n0):
                return E.matmul(banks[4 + sl][:, 0:512], eyeb32[0:32, e * 128:(e + 1) * 128],
                                gT[0:32, n0:n0 + 512], start=True, stop=True)
            tb = b.op("tensor", fbc, inc="t" if sl == 3 else None)
        b.wait("vector", "t", tb)
        vv = b.op("vector", lambda E: E.tensor_copy(gbc[:, 0:2048], bigap[:, 2048:4096]),
                  inc="v")
        b.wait("tensor", "v", vv)
        b.wait("tensor", "dg", lvd[e])
        ev_war = {}
        gt_war = {}
        ev2_war = {}
        for sl in range(4):
            n0 = sl * 512
            b.wait("vector", "t", b.counts["t"])
            for ic in range(4):
                bk = banks[ic]
                if ic in ev_war:
                    b.wait("tensor", "s", ev_war[ic])
                for kc in range(8):
                    def fmm(E, e=e, sl=sl, ic=ic, kc=kc, n0=n0, bk=bk):
                        return E.matmul(bk[:, 0:512],
                                        k_sb[:, kc * 512 + ic * 128:kc * 512 + (ic + 1) * 128],
                                        xln[:, kc * TOK + n0:kc * TOK + n0 + 512],
                                        start=(kc == 0), stop=(kc == 7))
                    tmm = b.op("tensor", fmm, inc="t" if kc == 7 else None)
                b.wait("scalar", "t", tmm)
                if (ic % 2) in gt_war:
                    b.wait("scalar", "v", gt_war[ic % 2])

                def fre(E, ic=ic, bk=bk):
                    return E.activation(sctmp[:, (ic % 2) * 512:(ic % 2) * 512 + 512],
                                        bk[:, 0:512], AF.Relu)
                sre = b.op("scalar", fre, inc="s")
                ev_war[ic] = sre
                b.wait("vector", "s", sre)

                def fmu(E, ic=ic, n0=n0):
                    return E.tensor_mul(scsc[:, ic * 512:(ic + 1) * 512],
                                        sctmp[:, (ic % 2) * 512:(ic % 2) * 512 + 512],
                                        gbc[:, n0:n0 + 512])
                gt_war[ic % 2] = b.op("vector", fmu, inc="v")
            if sl == 3 and e < 31:
                b.wait("gpsimd", "t", b.counts["t"])
                issue_k(e + 1)
            vhm = b.counts["v"]
            b.wait("tensor", "v", vhm)
            for oc in range(8):
                bk = banks[4 + oc % 4]
                if (oc % 4) in ev2_war:
                    b.wait("tensor", "v", ev2_war[oc % 4])
                for ic in range(4):
                    def fm2(E, oc=oc, ic=ic, bk=bk):
                        return E.matmul(bk[:, 0:512],
                                        v_sb[:, ic * 1024 + oc * 128:ic * 1024 + (oc + 1) * 128],
                                        scsc[:, ic * 512:(ic + 1) * 512],
                                        start=(ic == 0), stop=(ic == 3))
                    tm2 = b.op("tensor", fm2, inc="t" if ic == 3 else None)
                b.wait("vector", "t", tm2)

                def fad(E, oc=oc, n0=n0, bk=bk):
                    xs = x[:, oc * TOK + n0:oc * TOK + n0 + 512]
                    return E.tensor_add(xs, xs, bk[:, 0:512])
                ev2_war[oc % 4] = b.op("vector", fad, inc="v")
            if sl == 3 and e < 31:
                b.wait("gpsimd", "t", b.counts["t"])
                issue_v(e + 1)
    vfin = b.counts["v"]
    b.wait("sync", "v", vfin)
    st = b.dma("sync", "st", xout.rearrange("(c p) n -> p c n", p=128),
               x[:].rearrange("p (c n) -> p c n", c=8))
    b.wait("sync", "st", st)
    return b.finalize()


def kernel_fused(**inputs):
    cst = _host_consts()
    cores = list(range(8))
    shards = shard_tokens(np.asarray(inputs["L0_attn_inputs"], dtype=np.float32))
    b1t = np.ascontiguousarray(np.asarray(inputs["L0_ffn_b1"]).T)
    selt = np.ascontiguousarray(np.asarray(inputs["L1_ffn_expert_sel"]).T)
    nc = build_fused()
    in_maps = []
    for ci in range(8):
        is_a = (ci % 2 == 0)
        in_maps.append({
            "xin": shards[ci],
            "wqkv": np.asarray(inputs["L0_attn_qkv_kernel"]),
            "bqkv": np.asarray(inputs["L0_attn_qkv_bias"]),
            "rw": np.asarray(inputs["L0_ffn_router_w"]),
            "rb_in": np.asarray(inputs["L0_ffn_router_b"]).reshape(16, 1),
            "w1": np.asarray(inputs["L0_ffn_w1"]), "b1t": b1t,
            "w2": np.asarray(inputs["L0_ffn_w2"]), "b2": np.asarray(inputs["L0_ffn_b2"]),
            "qw": np.asarray(inputs["L1_attn_q_w"]), "kw": np.asarray(inputs["L1_attn_k_w"]),
            "vw": np.asarray(inputs["L1_attn_v_w"]), "ow": np.asarray(inputs["L1_attn_o_w"]),
            "selt": selt, "keys": np.asarray(inputs["L1_ffn_keys"]),
            "values": np.asarray(inputs["L1_ffn_values"]),
            "tri1": cst["tri1"], "tri2": cst["tri2"], "ident": cst["ident"],
            "idf": cst["idf"], "eyeb16": cst["eyeb16"], "eyeb32": cst["eyeb32"],
            "al3": cst["al3"], "ones3": cst["ones3"],
            "m1": cst["m1"][is_a], "m2": cst["m2"][is_a],
        })
    res = run_bass_kernel_spmd(nc, in_maps, core_ids=cores)
    return unshard_tokens([r["xout"] for r in res.results], np.float32)


def kernel(**inputs):
    return kernel_fused(**inputs)
